# version 04
# baseline (speedup 1.0000x reference)
import contextlib
import numpy as np
import ml_dtypes
import concourse.bass as bass
import concourse.mybir as mybir
from concourse.bass_utils import run_bass_kernel_spmd

F32 = mybir.dt.float32
BF16 = mybir.dt.bfloat16
I32 = mybir.dt.int32
AF = mybir.ActivationFunctionType
OP = mybir.AluOpType
AX = mybir.AxisListType

BF = ml_dtypes.bfloat16

DM = 2048
FC = 16
TLOC = 2048
TT = 4
TN = 512
NH = 16
NG = 4
HD = 128
SEQ = 4096
I0 = 8192
NE = 8
IM = 1024
LHAT = (12, 16, 28, 32)
MSK_RANGE = ((0, 12), (0, 16), (16, 28), (20, 32))
N_MSK = sum(b - a for a, b in MSK_RANGE)
SCALE = 1.0 / np.sqrt(128.0)


class S:

    def __init__(self, h):
        self.h = h
        self.n = 0

    def i(self, instr, k=1):
        instr.then_inc(self.h, k)
        self.n += k
        return self.n


def _core_tokens(core):
    s, half = core // 2, core % 2
    if half == 0:
        return s, [(0, 1024), (3072, 4096)]
    return s, [(1024, 3072)]


GRPS_A = (0, 1, 6, 7)
GRPS_B = (2, 3, 4, 5)


def _host_masks(core):
    half = core % 2
    grps = GRPS_A if half == 0 else GRPS_B
    tiles = []
    for r in range(4):
        g = grps[r]
        q_glob = np.arange(g * 512, (g + 1) * 512)
        a, b = MSK_RANGE[r]
        for kc in range(a, b):
            k_glob = np.arange(kc * 128, (kc + 1) * 128)
            tiles.append((k_glob[:, None] <= q_glob[None, :]).astype(BF))
    return np.stack(tiles)


def _prep_inputs(inputs):
    g = {}
    g["l0_wq"] = np.ascontiguousarray(inputs["L0_attn_Wq"])
    g["l0_wk"] = np.ascontiguousarray(inputs["L0_attn_Wk"])
    g["l0_wv"] = np.ascontiguousarray(inputs["L0_attn_Wv"])
    g["l0_wo"] = np.ascontiguousarray(inputs["L0_attn_Wo"])
    g["gate_wT"] = np.ascontiguousarray(inputs["L0_ffn_gate_w"].T)
    g["up_wT"] = np.ascontiguousarray(inputs["L0_ffn_up_w"].T)
    g["down_wT"] = np.ascontiguousarray(inputs["L0_ffn_down_w"].T)
    g["sg"] = np.ascontiguousarray(inputs["L0_ffn_gate_scale"].reshape(64, 128).T)
    g["su"] = np.ascontiguousarray(inputs["L0_ffn_up_scale"].reshape(64, 128).T)
    g["sd"] = np.ascontiguousarray(inputs["L0_ffn_down_scale"].reshape(16, 128).T)
    g["l1_wq"] = np.ascontiguousarray(inputs["L1_attn_Wq"])
    g["l1_wk"] = np.ascontiguousarray(inputs["L1_attn_Wk"])
    g["l1_wv"] = np.ascontiguousarray(inputs["L1_attn_Wv"])
    g["l1_bq"] = np.ascontiguousarray(inputs["L1_attn_bq"].T)
    g["l1_bk"] = np.ascontiguousarray(inputs["L1_attn_bk"].T)
    g["l1_bv"] = np.ascontiguousarray(inputs["L1_attn_bv"].T)
    g["routerT"] = np.ascontiguousarray(inputs["L1_ffn_router_gate"].T)
    egu = np.asarray(inputs["L1_ffn_expert_gate_up"])
    g["egu"] = np.ascontiguousarray(
        np.concatenate([egu[..., 0::2], egu[..., 1::2]], axis=-1))
    g["edown"] = np.ascontiguousarray(inputs["L1_ffn_expert_down"])
    sel = np.zeros((8, 8 * 128), dtype=np.float32)
    for e in range(8):
        sel[e, e * 128:(e + 1) * 128] = 1.0
    g["sel8"] = sel
    g["eye"] = np.eye(128, dtype=BF)
    g["eyef"] = np.eye(128, dtype=np.float32)
    x0 = np.asarray(inputs["L0_attn_x"])
    per = []
    for c in range(8):
        s, spans = _core_tokens(c)
        tok = np.concatenate([x0[s, a:b] for a, b in spans], axis=0)
        d = {
            "x0": np.ascontiguousarray(tok.T),
            "amask": _host_masks(c),
        }
        per.append(d)
    return g, per


def _unprep_output(outs):
    res = np.empty((4, 4096, 2048), dtype=np.float32)
    for c in range(8):
        s, spans = _core_tokens(c)
        tm = outs[c].T
        off = 0
        for a, b in spans:
            res[s, a:b] = tm[off:off + (b - a)]
            off += b - a
    return res



def emit_ln(nc, block, sm, B, io):
    x, xn, sxn = io["x"], io["xn"], io["sxn"]
    xr = x.rearrange("(fc p) t -> p fc t", p=128)
    xnr = xn.rearrange("(fc p) t -> p fc t", p=128)
    sld = sm.stream("ld_lnx")
    sst = sm.stream("st_ln")
    spe, sact, sdve = sm["pe"], sm["act"], sm["dve"]

    @block.vector
    def _(v):
        v.memset(B["onesf"][:, :], 1.0)
        v.memset(B["onescf"][:, :], 1.0)
        mi = v.memset(B["epsrow"][:, :], 1e-5)
        sdve.i(mi)

    @block.scalar
    def _(sc):
        sc.wait_ge(sdve.h, sdve.n)
        sact.i(sc.copy(out=B["onesb"][:, :], in_=B["onesf"][:, :]))

    for t in range(TT):
        ts = slice(t * TN, (t + 1) * TN)
        xt_free_at = sm.get("_ln_xt_free", 0)

        @block.sync
        def _(sy, t=t, ts=ts, w=xt_free_at):
            if t > 0:
                sy.wait_ge(sdve.h, w)
            sld.i(sy.dma_start(out=B["xt"][:, :, :], in_=xr[:, :, ts]), 16)
        ld_done = sld.n

        @block.scalar
        def _(sc, w=ld_done):
            sc.wait_ge(sld.h, w)
            for fc in range(FC):
                ins = sc.activation(out=B["xq"][:, fc, :], in_=B["xt"][:, fc, :],
                                    func=AF.Square)
                if fc == FC - 1:
                    sact.i(ins)
        sq_done = sact.n

        @block.tensor
        def _(te, w=ld_done, wq=sq_done):
            te.wait_ge(sld.h, w)
            for fc in range(FC):
                ins = te.matmul(out=B["ps1"][:, :], lhsT=B["onesf"][:, :],
                                rhs=B["xt"][:, fc, :], start=(fc == 0), stop=(fc == FC - 1))
            spe.i(ins)
            te.wait_ge(sact.h, wq)
            for fc in range(FC):
                ins = te.matmul(out=B["ps2"][:, :], lhsT=B["onesf"][:, :],
                                rhs=B["xq"][:, fc, :], start=(fc == 0), stop=(fc == FC - 1))
            spe.i(ins)
        stats_done = spe.n

        @block.scalar
        def _(sc, w=stats_done):
            sc.wait_ge(spe.h, w)
            sact.i(sc.activation(out=B["mrow"][:, :], in_=B["ps1"][:, :],
                                 func=AF.Copy, scale=1.0 / DM))
        m_done = sact.n

        @block.scalar
        def _(sc, w=m_done):
            sc.wait_ge(sact.h, w)
            sact.i(sc.activation(out=B["msq"][:, :], in_=B["mrow"][:, :], func=AF.Square))
        msq_done = sact.n

        @block.vector
        def _(v, w=msq_done, wp=stats_done):
            v.wait_ge(sact.h, w)
            v.wait_ge(spe.h, wp)
            sdve.i(v.scalar_tensor_tensor(out=B["varrow"][:, :], in0=B["ps2"][:, :],
                                          scalar=1.0 / DM, in1=B["msq"][:, :],
                                          op0=OP.mult, op1=OP.subtract))
        var_done = sdve.n

        @block.scalar
        def _(sc, w=var_done):
            sc.wait_ge(sdve.h, w)
            sact.i(sc.activation(out=B["sdrow"][:, :], in_=B["varrow"][:, :],
                                 func=AF.Sqrt, bias=B["epsrow"][:, 0:1]))
        sd_done = sact.n

        @block.vector
        def _(v, w=sd_done):
            v.wait_ge(sact.h, w)
            sdve.i(v.reciprocal(out=B["rstd"][:, :], in_=B["sdrow"][:, :]))
        rstd_done = sdve.n

        @block.tensor
        def _(te, w=rstd_done, wm=m_done):
            te.wait_ge(sact.h, wm)
            te.wait_ge(sdve.h, w)
            te.matmul(out=B["pbm"][:, :], lhsT=B["onescf"][:, :], rhs=B["mrow"][:, :],
                      start=True, stop=True)
            spe.i(te.matmul(out=B["pbr"][:, :], lhsT=B["onescf"][:, :],
                            rhs=B["rstd"][:, :], start=True, stop=True))
        bc_done = spe.n

        @block.vector
        def _(v, w=bc_done):
            v.wait_ge(spe.h, w)
            v.tensor_copy(out=B["mbc"][:, :], in_=B["pbm"][:, :])
            sdve.i(v.tensor_copy(out=B["rbc"][:, :], in_=B["pbr"][:, :]))
        bc_sb = sdve.n

        xnt_busy = sm.get("_ln_xnt_busy")

        @block.vector
        def _(v, w=bc_sb, wb=xnt_busy):
            v.wait_ge(sdve.h, w)
            if wb is not None:
                v.wait_ge(sst.h, wb[0])
                v.wait_ge(spe.h, wb[1])
            for fc in range(FC):
                sdve.i(v.tensor_tensor(out=B["tmp"][:, :], in0=B["xt"][:, fc, :],
                                       in1=B["mbc"][:, :], op=OP.subtract))
                v.wait_ge(sdve.h, sdve.n)
                ins = v.tensor_tensor(out=B["xnt"][:, fc, :], in0=B["tmp"][:, :],
                                      in1=B["rbc"][:, :], op=OP.mult)
            sdve.i(ins)
        xn_done = sdve.n
        sm["_ln_xt_free"] = xn_done

        @block.tensor
        def _(te, w=xn_done):
            te.wait_ge(sdve.h, w)
            for fc in range(FC):
                ins = te.matmul(out=B["psx"][:, :], lhsT=B["onesb"][:, :],
                                rhs=B["xnt"][:, fc, :], start=(fc == 0), stop=(fc == FC - 1))
            spe.i(ins)
        sx_done = spe.n

        @block.vector
        def _(v, w=sx_done):
            v.wait_ge(spe.h, w)
            sdve.i(v.tensor_copy(out=B["sxrow"][:, :], in_=B["psx"][:, :]))
        sx_sb = sdve.n

        @block.gpsimd
        def _(gp, ts=ts, w=xn_done, w2=sx_sb):
            gp.wait_ge(sdve.h, w)
            sst.i(gp.dma_start(out=xnr[:, :, ts], in_=B["xnt"][:, :, :]), 16)
            gp.wait_ge(sdve.h, w2)
            sst.i(gp.dma_start(out=sxn[0:1, ts], in_=B["sxrow"][:, :]), 16)
        sm["_ln_xnt_busy"] = (sst.n, sx_done)


def ln_bufs(nc, stack, pfx=""):
    B = {}
    def sb(name, shape, dt):
        B[name] = stack.enter_context(nc.sbuf_tensor(pfx + name, shape, dt))
    def ps(name, shape, dt):
        B[name] = stack.enter_context(nc.psum_tensor(pfx + name, shape, dt))
    sb("xt", [128, FC, TN], F32)
    sb("xq", [128, FC, TN], F32)
    sb("xnt", [128, FC, TN], BF16)
    sb("tmp", [128, TN], F32)
    sb("onesf", [128, 1], F32)
    sb("onescf", [1, 128], F32)
    sb("onesb", [128, 1], BF16)
    sb("mrow", [1, TN], F32)
    sb("epsrow", [1, 1], F32)
    sb("msq", [1, TN], F32)
    sb("varrow", [1, TN], F32)
    sb("sdrow", [1, TN], F32)
    sb("rstd", [1, TN], F32)
    sb("mbc", [128, TN], F32)
    sb("rbc", [128, TN], F32)
    sb("sxrow", [1, TN], BF16)
    ps("ps1", [1, TN], F32)
    ps("ps2", [1, TN], F32)
    ps("psx", [1, TN], F32)
    ps("pbm", [128, TN], F32)
    ps("pbr", [128, TN], F32)
    return B


class Sems(dict):

    def __init__(self, nc, stack):
        super().__init__()
        self._nc = nc
        self._stack = stack
        for name in ["pe", "act", "dve", "gp", "cc"]:
            self[name] = S(stack.enter_context(nc.semaphore(name)))
        self._streams = {}

    def stream(self, name):
        if name not in self._streams:
            self._streams[name] = S(self._stack.enter_context(self._nc.semaphore(name)))
        return self._streams[name]

    def all_store_streams(self):
        return [s for n, s in self._streams.items() if n.startswith("st_")]


def make_sems(nc, stack):
    return Sems(nc, stack)


def finalize_store_wait(block, sm):
    @block.gpsimd
    def _(gp):
        for s in sm.all_store_streams():
            gp.wait_ge(s.h, s.n)



def build_ln_launch():
    nc = bass.Bass("TRN2", target_bir_lowering=False, debug=False)
    io = {
        "x": nc.dram_tensor("x", [DM, TLOC], F32, kind="ExternalInput").ap(),
        "xn": nc.dram_tensor("xn", [DM, TLOC], BF16, kind="ExternalOutput").ap(),
        "sxn": nc.dram_tensor("sxn", [1, TLOC], BF16, kind="ExternalOutput").ap(),
    }
    with contextlib.ExitStack() as stack:
        B = ln_bufs(nc, stack)
        sm = make_sems(nc, stack)
        blk = stack.enter_context(nc.Block())
        emit_ln(nc, blk, sm, B, io)
        finalize_store_wait(blk, sm)
    return nc




def emit_qkv(nc, block, sm, B, io, layer):
    sld = sm.stream("ld_qkvx")
    sldw = [sm.stream("ld_w0"), sm.stream("ld_w1")]
    sstq = [sm.stream("st_q0"), sm.stream("st_q1")]
    sstkv = sm.stream("st_kv")
    spe, sact, sdve = sm["pe"], sm["act"], sm["dve"]
    xnr = io["xn"].rearrange("(fc p) t -> p fc t", p=128)

    @block.sync
    def _(sy):
        sld.i(sy.dma_start(out=B["xn_sb"][:, :, :], in_=xnr[:, :, :]), 16)
        sld.i(sy.dma_start(out=B["eye_sb"][:, :], in_=io["eye"]), 16)
        if layer == 1:
            sld.i(sy.dma_start(out=B["bq_sb"][:, :], in_=io["bqT"]), 16)
            sld.i(sy.dma_start(out=B["bk_sb"][:, :], in_=io["bkT"]), 16)
            sld.i(sy.dma_start(out=B["bv_sb"][:, :], in_=io["bvT"]), 16)
    xn_ready = sld.n

    chunks = [("q", h) for h in range(NH)] + [("k", g) for g in range(NG)] \
        + [("v", g) for g in range(NG)]
    w_done_pe = {}

    for ci, (kind, idx) in enumerate(chunks):
        slot = ci % 2
        @block.gpsimd
        def _(gp, kind=kind, idx=idx, slot=slot, w=w_done_pe.get(slot)):
            if w is not None:
                gp.wait_ge(spe.h, w)
            if layer == 0:
                src = {"q": io["wq"], "k": io["wk"], "v": io["wv"]}[kind]
                ap = src.rearrange("(fc p) d -> p fc d", p=128)[:, :, idx * 128:(idx + 1) * 128]
            else:
                src = {"q": io["wq"], "k": io["wk"], "v": io["wv"]}[kind]
                ap = src[idx].rearrange("(fc p) d -> p fc d", p=128)[:, :, :]
            sldw[slot].i(gp.dma_start(out=B["wchunk"][slot][:, :, :], in_=ap), 16)
        w_ready = sldw[slot].n

        for t in range(TT):
            ts = slice(t * TN, (t + 1) * TN)
            pq = B["pq"][t % 2]
            pq_free = sm.get(("_qkv_pq", t % 2))

            @block.tensor
            def _(te, w=w_ready, pq=pq, ts=ts, slot=slot, pf=pq_free, xw=xn_ready):
                te.wait_ge(sld.h, xw)
                te.wait_ge(sldw[slot].h, w)
                if pf is not None:
                    te.wait_ge(sdve.h, pf)
                for fc in range(FC):
                    ins = te.matmul(out=pq[:, :], lhsT=B["wchunk"][slot][:, fc, :],
                                    rhs=B["xn_sb"][:, fc, ts],
                                    start=(fc == 0), stop=(fc == FC - 1))
                spe.i(ins)
            mm_done = spe.n

            if kind == "q":
                dst = B["qsb"][idx % 2]
                dsl = slice(t * TN, (t + 1) * TN)
            elif kind == "k":
                dst = B["kvsb"]
                dsl = slice(idx * 2048 + t * TN, idx * 2048 + (t + 1) * TN)
            else:
                dst = B["vtmp"]
                dsl = slice(0, TN)
            qst_wait = sm.get(("_qkv_qst", idx % 2)) if kind == "q" else None

            @block.vector
            def _(v, w=mm_done, pq=pq, kind=kind, idx=idx, dst=dst, dsl=dsl, qw=qst_wait):
                v.wait_ge(spe.h, w)
                if qw is not None and dsl.start == 0:
                    v.wait_ge(sstq[idx % 2].h, qw)
                if layer == 0:
                    ins = v.tensor_copy(out=dst[:, dsl], in_=pq[:, :])
                else:
                    bias = {"q": B["bq_sb"], "k": B["bk_sb"], "v": B["bv_sb"]}[kind]
                    ins = v.tensor_scalar(out=dst[:, dsl], in0=pq[:, :],
                                          scalar1=bias[:, idx:idx + 1], scalar2=None,
                                          op0=OP.add)
                sdve.i(ins)
            sm[("_qkv_pq", t % 2)] = sdve.n
            ev_done = sdve.n

            if kind == "v":
                for j in range(4):
                    kc = t * 4 + j

                    @block.tensor
                    def _(te, w=ev_done, j=j, kc=kc, idx=idx):
                        te.wait_ge(sdve.h, w)
                        if ("_qkv_pt",) in sm:
                            te.wait_ge(sdve.h, sm[("_qkv_pt",)])
                        spe.i(te.transpose(out=B["pt"][:, :],
                                           in_=B["vtmp"][:, j * 128:(j + 1) * 128],
                                           identity=B["eye_sb"][:, :]))
                    tr_done = spe.n

                    @block.vector
                    def _(v, w=tr_done, kc=kc, idx=idx):
                        v.wait_ge(spe.h, w)
                        col = 8192 + idx * 2048 + kc * 128
                        sdve.i(v.tensor_copy(out=B["kvsb"][:, col:col + 128],
                                             in_=B["pt"][:, :]))
                    sm[("_qkv_pt",)] = sdve.n

        if kind == "q":
            q_ev = sdve.n

            @block.gpsimd
            def _(gp, idx=idx, w=q_ev):
                gp.wait_ge(sdve.h, w)
                sstq[idx % 2].i(gp.dma_start(out=io["q_buf"][idx], in_=B["qsb"][idx % 2][:, :]), 16)
            sm[("_qkv_qst", idx % 2)] = sstq[idx % 2].n
        w_done_pe[slot] = spe.n

    @block.gpsimd
    def _(gp):
        gp.wait_ge(sdve.h, sdve.n)
        sstkv.i(gp.dma_start(out=io["kv_cc"], in_=B["kvsb"][:, :]), 16)


def qkv_bufs(nc, stack, layer, pfx=""):
    B = {}
    def sb(name, shape, dt):
        B[name] = stack.enter_context(nc.sbuf_tensor(pfx + name, shape, dt))
    def ps(name, shape, dt):
        B[name] = stack.enter_context(nc.psum_tensor(pfx + name, shape, dt))
    sb("xn_sb", [128, FC, TLOC], BF16)
    B["wchunk"] = [stack.enter_context(nc.sbuf_tensor(pfx + f"wchunk{i}", [128, FC, 128], BF16))
                   for i in range(2)]
    B["qsb"] = [stack.enter_context(nc.sbuf_tensor(pfx + f"qsb{i}", [128, TLOC], BF16))
                for i in range(2)]
    sb("kvsb", [128, 16384], BF16)
    sb("vtmp", [128, TN], BF16)
    sb("eye_sb", [128, 128], BF16)
    if layer == 1:
        sb("bq_sb", [128, NH], F32)
        sb("bk_sb", [128, NG], F32)
        sb("bv_sb", [128, NG], F32)
    B["pq"] = [stack.enter_context(nc.psum_tensor(pfx + f"pq{i}", [128, TN], F32))
               for i in range(2)]
    ps("pt", [128, 128], BF16)
    return B


def build_qkv_launch(layer):
    nc = bass.Bass("TRN2", target_bir_lowering=False, debug=False)
    io = {
        "xn": nc.dram_tensor("xn", [DM, TLOC], BF16, kind="ExternalInput").ap(),
        "eye": nc.dram_tensor("eye", [128, 128], BF16, kind="ExternalInput").ap(),
        "q_buf": nc.dram_tensor("q_buf", [NH, 128, TLOC], BF16, kind="ExternalOutput").ap(),
        "kv_cc": nc.dram_tensor("kv_cc", [128, 16384], BF16, kind="ExternalOutput").ap(),
    }
    if layer == 0:
        io["wq"] = nc.dram_tensor("wq", [DM, DM], F32, kind="ExternalInput").ap()
        io["wk"] = nc.dram_tensor("wk", [DM, 512], F32, kind="ExternalInput").ap()
        io["wv"] = nc.dram_tensor("wv", [DM, 512], F32, kind="ExternalInput").ap()
    else:
        io["wq"] = nc.dram_tensor("wq", [NH, DM, HD], F32, kind="ExternalInput").ap()
        io["wk"] = nc.dram_tensor("wk", [NG, DM, HD], F32, kind="ExternalInput").ap()
        io["wv"] = nc.dram_tensor("wv", [NG, DM, HD], F32, kind="ExternalInput").ap()
        io["bqT"] = nc.dram_tensor("bqT", [128, NH], F32, kind="ExternalInput").ap()
        io["bkT"] = nc.dram_tensor("bkT", [128, NG], F32, kind="ExternalInput").ap()
        io["bvT"] = nc.dram_tensor("bvT", [128, NG], F32, kind="ExternalInput").ap()
    with contextlib.ExitStack() as stack:
        B = qkv_bufs(nc, stack, layer)
        sm = make_sems(nc, stack)
        blk = stack.enter_context(nc.Block())
        emit_qkv(nc, blk, sm, B, io, layer)
        finalize_store_wait(blk, sm)
    return nc




def _mask_index():
    d = {}
    mi = 0
    for r in range(4):
        a, b = MSK_RANGE[r]
        for kc in range(a, b):
            d[(r, kc)] = mi
            mi += 1
    return d

MASK_IDX = _mask_index()


def emit_attn(nc, block, sm, B, io, layer):
    sldm = sm.stream("ld_msk")
    sldkv = sm.stream("ld_kv")
    sldq = [sm.stream("ld_aq0"), sm.stream("ld_aq1")]
    ssto = [sm.stream("st_ao0"), sm.stream("st_ao1")]
    spe, sact, sdve, sgp = sm["pe"], sm["act"], sm["dve"], sm["gp"]
    kvp = io["kv_pair"]

    @block.sync
    def _(sy):
        mr = io["amask"].rearrange("m p c -> p m c")
        sldm.i(sy.dma_start(out=B["msk"][:, :, :], in_=mr), 16)
    msk_ready = sldm.n

    @block.vector
    def _(v):
        sdve.i(v.memset(B["onesb2"][:, :], 1.0))
        v.memset(B["onescf2"][:, :], 1.0)
    ones_dve = sdve.n

    for g in range(NG):
        kv_free = sm.get("_att_kv_free")

        @block.sync
        def _(sy, g=g, w=kv_free, ccd=io.get("cc_dep")):
            if ccd is not None and g == 0:
                sy.wait_ge(ccd[0], ccd[1])
            if w is not None:
                sy.wait_ge(spe.h, w)
            c0 = g * 2048
            sldkv.i(sy.dma_start(out=B["ksb"][:, 0:1024], in_=kvp[0:128, c0:c0 + 1024]), 16)
            sldkv.i(sy.dma_start(out=B["ksb"][:, 1024:3072], in_=kvp[128:256, c0:c0 + 2048]), 16)
            sldkv.i(sy.dma_start(out=B["ksb"][:, 3072:4096], in_=kvp[0:128, c0 + 1024:c0 + 2048]), 16)
            v0 = 8192 + g * 2048
            sldkv.i(sy.dma_start(out=B["vsb"][:, 0:1024], in_=kvp[0:128, v0:v0 + 1024]), 16)
            sldkv.i(sy.dma_start(out=B["vsb"][:, 1024:3072], in_=kvp[128:256, v0:v0 + 2048]), 16)
            sldkv.i(sy.dma_start(out=B["vsb"][:, 3072:4096], in_=kvp[0:128, v0 + 1024:v0 + 2048]), 16)
        kv_ready = sldkv.n

        for hp in range(2):
            heads = (g * 4 + 2 * hp, g * 4 + 2 * hp + 1)
            q_ready = [None, None]
            for hi in (0, 1):
                q_free = sm.get(("_att_q_free", hi))

                @block.sync
                def _(sy, h=heads[hi], hi=hi, w=q_free):
                    if w is not None:
                        sy.wait_ge(spe.h, w)
                    sldq[hi].i(sy.dma_start(out=B["qsb2"][hi][:, :], in_=io["q_buf"][h]), 16)
                q_ready[hi] = sldq[hi].n

            for r in range(4):
                L = LHAT[r]
                nb = L // 2
                qsl = slice(r * TN, (r + 1) * TN)
                avrs_free = [sm.get(("_att_avrs", hi)) for hi in (0, 1)]

                for b in range(nb):
                    for hi in (0, 1):
                        s_idx = 2 * b + hi
                        sslot = s_idx % 2
                        psc_free = sm.get(("_att_psc", sslot))

                        @block.tensor
                        def _(te, b=b, sslot=sslot, qsl=qsl, hi=hi,
                              wk=kv_ready, wq=q_ready[hi], wp=psc_free,
                              wav=(avrs_free[hi] if b == 0 else None)):
                            te.wait_ge(sldq[hi].h, wq)
                            if b == 0:
                                te.wait_ge(sldkv.h, wk)
                            if wp is not None:
                                te.wait_ge(sact.h, wp)
                            if wav is not None:
                                te.wait_ge(sdve.h, wav)
                            for j in range(2):
                                kc = 2 * b + j
                                ins = te.matmul(out=B["psc"][sslot][:, j * TN:(j + 1) * TN],
                                                lhsT=B["ksb"][:, kc * 128:(kc + 1) * 128],
                                                rhs=B["qsb2"][hi][:, qsl],
                                                start=True, stop=True)
                            spe.i(ins)
                        sc_done = spe.n

                        pm_free = sm.get(("_att_pm", sslot))

                        @block.scalar
                        def _(sc, sslot=sslot, w=sc_done, wf=pm_free):
                            sc.wait_ge(spe.h, w)
                            if wf is not None:
                                sc.wait_ge(spe.h, wf)
                            sact.i(sc.activation(out=B["p_sb"][sslot][:, :],
                                                 in_=B["psc"][sslot][:, :],
                                                 func=AF.Exp, scale=SCALE))
                        sm[("_att_psc", sslot)] = sact.n
                        exp_done = sact.n

                        mtiles = [(j, MASK_IDX[(r, 2 * b + j)]) for j in range(2)
                                  if (r, 2 * b + j) in MASK_IDX]
                        mask_done = None
                        if mtiles:
                            @block.vector
                            def _(v, sslot=sslot, mtiles=mtiles, w=exp_done):
                                v.wait_ge(sact.h, w)
                                v.wait_ge(sldm.h, msk_ready)
                                v.wait_ge(sdve.h, ones_dve)
                                for j, mi in mtiles:
                                    ins = v.tensor_tensor(
                                        out=B["p_sb"][sslot][:, j * TN:(j + 1) * TN],
                                        in0=B["p_sb"][sslot][:, j * TN:(j + 1) * TN],
                                        in1=B["msk"][:, mi, :], op=OP.mult)
                                sdve.i(ins)
                            mask_done = sdve.n

                        @block.tensor
                        def _(te, b=b, nb=nb, hi=hi, sslot=sslot,
                              wa=exp_done, wm=mask_done):
                            if wm is not None:
                                te.wait_ge(sdve.h, wm)
                            else:
                                te.wait_ge(sact.h, wa)
                            first = (b == 0)
                            for j in range(2):
                                kc = 2 * b + j
                                last = (b == nb - 1 and j == 1)
                                te.matmul(out=B["prs"][hi][0:1, :], lhsT=B["onesb2"][:, :],
                                          rhs=B["p_sb"][sslot][:, j * TN:(j + 1) * TN],
                                          start=(first and j == 0), stop=last)
                                ins = te.matmul(out=B["pav"][hi][:, :],
                                                lhsT=B["vsb"][:, kc * 128:(kc + 1) * 128],
                                                rhs=B["p_sb"][sslot][:, j * TN:(j + 1) * TN],
                                                start=(first and j == 0), stop=last)
                            spe.i(ins)
                        sm[("_att_pm", sslot)] = spe.n
                av_done = spe.n

                @block.vector
                def _(v, w=av_done):
                    v.wait_ge(spe.h, w)
                    v.reciprocal(out=B["rr"][0][:, :], in_=B["prs"][0][0:1, :])
                    sdve.i(v.reciprocal(out=B["rr"][1][:, :], in_=B["prs"][1][0:1, :]))
                rec_done = sdve.n

                @block.tensor
                def _(te, w=rec_done):
                    te.wait_ge(sdve.h, w)
                    te.matmul(out=B["prs"][0][:, :], lhsT=B["onescf2"][:, :],
                              rhs=B["rr"][0][:, :], start=True, stop=True)
                    spe.i(te.matmul(out=B["prs"][1][:, :], lhsT=B["onescf2"][:, :],
                                    rhs=B["rr"][1][:, :], start=True, stop=True))
                bc_done = spe.n

                for hi in (0, 1):
                    oslot = hi
                    ost_free = sm.get(("_att_ost", oslot))

                    @block.vector
                    def _(v, hi=hi, oslot=oslot, w=bc_done, wo=ost_free):
                        v.wait_ge(spe.h, w)
                        if wo is not None:
                            v.wait_ge(ssto[oslot].h, wo)
                        sdve.i(v.tensor_copy(out=B["rn_sb"][hi][:, :], in_=B["prs"][hi][:, :]))
                        v.wait_ge(sdve.h, sdve.n)
                        sdve.i(v.tensor_tensor(out=B["osb"][oslot][:, :],
                                               in0=B["pav"][hi][:, :],
                                               in1=B["rn_sb"][hi][:, :], op=OP.mult))
                    sm[("_att_avrs", hi)] = sdve.n
                    ev_done = sdve.n

                    @block.gpsimd
                    def _(gp, h=heads[hi], r=r, oslot=oslot, w=ev_done):
                        gp.wait_ge(sdve.h, w)
                        ssto[oslot].i(gp.dma_start(out=io["attn_out"][h][:, r * TN:(r + 1) * TN],
                                                   in_=B["osb"][oslot][:, :]), 16)
                    sm[("_att_ost", oslot)] = ssto[oslot].n
            for hi in (0, 1):
                sm[("_att_q_free", hi)] = spe.n
        sm["_att_kv_free"] = spe.n

def attn_bufs(nc, stack, layer, pfx=""):
    B = {}
    def sb(name, shape, dt):
        B[name] = stack.enter_context(nc.sbuf_tensor(pfx + name, shape, dt))
    sb("ksb", [128, SEQ], BF16)
    sb("vsb", [128, SEQ], BF16)
    B["qsb2"] = [stack.enter_context(nc.sbuf_tensor(pfx + f"qsb2{i}", [128, TLOC], BF16))
                 for i in range(2)]
    sb("msk", [128, N_MSK, TN], BF16)
    B["p_sb"] = [stack.enter_context(nc.sbuf_tensor(pfx + f"p_sb{i}", [128, 2 * TN], BF16))
                 for i in range(2)]
    sb("onesb2", [128, 1], BF16)
    sb("onescf2", [1, 128], F32)
    B["rr"] = [stack.enter_context(nc.sbuf_tensor(pfx + f"rr{i}", [1, TN], F32))
               for i in range(2)]
    B["rn_sb"] = [stack.enter_context(nc.sbuf_tensor(pfx + f"rn_sb{i}", [128, TN], F32))
                  for i in range(2)]
    odt = F32 if layer == 1 else BF16
    B["osb"] = [stack.enter_context(nc.sbuf_tensor(pfx + f"osb{i}", [128, TN], odt))
                for i in range(2)]
    B["psc"] = [stack.enter_context(nc.psum_tensor(pfx + f"psc{i}", [128, 2 * TN], F32))
                for i in range(2)]
    B["pav"] = [stack.enter_context(nc.psum_tensor(pfx + f"pav{i}", [128, TN], F32))
                for i in range(2)]
    B["prs"] = [stack.enter_context(nc.psum_tensor(pfx + f"prs{i}", [128, TN], F32))
                for i in range(2)]
    return B


def build_attn_launch(layer):
    nc = bass.Bass("TRN2", target_bir_lowering=False, debug=False)
    odt = F32 if layer == 1 else BF16
    io = {
        "q_buf": nc.dram_tensor("q_buf", [NH, 128, TLOC], BF16, kind="ExternalInput").ap(),
        "kv_pair": nc.dram_tensor("kv_pair", [256, 16384], BF16, kind="ExternalInput").ap(),
        "amask": nc.dram_tensor("amask", [N_MSK, 128, TN], BF16, kind="ExternalInput").ap(),
        "attn_out": nc.dram_tensor("attn_out", [NH, 128, TLOC], odt, kind="ExternalOutput").ap(),
    }
    with contextlib.ExitStack() as stack:
        B = attn_bufs(nc, stack, layer)
        sm = make_sems(nc, stack)
        blk = stack.enter_context(nc.Block())
        emit_attn(nc, blk, sm, B, io, layer)
        finalize_store_wait(blk, sm)
    return nc




def emit_wo(nc, block, sm, B, io):
    slda = sm.stream("ld_woa")
    sldw = [sm.stream("ld_wow0"), sm.stream("ld_wow1")]
    ssto = [sm.stream("st_wo0"), sm.stream("st_wo1")]
    spe, sdve = sm["pe"], sm["dve"]

    @block.sync
    def _(sy):
        ar = io["attn_out"].rearrange("h p t -> p h t")
        slda.i(sy.dma_start(out=B["asb"][:, :, :], in_=ar), 16)
    a_ready = slda.n

    for oc in range(FC):
        slot = oc % 2
        w_free = sm.get(("_wo_wfree", slot))

        @block.gpsimd
        def _(gp, oc=oc, slot=slot, w=w_free):
            if w is not None:
                gp.wait_ge(spe.h, w)
            ap = io["wo"].rearrange("(ac p) o -> p ac o", p=128)[:, :, oc * 128:(oc + 1) * 128]
            sldw[slot].i(gp.dma_start(out=B["wosb"][slot][:, :, :], in_=ap), 16)
        w_ready = sldw[slot].n

        for t in range(TT):
            ts = slice(t * TN, (t + 1) * TN)
            pslot = t % 2
            p_free = sm.get(("_wo_pfree", pslot))

            @block.tensor
            def _(te, slot=slot, pslot=pslot, ts=ts, w=w_ready, pf=p_free, wa=a_ready):
                te.wait_ge(slda.h, wa)
                te.wait_ge(sldw[slot].h, w)
                if pf is not None:
                    te.wait_ge(sdve.h, pf)
                for ac in range(FC):
                    ins = te.matmul(out=B["pw"][pslot][:, :],
                                    lhsT=B["wosb"][slot][:, ac, :],
                                    rhs=B["asb"][:, ac, ts],
                                    start=(ac == 0), stop=(ac == FC - 1))
                spe.i(ins)
            mm_done = spe.n

            oslot = oc % 2
            st_free = sm.get(("_wo_sfree", oslot)) if t == 0 else None

            @block.vector
            def _(v, pslot=pslot, oslot=oslot, ts=ts, w=mm_done, sf=st_free):
                v.wait_ge(spe.h, w)
                if sf is not None:
                    v.wait_ge(ssto[oslot].h, sf)
                sdve.i(v.tensor_copy(out=B["wout"][oslot][:, ts], in_=B["pw"][pslot][:, :]))
            sm[("_wo_pfree", pslot)] = sdve.n
        ev_done = sdve.n

        @block.gpsimd
        def _(gp, oc=oc, w=ev_done):
            gp.wait_ge(sdve.h, w)
            ssto[oc % 2].i(gp.dma_start(out=io["wo_out"][oc * 128:(oc + 1) * 128, :],
                                        in_=B["wout"][oc % 2][:, :]), 16)
        sm[("_wo_sfree", oc % 2)] = ssto[oc % 2].n
        sm[("_wo_wfree", oc % 2)] = spe.n


def wo_bufs(nc, stack, pfx=""):
    B = {}
    B["asb"] = stack.enter_context(nc.sbuf_tensor(pfx + "asb", [128, FC, TLOC], BF16))
    B["wosb"] = [stack.enter_context(nc.sbuf_tensor(pfx + f"wosb{i}", [128, FC, 128], BF16))
                 for i in range(2)]
    B["wout"] = [stack.enter_context(nc.sbuf_tensor(pfx + f"wout{i}", [128, TLOC], F32))
                 for i in range(2)]
    B["pw"] = [stack.enter_context(nc.psum_tensor(pfx + f"pw{i}", [128, TN], F32))
               for i in range(2)]
    return B


def build_wo_launch():
    nc = bass.Bass("TRN2", target_bir_lowering=False, debug=False)
    io = {
        "attn_out": nc.dram_tensor("attn_out", [NH, 128, TLOC], BF16, kind="ExternalInput").ap(),
        "wo": nc.dram_tensor("wo", [DM, DM], F32, kind="ExternalInput").ap(),
        "wo_out": nc.dram_tensor("wo_out", [DM, TLOC], F32, kind="ExternalOutput").ap(),
    }
    with contextlib.ExitStack() as stack:
        B = wo_bufs(nc, stack)
        sm = make_sems(nc, stack)
        blk = stack.enter_context(nc.Block())
        emit_wo(nc, blk, sm, B, io)
        finalize_store_wait(blk, sm)
    return nc


def emit_ffn0_a(nc, block, sm, B, io):
    sldx = sm.stream("ld_f0x")
    sldw = [sm.stream("ld_f0w0"), sm.stream("ld_f0w1")]
    ssth = [sm.stream("st_f0h0"), sm.stream("st_f0h1")]
    spe, sact, sdve = sm["pe"], sm["act"], sm["dve"]
    xnr = io["xn"].rearrange("(fc p) t -> p fc t", p=128)
    hr = io["h"].rearrange("(ic p) t -> p ic t", p=128)

    @block.sync
    def _(sy):
        sldx.i(sy.dma_start(out=B["xn_sb"][:, :, :], in_=xnr), 16)
        sldx.i(sy.dma_start(out=B["sxn_sb"][:, :], in_=io["sxn"]), 16)
        sldx.i(sy.dma_start(out=B["sg_sb"][:, :], in_=io["sg"]), 16)
        sldx.i(sy.dma_start(out=B["su_sb"][:, :], in_=io["su"]), 16)
    x_ready = sldx.n

    @block.vector
    def _(v):
        v.memset(B["neg128"][:, :], -128.0)
        sdve.i(v.memset(B["onesb3"][:, :], 1.0))
    const_dve = sdve.n

    for ib in range(16):
        slot = ib % 2
        w_free = sm.get(("_f0a_wfree", slot))

        @block.gpsimd
        def _(gp, ib=ib, slot=slot, w=w_free):
            if w is not None:
                gp.wait_ge(spe.h, w)
            isl = slice(ib * 512, (ib + 1) * 512)
            gw = io["gate_wT"].rearrange("(fc p) i -> p fc i", p=128)[:, :, isl]
            uw = io["up_wT"].rearrange("(fc p) i -> p fc i", p=128)[:, :, isl]
            sldw[slot].i(gp.dma_start(out=B["gwsb"][slot][:, :, :], in_=gw), 16)
            sldw[slot].i(gp.dma_start(out=B["uwsb"][slot][:, :, :], in_=uw), 16)
        w_ready = sldw[slot].n

        for ic4 in range(4):
            i = ib * 4 + ic4
            hslot = i % 2
            h_free = sm.get(("_f0a_hfree", hslot))
            for t in range(TT):
                ts = slice(t * TN, (t + 1) * TN)
                pp = t % 2
                pg, pu = B["pg"][pp], B["pu"][pp]
                p_free = sm.get(("_f0a_pfree", pp))

                @block.tensor
                def _(te, ib=ib, ic4=ic4, t=t, ts=ts, slot=slot, pg=pg, pu=pu,
                      w=w_ready, xw=x_ready, pf=p_free, cd=const_dve):
                    te.wait_ge(sldx.h, xw)
                    te.wait_ge(sldw[slot].h, w)
                    te.wait_ge(sdve.h, cd)
                    if pf is not None:
                        te.wait_ge(sdve.h, pf)
                    csl = slice(ic4 * 128, (ic4 + 1) * 128)
                    for fc in range(FC):
                        te.matmul(out=pg[:, :], lhsT=B["gwsb"][slot][:, fc, csl],
                                  rhs=B["xn_sb"][:, fc, ts], start=(fc == 0), stop=False)
                    te.matmul(out=pg[:, :], lhsT=B["neg128"][:, :],
                              rhs=B["sxn_sb"][0:1, ts], start=False, stop=True)
                    for fc in range(FC):
                        te.matmul(out=pu[:, :], lhsT=B["uwsb"][slot][:, fc, csl],
                                  rhs=B["xn_sb"][:, fc, ts], start=(fc == 0), stop=False)
                    ins = te.matmul(out=pu[:, :], lhsT=B["neg128"][:, :],
                                    rhs=B["sxn_sb"][0:1, ts], start=False, stop=True)
                    spe.i(ins)
                mm_done = spe.n

                aslot = t % 2

                @block.scalar
                def _(sc, i=i, pg=pg, aslot=aslot, w=mm_done):
                    sc.wait_ge(spe.h, w)
                    af = sm.get(("_f0a_afree", aslot))
                    if af is not None:
                        sc.wait_ge(sdve.h, af)
                    sact.i(sc.activation(out=B["silu_sb"][aslot][:, :], in_=pg[:, :],
                                         func=AF.Silu, scale=B["sg_sb"][:, i:i + 1]))
                silu_done = sact.n

                @block.vector
                def _(v, i=i, t=t, ts=ts, pu=pu, aslot=aslot, hslot=hslot,
                      w=silu_done, hf=(h_free if t == 0 else None)):
                    v.wait_ge(sact.h, w)
                    if hf is not None:
                        v.wait_ge(ssth[hslot].h, hf[0])
                        v.wait_ge(spe.h, hf[1])
                    sdve.i(v.scalar_tensor_tensor(out=B["hsb"][hslot][:, ts], in0=pu[:, :],
                                                  scalar=B["su_sb"][:, i:i + 1],
                                                  in1=B["silu_sb"][aslot][:, :],
                                                  op0=OP.mult, op1=OP.mult))
                sm[("_f0a_afree", aslot)] = sdve.n
                sm[("_f0a_pfree", pp)] = sdve.n
                h_done = sdve.n

                @block.tensor
                def _(te, i=i, t=t, ts=ts, hslot=hslot, w=h_done):
                    te.wait_ge(sdve.h, w)
                    ins = te.matmul(out=B["phrs"][t][:, :], lhsT=B["onesb3"][:, :],
                                    rhs=B["hsb"][hslot][:, ts],
                                    start=(i == 0), stop=(i == 63))
                    spe.i(ins)
            hrs_done = spe.n

            @block.gpsimd
            def _(gp, i=i, hslot=hslot, w=hrs_done):
                gp.wait_ge(spe.h, w)
                ssth[hslot].i(gp.dma_start(out=hr[:, i, :], in_=B["hsb"][hslot][:, :]), 16)
            sm[("_f0a_hfree", hslot)] = (ssth[hslot].n, spe.n)
        sm[("_f0a_wfree", slot)] = spe.n

    @block.vector
    def _(v):
        v.wait_ge(spe.h, spe.n)
        for t in range(TT):
            ins = v.tensor_copy(out=B["hrs_sb"][0:1, t, :], in_=B["phrs"][t][0:1, :])
        sdve.i(ins)
    sm["_f0_hrs_ready"] = sdve.n


def emit_ffn0_b(nc, block, sm, B, io):
    sldw = [sm.stream("ld_f0dw0"), sm.stream("ld_f0dw1")]
    sldh = [sm.stream("ld_f0h0"), sm.stream("ld_f0h1")]
    sldx = [sm.stream("ld_f0r0"), sm.stream("ld_f0r1")]
    sstx = [sm.stream("st_f0x0"), sm.stream("st_f0x1")]
    sldc = sm.stream("ld_f0c")
    spe, sact, sdve = sm["pe"], sm["act"], sm["dve"]
    hr = io["h"].rearrange("(ic p) t -> p ic t", p=128)
    dwr = io["down_wT"].rearrange("(ic p) o -> p ic o", p=128)
    x0r = io["x0"].rearrange("(oc p) t -> p oc t", p=128)
    wor = io["wo_out"].rearrange("(oc p) t -> p oc t", p=128)
    x1r = io["x1"].rearrange("(oc p) t -> p oc t", p=128)
    hrs_ready = sm["_f0_hrs_ready"]

    @block.sync
    def _(sy):
        sldc.i(sy.dma_start(out=B["sd_sb"][:, :], in_=io["sd"]), 16)
    sd_ready = sldc.n

    for mt in range(4):
        osl = slice(mt * 512, (mt + 1) * 512)
        dw_ready = [None, None]
        for half in range(2):
            dw_free = sm.get(("_f0b_dwfree", half))

            @block.gpsimd
            def _(gp, mt=mt, half=half, osl=osl, w=dw_free):
                if mt == 0:
                    gp.wait_ge(sdve.h, hrs_ready)
                if w is not None:
                    gp.wait_ge(spe.h, w)
                sldw[half].i(gp.dma_start(out=B["dwsb"][half][:, :, :],
                                          in_=dwr[:, half * 32:(half + 1) * 32, osl]), 16)
            dw_ready[half] = sldw[half].n

        for t in range(TT):
            ts = slice(t * TN, (t + 1) * TN)
            ht_ready = [None, None]
            for half in range(2):
                ht_free = sm.get(("_f0b_htfree", half))

                @block.sync
                def _(sy, t=t, ts=ts, half=half, mt=mt, w=ht_free):
                    if mt == 0 and t == 0:
                        sy.wait_ge(sdve.h, hrs_ready)
                    if w is not None:
                        sy.wait_ge(spe.h, w)
                    sldh[half].i(sy.dma_start(out=B["htsb"][half][:, :, :],
                                              in_=hr[:, half * 32:(half + 1) * 32, ts]), 16)
                ht_ready[half] = sldh[half].n

            for oc4 in range(4):
                o = mt * 4 + oc4
                pp = oc4 % 2
                p_free = sm.get(("_f0b_pfree", pp))
                rslot = oc4 % 2
                r_free = sm.get(("_f0b_rfree", rslot))

                @block.sync
                def _(sy, o=o, ts=ts, rslot=rslot, w=r_free,
                      first=(mt == 0 and t == 0 and oc4 < 2)):
                    if first:
                        sy.wait_ge(sdve.h, hrs_ready)
                    if w is not None:
                        sy.wait_ge(sdve.h, w)
                    sldx[rslot].i(sy.dma_start(out=B["x0t"][rslot][:, :], in_=x0r[:, o, ts]), 16)
                    sldx[rslot].i(sy.dma_start(out=B["wot"][rslot][:, :], in_=wor[:, o, ts]), 16)
                r_ready = sldx[rslot].n

                @block.tensor
                def _(te, o=o, t=t, pp=pp, wd=dw_ready, wh=ht_ready, pf=p_free,
                      whr=hrs_ready):
                    te.wait_ge(sdve.h, whr)
                    if pf is not None:
                        te.wait_ge(sdve.h, pf)
                    csl = slice((o % 4) * 128, (o % 4 + 1) * 128)
                    for half in range(2):
                        te.wait_ge(sldw[half].h, wd[half])
                        te.wait_ge(sldh[half].h, wh[half])
                        for ic in range(32):
                            te.matmul(out=B["pd"][pp][:, :],
                                      lhsT=B["dwsb"][half][:, ic, csl],
                                      rhs=B["htsb"][half][:, ic, :],
                                      start=(half == 0 and ic == 0), stop=False)
                    ins = te.matmul(out=B["pd"][pp][:, :], lhsT=B["neg128"][:, :],
                                    rhs=B["hrs_sb"][0:1, t, :], start=False, stop=True)
                    spe.i(ins)
                mm_done = spe.n

                xslot = (mt * 16 + t * 4 + oc4) % 2
                xs_free = sm.get(("_f0b_xsfree", xslot))

                @block.vector
                def _(v, o=o, pp=pp, rslot=rslot, xslot=xslot, w=mm_done,
                      wr=r_ready, xf=xs_free, wsd=sd_ready):
                    v.wait_ge(spe.h, w)
                    v.wait_ge(sldx[rslot].h, wr)
                    v.wait_ge(sldc.h, wsd)
                    if xf is not None:
                        v.wait_ge(sstx[xslot].h, xf)
                    sdve.i(v.scalar_tensor_tensor(out=B["tmpb"][:, :], in0=B["pd"][pp][:, :],
                                                  scalar=B["sd_sb"][:, o:o + 1],
                                                  in1=B["wot"][rslot][:, :],
                                                  op0=OP.mult, op1=OP.add))
                    v.wait_ge(sdve.h, sdve.n)
                    sdve.i(v.tensor_tensor(out=B["x1t"][xslot][:, :], in0=B["tmpb"][:, :],
                                           in1=B["x0t"][rslot][:, :], op=OP.add))
                sm[("_f0b_pfree", pp)] = sdve.n
                sm[("_f0b_rfree", rslot)] = sdve.n
                ev_done = sdve.n

                @block.gpsimd
                def _(gp, o=o, ts=ts, xslot=xslot, w=ev_done):
                    gp.wait_ge(sdve.h, w)
                    sstx[xslot].i(gp.dma_start(out=x1r[:, o, ts], in_=B["x1t"][xslot][:, :]), 16)
                sm[("_f0b_xsfree", xslot)] = sstx[xslot].n
            for half in range(2):
                sm[("_f0b_htfree", half)] = spe.n
        for half in range(2):
            sm[("_f0b_dwfree", half)] = spe.n

def ffn0_small_bufs(nc, stack, pfx=""):
    B = {}
    B["neg128"] = stack.enter_context(nc.sbuf_tensor(pfx + "neg128", [1, 128], BF16))
    B["onesb3"] = stack.enter_context(nc.sbuf_tensor(pfx + "onesb3", [128, 1], BF16))
    B["hrs_sb"] = stack.enter_context(nc.sbuf_tensor(pfx + "hrs_sb", [1, TT, TN], BF16))
    B["sxn_sb"] = stack.enter_context(nc.sbuf_tensor(pfx + "sxn_sb", [1, TLOC], BF16))
    B["sg_sb"] = stack.enter_context(nc.sbuf_tensor(pfx + "sg_sb", [128, 64], F32))
    B["su_sb"] = stack.enter_context(nc.sbuf_tensor(pfx + "su_sb", [128, 64], F32))
    B["sd_sb"] = stack.enter_context(nc.sbuf_tensor(pfx + "sd_sb", [128, 16], F32))
    return B


def ffn0_a_bufs(nc, sa, B, pfx=""):
    B["xn_sb"] = sa.enter_context(nc.sbuf_tensor(pfx + "xn_sb", [128, FC, TLOC], BF16))
    B["gwsb"] = [sa.enter_context(nc.sbuf_tensor(pfx + f"gwsb{i}", [128, FC, 512], BF16))
                 for i in range(2)]
    B["uwsb"] = [sa.enter_context(nc.sbuf_tensor(pfx + f"uwsb{i}", [128, FC, 512], BF16))
                 for i in range(2)]
    B["hsb"] = [sa.enter_context(nc.sbuf_tensor(pfx + f"hsb{i}", [128, TLOC], BF16))
                for i in range(2)]
    B["silu_sb"] = [sa.enter_context(nc.sbuf_tensor(pfx + f"silu{i}", [128, TN], F32))
                    for i in range(2)]
    B["pg"] = [sa.enter_context(nc.psum_tensor(pfx + f"pg{i}", [128, TN], F32))
               for i in range(2)]
    B["pu"] = [sa.enter_context(nc.psum_tensor(pfx + f"pu{i}", [128, TN], F32))
               for i in range(2)]
    B["phrs"] = [sa.enter_context(nc.psum_tensor(pfx + f"phrs{i}", [1, TN], F32))
                 for i in range(TT)]
    return B


def ffn0_b_bufs(nc, sb_, B, pfx=""):
    B["dwsb"] = [sb_.enter_context(nc.sbuf_tensor(pfx + f"dwsb{i}", [128, 32, 512], BF16))
                 for i in range(2)]
    B["htsb"] = [sb_.enter_context(nc.sbuf_tensor(pfx + f"htsb{i}", [128, 32, TN], BF16))
                 for i in range(2)]
    B["x0t"] = [sb_.enter_context(nc.sbuf_tensor(pfx + f"x0t{i}", [128, TN], F32))
                for i in range(2)]
    B["wot"] = [sb_.enter_context(nc.sbuf_tensor(pfx + f"wot{i}", [128, TN], F32))
                for i in range(2)]
    B["x1t"] = [sb_.enter_context(nc.sbuf_tensor(pfx + f"x1t{i}", [128, TN], F32))
                for i in range(2)]
    B["tmpb"] = sb_.enter_context(nc.sbuf_tensor(pfx + "tmpb", [128, TN], F32))
    B["pd"] = [sb_.enter_context(nc.psum_tensor(pfx + f"pd{i}", [128, TN], F32))
               for i in range(2)]
    return B


def moe_router_scoped(nc, rstack, B, pfx=""):
    B["pl_t"] = rstack.enter_context(nc.psum_tensor(pfx + "pl", [128, TN], F32))
    B["pw8_t"] = rstack.enter_context(nc.psum_tensor(pfx + "pw8", [128, TLOC], F32))
    return B


def moe_expert_psums(nc, estack, B, pfx=""):
    B["pwb"] = estack.enter_context(nc.psum_tensor(pfx + "pwb", [128, TN], F32))
    B["pg2"] = [estack.enter_context(nc.psum_tensor(pfx + f"pg2{i}", [128, TN], F32))
                for i in range(2)]
    B["pu2"] = [estack.enter_context(nc.psum_tensor(pfx + f"pu2{i}", [128, TN], F32))
                for i in range(2)]
    B["pd2"] = [estack.enter_context(nc.psum_tensor(pfx + f"pd2{i}", [128, TN], F32))
                for i in range(2)]
    return B


def build_ffn0_launch():
    nc = bass.Bass("TRN2", target_bir_lowering=False, debug=False)
    io = {
        "xn": nc.dram_tensor("xn", [DM, TLOC], BF16, kind="ExternalInput").ap(),
        "sxn": nc.dram_tensor("sxn", [1, TLOC], BF16, kind="ExternalInput").ap(),
        "gate_wT": nc.dram_tensor("gate_wT", [DM, I0], I32, kind="ExternalInput").ap(),
        "up_wT": nc.dram_tensor("up_wT", [DM, I0], I32, kind="ExternalInput").ap(),
        "down_wT": nc.dram_tensor("down_wT", [I0, DM], I32, kind="ExternalInput").ap(),
        "sg": nc.dram_tensor("sg", [128, 64], F32, kind="ExternalInput").ap(),
        "su": nc.dram_tensor("su", [128, 64], F32, kind="ExternalInput").ap(),
        "sd": nc.dram_tensor("sd", [128, 16], F32, kind="ExternalInput").ap(),
        "x0": nc.dram_tensor("x0", [DM, TLOC], F32, kind="ExternalInput").ap(),
        "wo_out": nc.dram_tensor("wo_out", [DM, TLOC], F32, kind="ExternalInput").ap(),
        "x1": nc.dram_tensor("x1", [DM, TLOC], F32, kind="ExternalOutput").ap(),
        "h": nc.dram_tensor("h", [I0, TLOC], BF16, kind="Internal").ap(),
    }
    with contextlib.ExitStack() as stack:
        B = ffn0_small_bufs(nc, stack, "")
        sm = make_sems(nc, stack)
        blk = stack.enter_context(nc.Block())
        with contextlib.ExitStack() as sa:
            ffn0_a_bufs(nc, sa, B, "")
            emit_ffn0_a(nc, blk, sm, B, io)
        with contextlib.ExitStack() as sb_:
            ffn0_b_bufs(nc, sb_, B, "")
            emit_ffn0_b(nc, blk, sm, B, io)
            finalize_store_wait(blk, sm)
    return nc
def emit_router(nc, block, sm, B, io):
    sldx = sm.stream("ld_rtx")
    sldr = sm.stream("ld_rtw")
    spe, sact, sdve = sm["pe"], sm["act"], sm["dve"]
    xnr = io["xn"].rearrange("(fc p) t -> p fc t", p=128)

    @block.gpsimd
    def _(gp):
        rr = io["routerT"].rearrange("(fc p) e -> p fc e", p=128)
        sldr.i(gp.dma_start(out=B["rt_sb"][:, :, :], in_=rr), 16)
        sldr.i(gp.dma_start(out=B["eyef_sb"][:, :], in_=io["eyef"]), 16)
        sldr.i(gp.dma_start(out=B["sel_sb"][:, :], in_=io["sel8"]), 16)
    r_ready = sldr.n

    for tc in range(16):
        xslot = tc % 2
        x_free = sm.get(("_rt_xfree", xslot))

        @block.sync
        def _(sy, tc=tc, xslot=xslot, w=x_free):
            if w is not None:
                sy.wait_ge(spe.h, w)
            sldx.i(sy.dma_start(out=B["gslice"][xslot][:, :, :],
                                in_=xnr[:, :, tc * 128:(tc + 1) * 128]), 16)
        x_ready = sldx.n

        pl_free = sm.get("_rt_plfree")

        @block.tensor
        def _(te, xslot=xslot, w=x_ready, wr=r_ready, pf=pl_free):
            te.wait_ge(sldx.h, w)
            te.wait_ge(sldr.h, wr)
            if pf is not None:
                te.wait_ge(sdve.h, pf)
            for fc in range(FC):
                ins = te.matmul(out=B["pl_t"][:, 0:NE], lhsT=B["gslice"][xslot][:, fc, :],
                                rhs=B["rt_sb"][:, fc, :], start=(fc == 0), stop=(fc == FC - 1))
            spe.i(ins)
        sm[("_rt_xfree", xslot)] = spe.n
        lg_done = spe.n

        @block.vector
        def _(v, w=lg_done):
            v.wait_ge(spe.h, w)
            sdve.i(v.tensor_reduce(out=B["m1"][:, :], in_=B["pl_t"][:, 0:NE], axis=AX.X, op=OP.max))
            v.wait_ge(sdve.h, sdve.n)
            sdve.i(v.tensor_scalar(out=B["eq1"][:, :], in0=B["pl_t"][:, 0:NE],
                                   scalar1=B["m1"][:, 0:1], scalar2=None, op0=OP.is_equal))
            v.wait_ge(sdve.h, sdve.n)
            sdve.i(v.scalar_tensor_tensor(out=B["l2"][:, :], in0=B["eq1"][:, :], scalar=-1e30,
                                          in1=B["pl_t"][:, 0:NE], op0=OP.mult, op1=OP.add))
            v.wait_ge(sdve.h, sdve.n)
            sdve.i(v.tensor_reduce(out=B["m2"][:, :], in_=B["l2"][:, :], axis=AX.X, op=OP.max))
            v.wait_ge(sdve.h, sdve.n)
            sdve.i(v.tensor_tensor(out=B["dd"][:, :], in0=B["m1"][:, :],
                                   in1=B["m2"][:, :], op=OP.subtract))
        sm["_rt_plfree"] = sdve.n
        mx_done = sdve.n

        @block.scalar
        def _(sc, w=mx_done):
            sc.wait_ge(sdve.h, w)
            sact.i(sc.activation(out=B["s1"][:, :], in_=B["dd"][:, :], func=AF.Sigmoid))
        s1_done = sact.n

        @block.vector
        def _(v, tc=tc, w=s1_done):
            v.wait_ge(sact.h, w)
            v.tensor_scalar(out=B["s2"][:, :], in0=B["s1"][:, :], scalar1=-1.0,
                            scalar2=1.0, op0=OP.mult, op1=OP.add)
            v.tensor_scalar(out=B["eq2"][:, :], in0=B["l2"][:, :],
                            scalar1=B["m2"][:, 0:1], scalar2=None, op0=OP.is_equal)
            sdve.i(v.tensor_scalar(out=B["w1"][:, :], in0=B["eq1"][:, :],
                                   scalar1=B["s1"][:, 0:1], scalar2=None, op0=OP.mult))
            v.wait_ge(sdve.h, sdve.n)
            sdve.i(v.scalar_tensor_tensor(out=B["wsb"][:, tc, :], in0=B["eq2"][:, :],
                                          scalar=B["s2"][:, 0:1], in1=B["w1"][:, :],
                                          op0=OP.mult, op1=OP.add))
        w_done = sdve.n

    all_w = sdve.n

    @block.tensor
    def _(te, w=all_w):
        te.wait_ge(sdve.h, w)
        for tc in range(16):
            ins = te.matmul(out=B["pw8_t"][0:NE, tc * 128:(tc + 1) * 128],
                            lhsT=B["wsb"][:, tc, :],
                            rhs=B["eyef_sb"][:, :], start=True, stop=True)
        spe.i(ins)
    tr_done = spe.n

    @block.vector
    def _(v, w=tr_done):
        v.wait_ge(spe.h, w)
        sdve.i(v.tensor_copy(out=B["w8sb"][0:8, :], in_=B["pw8_t"][0:NE, 0:TLOC]))
    sm["_rt_done"] = (sdve.n, spe.n)



def emit_moe_prepass(nc, block, sm, B, io):
    stmo = [sm.stream(f"st_mo{t}") for t in range(TT)]
    sdve = sm["dve"]
    a1r = io["attn1"].rearrange("h p t -> (h p) t")
    rt_done = sm["_rt_done"]

    @block.gpsimd
    def _(gp):
        gp.wait_ge(sdve.h, rt_done[0])
        for t in range(TT):
            ts = slice(t * TN, (t + 1) * TN)
            stmo[t].i(gp.dma_start(out=io["out"][:, ts], in_=io["x1"][:, ts]), 16)
        for t in range(TT):
            ts = slice(t * TN, (t + 1) * TN)
            gp.wait_ge(stmo[t].h, stmo[t].n)
            stmo[t].i(gp.dma_start(out=io["out"][:, ts], in_=a1r[:, ts],
                                   accum_op=OP.add), 16)
            sm[("_me_posb", t)] = (stmo[t].n, t)
    sm["_me_prepass_done"] = sdve.n


def emit_moe(nc, block, sm, B, io):
    sldx = sm.stream("ld_mex")
    sldg = [sm.stream("ld_meg0"), sm.stream("ld_meg1")]
    sldd2 = [sm.stream("ld_med0"), sm.stream("ld_med1")]
    stmo = [sm.stream(f"st_mo{t}") for t in range(TT)]
    spe, sact, sdve = sm["pe"], sm["act"], sm["dve"]
    xnr = io["xn"].rearrange("(fc p) t -> p fc t", p=128)
    outr = io["out"].rearrange("(oc p) t -> p oc t", p=128)
    rt_done = sm["_rt_done"]
    prepass_done = sm["_me_prepass_done"]
    prepass_keys = [(t, sm[("_me_posb", t)][0]) for t in range(TT)
                    if ("_me_posb", t) in sm]

    @block.sync
    def _(sy):
        sldx.i(sy.dma_start(out=B["xn_sb2"][:, :, :], in_=xnr), 16)
    xn_ready = sldx.n

    for e in range(NE):
        hslot = e % 2
        h_free = sm.get(("_me_hfree", hslot))
        for ic in range(8):
            wslot = ic % 2
            w_free = sm.get(("_me_wfree", wslot))

            @block.gpsimd
            def _(gp, e=e, ic=ic, wslot=wslot, w=w_free, first=(e == 0 and ic < 2)):
                if first:
                    gp.wait_ge(sdve.h, max(rt_done[0], prepass_done))
                if w is not None:
                    gp.wait_ge(spe.h, w)
                gu = io["egu"][e].rearrange("(fc p) o -> p fc o", p=128)
                sldg[wslot].i(gp.dma_start(
                    out=B["gslice"][wslot][:, :, :],
                    in_=gu[:, :, ic * 128:(ic + 1) * 128]), 16)
                sldg[wslot].i(gp.dma_start(
                    out=B["uslice"][wslot][:, :, :],
                    in_=gu[:, :, 1024 + ic * 128:1024 + (ic + 1) * 128]), 16)
            w_ready = sldg[wslot].n

            for t in range(TT):
                ts = slice(t * TN, (t + 1) * TN)
                pp = (ic * TT + t) % 2
                p_free = sm.get(("_me_pgu", pp))

                @block.tensor
                def _(te, ts=ts, pp=pp, wslot=wslot, w=w_ready, xw=xn_ready,
                      pf=p_free, hf=(h_free if ic == 0 and t == 0 else None)):
                    te.wait_ge(sldx.h, xw)
                    te.wait_ge(sldg[wslot].h, w)
                    if pf is not None:
                        te.wait_ge(sdve.h, pf)
                    if hf is not None:
                        te.wait_ge(spe.h, hf)
                    for fc in range(FC):
                        te.matmul(out=B["pg2"][pp][:, :], lhsT=B["gslice"][wslot][:, fc, :],
                                  rhs=B["xn_sb2"][:, fc, ts], start=(fc == 0), stop=(fc == FC - 1))
                    for fc in range(FC):
                        ins = te.matmul(out=B["pu2"][pp][:, :], lhsT=B["uslice"][wslot][:, fc, :],
                                        rhs=B["xn_sb2"][:, fc, ts], start=(fc == 0), stop=(fc == FC - 1))
                    spe.i(ins)
                mm_done = spe.n

                aslot = (ic * TT + t) % 2

                @block.scalar
                def _(sc, pp=pp, aslot=aslot, w=mm_done):
                    sc.wait_ge(spe.h, w)
                    af = sm.get(("_me_afree", aslot))
                    if af is not None:
                        sc.wait_ge(sdve.h, af)
                    sact.i(sc.activation(out=B["silu2"][aslot][:, :], in_=B["pg2"][pp][:, :],
                                         func=AF.Silu))
                silu_done = sact.n

                @block.vector
                def _(v, ic=ic, t=t, ts=ts, pp=pp, aslot=aslot, hslot=hslot, w=silu_done):
                    v.wait_ge(sact.h, w)
                    sdve.i(v.scalar_tensor_tensor(out=B["h_et"][hslot][:, ic, ts],
                                                  in0=B["pu2"][pp][:, :], scalar=1.0,
                                                  in1=B["silu2"][aslot][:, :],
                                                  op0=OP.mult, op1=OP.mult))
                sm[("_me_afree", aslot)] = sdve.n
                sm[("_me_pgu", pp)] = sdve.n
            sm[("_me_wfree", wslot)] = spe.n
        h_done = sdve.n

        for t in range(TT):
            ts = slice(t * TN, (t + 1) * TN)
            pwb_free = sm.get("_me_pwbfree")

            @block.tensor
            def _(te, e=e, ts=ts, w=rt_done, pf=pwb_free):
                te.wait_ge(sdve.h, w[0])
                if pf is not None:
                    te.wait_ge(sdve.h, pf)
                spe.i(te.matmul(out=B["pwb"][:, :], lhsT=B["sel_sb"][:, e * 128:(e + 1) * 128],
                                rhs=B["w8sb"][0:8, ts], start=True, stop=True))
            wbmm = spe.n

            @block.vector
            def _(v, t=t, w=wbmm, wp=sm.get("_me_wbfree")):
                v.wait_ge(spe.h, w)
                if wp is not None:
                    v.wait_ge(sdve.h, wp)
                sdve.i(v.tensor_copy(out=B["wbcs4"][:, t, :], in_=B["pwb"][:, :]))
                v.wait_ge(sdve.h, sdve.n)
            sm["_me_pwbfree"] = sdve.n
        wbc_done = sdve.n

        for oc in range(16):
            dslot = oc % 2
            ds_free = sm.get(("_me_dsfree", dslot))

            @block.gpsimd
            def _(gp, e=e, oc=oc, dslot=dslot, w=ds_free,
                  first=(e == 0 and oc < 2)):
                if first:
                    gp.wait_ge(sdve.h, max(rt_done[0], prepass_done))
                if w is not None:
                    gp.wait_ge(spe.h, w)
                dw = io["edown"][e].rearrange("(ic p) o -> p ic o", p=128)
                sldd2[dslot].i(gp.dma_start(
                    out=B["dslice"][dslot][:, :, :],
                    in_=dw[:, :, oc * 128:(oc + 1) * 128]), 16)
            d_rdy = sldd2[dslot].n

            mslot = oc % 2
            m_free = sm.get(("_me_mo", mslot))
            for t in range(TT):
                ts = slice(t * TN, (t + 1) * TN)
                pp = (oc * TT + t) % 2
                pd_free = sm.get(("_me_pd", pp))

                @block.tensor
                def _(te, ts=ts, pp=pp, dslot=dslot, hslot=hslot,
                      w=h_done, wd=d_rdy, pf=pd_free):
                    te.wait_ge(sdve.h, w)
                    te.wait_ge(sldd2[dslot].h, wd)
                    if pf is not None:
                        te.wait_ge(sdve.h, pf)
                    for ic in range(8):
                        ins = te.matmul(out=B["pd2"][pp][:, :],
                                        lhsT=B["dslice"][dslot][:, ic, :],
                                        rhs=B["h_et"][hslot][:, ic, ts],
                                        start=(ic == 0), stop=(ic == 7))
                    spe.i(ins)
                dmm_done = spe.n

                @block.vector
                def _(v, t=t, ts=ts, pp=pp, mslot=mslot, w=dmm_done,
                      wb=wbc_done, mf=(m_free if t == 0 else None)):
                    v.wait_ge(spe.h, w)
                    v.wait_ge(sdve.h, wb)
                    if mf is not None:
                        v.wait_ge(stmo[mf[1]].h, mf[0])
                    sdve.i(v.tensor_tensor(out=B["mo_oc"][mslot][:, ts],
                                           in0=B["pd2"][pp][:, :],
                                           in1=B["wbcs4"][:, t, :], op=OP.mult))
                sm[("_me_pd", pp)] = sdve.n
            ev_done = sdve.n
            sm[("_me_dsfree", dslot)] = spe.n

            @block.gpsimd
            def _(gp, e=e, oc=oc, mslot=mslot, w=ev_done, pp_done=prepass_keys):
                gp.wait_ge(sdve.h, w)
                st = stmo[oc % 4]
                if e == 0:
                    for tt_, cnt in pp_done:
                        gp.wait_ge(stmo[tt_].h, cnt)
                gp.wait_ge(st.h, st.n)
                st.i(gp.dma_start(out=outr[:, oc, :], in_=B["mo_oc"][mslot][:, :],
                                  accum_op=OP.add), 16)
            sm[("_me_mo", mslot)] = (stmo[oc % 4].n, oc % 4)
        sm["_me_wbfree"] = sdve.n
        sm[("_me_hfree", hslot)] = spe.n
        sm["_me_dfree"] = spe.n


def build_moe_launch():
    pfx = ""
    nc = bass.Bass("TRN2", target_bir_lowering=False, debug=False)
    io = {
        "xn": nc.dram_tensor("xn", [DM, TLOC], BF16, kind="ExternalInput").ap(),
        "routerT": nc.dram_tensor("routerT", [DM, NE], F32, kind="ExternalInput").ap(),
        "egu": nc.dram_tensor("egu", [NE, DM, DM], F32, kind="ExternalInput").ap(),
        "edown": nc.dram_tensor("edown", [NE, IM, DM], F32, kind="ExternalInput").ap(),
        "sel8": nc.dram_tensor("sel8", [NE, NE * 128], F32, kind="ExternalInput").ap(),
        "eyef": nc.dram_tensor("eyef", [128, 128], F32, kind="ExternalInput").ap(),
        "x1": nc.dram_tensor("x1", [DM, TLOC], F32, kind="ExternalInput").ap(),
        "attn1": nc.dram_tensor("attn1", [NH, 128, TLOC], F32, kind="ExternalInput").ap(),
        "out": nc.dram_tensor("out", [DM, TLOC], F32, kind="ExternalOutput").ap(),
    }
    with contextlib.ExitStack() as stack:
        B = moe_bufs(nc, stack)
        sm = make_sems(nc, stack)
        blk = stack.enter_context(nc.Block())
        with contextlib.ExitStack() as rstack:
            moe_router_scoped(nc, rstack, B, "")
            emit_router(nc, blk, sm, B, io)
            emit_moe_prepass(nc, blk, sm, B, io)
        with contextlib.ExitStack() as estack:
            B["pwb"] = estack.enter_context(nc.psum_tensor("pwb", [128, TN], F32))
            B["pg2"] = [estack.enter_context(nc.psum_tensor(f"pg2{i}", [128, TN], F32))
                        for i in range(2)]
            B["pu2"] = [estack.enter_context(nc.psum_tensor(f"pu2{i}", [128, TN], F32))
                        for i in range(2)]
            B["pd2"] = [estack.enter_context(nc.psum_tensor(f"pd2{i}", [128, TN], F32))
                        for i in range(2)]
            emit_moe(nc, blk, sm, B, io)
            finalize_store_wait(blk, sm)
    return nc


def moe_bufs(nc, stack, pfx=""):
    B = {}
    def sb(name, shape, dt):
        B[name] = stack.enter_context(nc.sbuf_tensor(pfx + name, shape, dt))
    sb("rt_sb", [128, FC, NE], BF16)
    sb("eyef_sb", [128, 128], F32)
    sb("sel_sb", [NE, NE * 128], F32)
    sb("m1", [128, 1], F32)
    sb("m2", [128, 1], F32)
    sb("dd", [128, 1], F32)
    sb("s1", [128, 1], F32)
    sb("s2", [128, 1], F32)
    sb("eq1", [128, NE], F32)
    sb("eq2", [128, NE], F32)
    sb("l2", [128, NE], F32)
    sb("w1", [128, NE], F32)
    sb("wsb", [128, FC, NE], F32)
    sb("w8sb", [NE, TLOC], F32)
    sb("xn_sb2", [128, FC, TLOC], BF16)
    B["gslice"] = [stack.enter_context(nc.sbuf_tensor(pfx + f"gslice{i}", [128, FC, 128], BF16))
                   for i in range(2)]
    B["uslice"] = [stack.enter_context(nc.sbuf_tensor(pfx + f"uslice{i}", [128, FC, 128], BF16))
                   for i in range(2)]
    B["dslice"] = [stack.enter_context(nc.sbuf_tensor(pfx + f"dslice{i}", [128, 8, 128], BF16))
                   for i in range(2)]
    B["h_et"] = [stack.enter_context(nc.sbuf_tensor(pfx + f"h_et{i}", [128, 8, TLOC], BF16))
                 for i in range(2)]
    B["silu2"] = [stack.enter_context(nc.sbuf_tensor(pfx + f"silu2{i}", [128, TN], F32))
                  for i in range(2)]
    sb("wbcs4", [128, TT, TN], F32)
    B["mo_oc"] = [stack.enter_context(nc.sbuf_tensor(pfx + f"mo_oc{i}", [128, TLOC], F32))
                  for i in range(2)]
    return B




def emit_kv_collective(nc, block, sm, cc_in, cc_out, kv_store_stream):
    scc = sm["cc"]

    @block.gpsimd
    def _(gp):
        gp.wait_ge(kv_store_stream.h, kv_store_stream.n)
        scc.i(gp.collective_compute(
            "AllGather", OP.bypass,
            replica_groups=[[0, 1], [2, 3], [4, 5], [6, 7]],
            ins=[cc_in.opt()], outs=[cc_out.opt()]))
    return scc.n


def build_fused():
    nc = bass.Bass("TRN2", target_bir_lowering=False, debug=False)
    EI, EO, IN = "ExternalInput", "ExternalOutput", "Internal"

    def dt_(name, shape, dt, kind):
        return nc.dram_tensor(name, shape, dt, kind=kind).ap()

    io = {
        "x0": dt_("x0", [DM, TLOC], F32, EI),
        "amask": dt_("amask", [N_MSK, 128, TN], BF16, EI),
        "eye": dt_("eye", [128, 128], BF16, EI),
        "eyef": dt_("eyef", [128, 128], F32, EI),
        "sel8": dt_("sel8", [NE, NE * 128], F32, EI),
        "wq0": dt_("wq0", [DM, DM], F32, EI),
        "wk0": dt_("wk0", [DM, 512], F32, EI),
        "wv0": dt_("wv0", [DM, 512], F32, EI),
        "wo_w": dt_("wo_w", [DM, DM], F32, EI),
        "gate_wT": dt_("gate_wT", [DM, I0], I32, EI),
        "up_wT": dt_("up_wT", [DM, I0], I32, EI),
        "down_wT": dt_("down_wT", [I0, DM], I32, EI),
        "sg": dt_("sg", [128, 64], F32, EI),
        "su": dt_("su", [128, 64], F32, EI),
        "sd": dt_("sd", [128, 16], F32, EI),
        "wq1": dt_("wq1", [NH, DM, HD], F32, EI),
        "wk1": dt_("wk1", [NG, DM, HD], F32, EI),
        "wv1": dt_("wv1", [NG, DM, HD], F32, EI),
        "bqT": dt_("bqT", [128, NH], F32, EI),
        "bkT": dt_("bkT", [128, NG], F32, EI),
        "bvT": dt_("bvT", [128, NG], F32, EI),
        "routerT": dt_("routerT", [DM, NE], F32, EI),
        "egu": dt_("egu", [NE, DM, DM], F32, EI),
        "edown": dt_("edown", [NE, IM, DM], F32, EI),
        "out": dt_("out", [DM, TLOC], F32, EO),
        "xn0": dt_("xn0", [DM, TLOC], BF16, IN),
        "sxn0": dt_("sxn0", [1, TLOC], BF16, IN),
        "xn1": dt_("xn1", [DM, TLOC], BF16, IN),
        "sxn1": dt_("sxn1", [1, TLOC], BF16, IN),
        "q0": dt_("q0", [NH, 128, TLOC], BF16, IN),
        "q1": dt_("q1", [NH, 128, TLOC], BF16, IN),
        "cc_in0": dt_("cc_in0", [128, 16384], BF16, IN),
        "cc_in1": dt_("cc_in1", [128, 16384], BF16, IN),
        "cc_out0": dt_("cc_out0", [256, 16384], BF16, IN),
        "cc_out1": dt_("cc_out1", [256, 16384], BF16, IN),
        "attn0": dt_("attn0", [NH, 128, TLOC], BF16, IN),
        "attn1": dt_("attn1", [NH, 128, TLOC], F32, IN),
        "wo_out": dt_("wo_out", [DM, TLOC], F32, IN),
        "h": dt_("h", [I0, TLOC], BF16, IN),
        "x1": dt_("x1", [DM, TLOC], F32, IN),
    }
    with contextlib.ExitStack() as S0:
        sm = make_sems(nc, S0)
        blk = S0.enter_context(nc.Block())

        def barrier():
            cnts = ([(sm[e].h, sm[e].n) for e in ("pe", "act", "dve", "gp")]
                    + [(s.h, s.n) for s in sm._streams.values()])
            cnts = [(hh, n) for hh, n in cnts if n > 0]
            for ename in ("tensor", "scalar", "vector", "gpsimd", "sync"):
                def f(eng, cnts=cnts):
                    for hh, n in cnts:
                        eng.wait_ge(hh, n)
                getattr(blk, ename)(f)

        for layer in (0, 1):
            P = f"l{layer}"
            xsrc = io["x0"] if layer == 0 else io["x1"]
            xn, sxn = io[f"xn{layer}"], io[f"sxn{layer}"]
            barrier()
            with contextlib.ExitStack() as s:
                B = ln_bufs(nc, s, P + "ln")
                emit_ln(nc, blk, sm, B, {"x": xsrc, "xn": xn, "sxn": sxn})
            barrier()
            with contextlib.ExitStack() as s:
                B = qkv_bufs(nc, s, layer, P + "qk")
                ioq = {"xn": xn, "eye": io["eye"], "q_buf": io[f"q{layer}"],
                       "kv_cc": io[f"cc_in{layer}"]}
                if layer == 0:
                    ioq.update(wq=io["wq0"], wk=io["wk0"], wv=io["wv0"])
                else:
                    ioq.update(wq=io["wq1"], wk=io["wk1"], wv=io["wv1"],
                               bqT=io["bqT"], bkT=io["bkT"], bvT=io["bvT"])
                emit_qkv(nc, blk, sm, B, ioq, layer)
                cc_n = emit_kv_collective(nc, blk, sm, io[f"cc_in{layer}"],
                                          io[f"cc_out{layer}"], sm.stream("st_kv"))
            barrier()
            with contextlib.ExitStack() as s:
                B = attn_bufs(nc, s, layer, P + "at")
                ioa = {"q_buf": io[f"q{layer}"], "kv_pair": io[f"cc_out{layer}"],
                       "amask": io["amask"], "attn_out": io[f"attn{layer}"],
                       "cc_dep": (sm["cc"].h, cc_n)}
                emit_attn(nc, blk, sm, B, ioa, layer)
            if layer == 0:
                barrier()
                with contextlib.ExitStack() as s:
                    B = wo_bufs(nc, s, "wo")
                    emit_wo(nc, blk, sm, B, {"attn_out": io["attn0"],
                                             "wo": io["wo_w"], "wo_out": io["wo_out"]})
                barrier()
                with contextlib.ExitStack() as s:
                    B = ffn0_small_bufs(nc, s, "f0")
                    iof = {"xn": xn, "sxn": sxn, "gate_wT": io["gate_wT"],
                           "up_wT": io["up_wT"], "down_wT": io["down_wT"],
                           "sg": io["sg"], "su": io["su"], "sd": io["sd"],
                           "x0": io["x0"], "wo_out": io["wo_out"], "x1": io["x1"],
                           "h": io["h"]}
                    with contextlib.ExitStack() as sa:
                        ffn0_a_bufs(nc, sa, B, "f0a")
                        emit_ffn0_a(nc, blk, sm, B, iof)
                    with contextlib.ExitStack() as sb_:
                        ffn0_b_bufs(nc, sb_, B, "f0b")
                        emit_ffn0_b(nc, blk, sm, B, iof)
            else:
                barrier()
                with contextlib.ExitStack() as s:
                    B = moe_bufs(nc, s, "me")
                    iom = {"xn": xn, "routerT": io["routerT"], "egu": io["egu"],
                           "edown": io["edown"], "sel8": io["sel8"],
                           "eyef": io["eyef"], "x1": io["x1"],
                           "attn1": io["attn1"], "out": io["out"]}
                    with contextlib.ExitStack() as rs:
                        moe_router_scoped(nc, rs, B, "me")
                        emit_router(nc, blk, sm, B, iom)
                        emit_moe_prepass(nc, blk, sm, B, iom)
                    with contextlib.ExitStack() as es:
                        moe_expert_psums(nc, es, B, "me")
                        emit_moe(nc, blk, sm, B, iom)
        finalize_store_wait(blk, sm)

        @blk.gpsimd
        def _(gp):
            gp.wait_ge(sm["cc"].h, sm["cc"].n)
    return nc


def kernel_fused(**inputs):
    g, per = _prep_inputs(inputs)
    nc = _get_nc("fused", build_fused)
    maps = []
    for c in range(8):
        m = {"x0": per[c]["x0"], "amask": per[c]["amask"],
             "eye": g["eye"], "eyef": g["eyef"], "sel8": g["sel8"],
             "wq0": g["l0_wq"], "wk0": g["l0_wk"], "wv0": g["l0_wv"],
             "wo_w": g["l0_wo"], "gate_wT": g["gate_wT"], "up_wT": g["up_wT"],
             "down_wT": g["down_wT"], "sg": g["sg"], "su": g["su"], "sd": g["sd"],
             "wq1": g["l1_wq"], "wk1": g["l1_wk"], "wv1": g["l1_wv"],
             "bqT": g["l1_bq"], "bkT": g["l1_bk"], "bvT": g["l1_bv"],
             "routerT": g["routerT"], "egu": g["egu"], "edown": g["edown"]}
        maps.append(m)
    res = run_bass_kernel_spmd(nc, maps, core_ids=list(range(8)))
    return _unprep_output([res.results[c]["out"] for c in range(8)])


_NC_CACHE = {}


def _get_nc(name, builder):
    if name not in _NC_CACHE:
        _NC_CACHE[name] = builder()
    return _NC_CACHE[name]


def kernel(**inputs):
    return kernel_fused(**inputs)


def kernel_unfused(**inputs):
    g, per = _prep_inputs(inputs)
    C = list(range(8))

    def run(nc, maps):
        return run_bass_kernel_spmd(nc, maps, core_ids=C).results

    out_final = None
    xcur = [per[c]["x0"] for c in C]
    for layer in (0, 1):
        r = run(_get_nc("ln", build_ln_launch), [{"x": xcur[c]} for c in C])
        xn = [r[c]["xn"] for c in C]
        sxn = [r[c]["sxn"] for c in C]
        if layer == 0:
            maps = [{"xn": xn[c], "eye": g["eye"], "wq": g["l0_wq"],
                     "wk": g["l0_wk"], "wv": g["l0_wv"]} for c in C]
            r = run(_get_nc("qkv0", lambda: build_qkv_launch(0)), maps)
        else:
            maps = [{"xn": xn[c], "eye": g["eye"], "wq": g["l1_wq"],
                     "wk": g["l1_wk"], "wv": g["l1_wv"], "bqT": g["l1_bq"],
                     "bkT": g["l1_bk"], "bvT": g["l1_bv"]} for c in C]
            r = run(_get_nc("qkv1", lambda: build_qkv_launch(1)), maps)
        q = [r[c]["q_buf"] for c in C]
        kv = [r[c]["kv_cc"] for c in C]
        kvp = [np.concatenate([kv[2 * (c // 2)], kv[2 * (c // 2) + 1]], axis=0)
               for c in C]
        maps = [{"q_buf": q[c], "kv_pair": kvp[c], "amask": per[c]["amask"]}
                for c in C]
        if layer == 0:
            r = run(_get_nc("att0", lambda: build_attn_launch(0)), maps)
            attn = [r[c]["attn_out"] for c in C]
            r = run(_get_nc("wo", build_wo_launch),
                    [{"attn_out": attn[c], "wo": g["l0_wo"]} for c in C])
            wo_out = [r[c]["wo_out"] for c in C]
            maps = [{"xn": xn[c], "sxn": sxn[c], "gate_wT": g["gate_wT"],
                     "up_wT": g["up_wT"], "down_wT": g["down_wT"], "sg": g["sg"],
                     "su": g["su"], "sd": g["sd"], "x0": xcur[c],
                     "wo_out": wo_out[c]} for c in C]
            r = run(_get_nc("ffn0", build_ffn0_launch), maps)
            xcur = [r[c]["x1"] for c in C]
        else:
            r = run(_get_nc("att1", lambda: build_attn_launch(1)), maps)
            attn = [r[c]["attn_out"] for c in C]
            maps = [{"xn": xn[c], "routerT": g["routerT"], "egu": g["egu"],
                     "edown": g["edown"], "sel8": g["sel8"], "eyef": g["eyef"],
                     "x1": xcur[c], "attn1": attn[c]} for c in C]
            r = run(_get_nc("moe", build_moe_launch), maps)
            out_final = [r[c]["out"] for c in C]
    return _unprep_output(out_final)


# version 05
# speedup vs baseline: 1.2672x; 1.2672x over previous
import contextlib
import numpy as np
import ml_dtypes
import concourse.bass as bass
import concourse.mybir as mybir
from concourse.bass_utils import run_bass_kernel_spmd

F32 = mybir.dt.float32
BF16 = mybir.dt.bfloat16
I32 = mybir.dt.int32
AF = mybir.ActivationFunctionType
OP = mybir.AluOpType
AX = mybir.AxisListType

BF = ml_dtypes.bfloat16

DM = 2048
FC = 16
TLOC = 2048
TT = 4
TN = 512
NH = 16
NG = 4
HD = 128
SEQ = 4096
I0 = 8192
NE = 8
IM = 1024
LHAT = (12, 16, 28, 32)
MSK_RANGE = ((0, 12), (0, 16), (16, 28), (20, 32))
N_MSK = sum(b - a for a, b in MSK_RANGE)
SCALE = 1.0 / np.sqrt(128.0)


class S:

    def __init__(self, h):
        self.h = h
        self.n = 0

    def i(self, instr, k=1):
        instr.then_inc(self.h, k)
        self.n += k
        return self.n


def _core_tokens(core):
    s, half = core // 2, core % 2
    if half == 0:
        return s, [(0, 1024), (3072, 4096)]
    return s, [(1024, 3072)]


GRPS_A = (0, 1, 6, 7)
GRPS_B = (2, 3, 4, 5)


def _host_masks(core):
    half = core % 2
    grps = GRPS_A if half == 0 else GRPS_B
    tiles = []
    for r in range(4):
        g = grps[r]
        q_glob = np.arange(g * 512, (g + 1) * 512)
        a, b = MSK_RANGE[r]
        for kc in range(a, b):
            k_glob = np.arange(kc * 128, (kc + 1) * 128)
            tiles.append((k_glob[:, None] <= q_glob[None, :]).astype(BF))
    return np.stack(tiles)


def _prep_inputs(inputs):
    g = {}
    g["l0_wq"] = np.ascontiguousarray(inputs["L0_attn_Wq"])
    g["l0_wk"] = np.ascontiguousarray(inputs["L0_attn_Wk"])
    g["l0_wv"] = np.ascontiguousarray(inputs["L0_attn_Wv"])
    g["l0_wo"] = np.ascontiguousarray(inputs["L0_attn_Wo"])
    g["gate_wT"] = np.ascontiguousarray(inputs["L0_ffn_gate_w"].T)
    g["up_wT"] = np.ascontiguousarray(inputs["L0_ffn_up_w"].T)
    g["down_wT"] = np.ascontiguousarray(inputs["L0_ffn_down_w"].T)
    g["sg"] = np.ascontiguousarray(inputs["L0_ffn_gate_scale"].reshape(64, 128).T)
    g["su"] = np.ascontiguousarray(inputs["L0_ffn_up_scale"].reshape(64, 128).T)
    g["sd"] = np.ascontiguousarray(inputs["L0_ffn_down_scale"].reshape(16, 128).T)
    g["l1_wq"] = np.ascontiguousarray(inputs["L1_attn_Wq"])
    g["l1_wk"] = np.ascontiguousarray(inputs["L1_attn_Wk"])
    g["l1_wv"] = np.ascontiguousarray(inputs["L1_attn_Wv"])
    g["l1_bq"] = np.ascontiguousarray(inputs["L1_attn_bq"].T)
    g["l1_bk"] = np.ascontiguousarray(inputs["L1_attn_bk"].T)
    g["l1_bv"] = np.ascontiguousarray(inputs["L1_attn_bv"].T)
    g["routerT"] = np.ascontiguousarray(inputs["L1_ffn_router_gate"].T)
    egu = np.asarray(inputs["L1_ffn_expert_gate_up"])
    g["egu"] = np.ascontiguousarray(
        np.concatenate([egu[..., 0::2], egu[..., 1::2]], axis=-1))
    g["edown"] = np.ascontiguousarray(inputs["L1_ffn_expert_down"])
    sel = np.zeros((8, 8 * 128), dtype=np.float32)
    for e in range(8):
        sel[e, e * 128:(e + 1) * 128] = 1.0
    g["sel8"] = sel
    g["eye"] = np.eye(128, dtype=BF)
    g["eyef"] = np.eye(128, dtype=np.float32)
    x0 = np.asarray(inputs["L0_attn_x"])
    per = []
    for c in range(8):
        s, spans = _core_tokens(c)
        tok = np.concatenate([x0[s, a:b] for a, b in spans], axis=0)
        d = {
            "x0": np.ascontiguousarray(tok.T),
            "amask": _host_masks(c),
        }
        per.append(d)
    return g, per


def _unprep_output(outs):
    res = np.empty((4, 4096, 2048), dtype=np.float32)
    for c in range(8):
        s, spans = _core_tokens(c)
        tm = outs[c].T
        off = 0
        for a, b in spans:
            res[s, a:b] = tm[off:off + (b - a)]
            off += b - a
    return res



def emit_ln(nc, block, sm, B, io):
    x, xn, sxn = io["x"], io["xn"], io["sxn"]
    xr = x.rearrange("(fc p) t -> p fc t", p=128)
    xnr = xn.rearrange("(fc p) t -> p fc t", p=128)
    sld = sm.stream("ld_lnx")
    sst = sm.stream("st_ln")
    spe, sact, sdve = sm["pe"], sm["act"], sm["dve"]

    @block.vector
    def _(v):
        v.memset(B["onesf"][:, :], 1.0)
        v.memset(B["onescf"][:, :], 1.0)
        mi = v.memset(B["epsrow"][:, :], 1e-5)
        sdve.i(mi)

    @block.scalar
    def _(sc):
        sc.wait_ge(sdve.h, sdve.n)
        sact.i(sc.copy(out=B["onesb"][:, :], in_=B["onesf"][:, :]))

    for t in range(TT):
        ts = slice(t * TN, (t + 1) * TN)
        xt_free_at = sm.get("_ln_xt_free", 0)

        @block.sync
        def _(sy, t=t, ts=ts, w=xt_free_at):
            if t > 0:
                sy.wait_ge(sdve.h, w)
            sld.i(sy.dma_start(out=B["xt"][:, :, :], in_=xr[:, :, ts]), 16)
        ld_done = sld.n

        @block.scalar
        def _(sc, w=ld_done):
            sc.wait_ge(sld.h, w)
            for fc in range(FC):
                ins = sc.activation(out=B["xq"][:, fc, :], in_=B["xt"][:, fc, :],
                                    func=AF.Square)
                if fc == FC - 1:
                    sact.i(ins)
        sq_done = sact.n

        @block.tensor
        def _(te, w=ld_done, wq=sq_done):
            te.wait_ge(sld.h, w)
            for fc in range(FC):
                ins = te.matmul(out=B["ps1"][:, :], lhsT=B["onesf"][:, :],
                                rhs=B["xt"][:, fc, :], start=(fc == 0), stop=(fc == FC - 1))
            spe.i(ins)
            te.wait_ge(sact.h, wq)
            for fc in range(FC):
                ins = te.matmul(out=B["ps2"][:, :], lhsT=B["onesf"][:, :],
                                rhs=B["xq"][:, fc, :], start=(fc == 0), stop=(fc == FC - 1))
            spe.i(ins)
        stats_done = spe.n

        @block.scalar
        def _(sc, w=stats_done):
            sc.wait_ge(spe.h, w)
            sact.i(sc.activation(out=B["mrow"][:, :], in_=B["ps1"][:, :],
                                 func=AF.Copy, scale=1.0 / DM))
        m_done = sact.n

        @block.scalar
        def _(sc, w=m_done):
            sc.wait_ge(sact.h, w)
            sact.i(sc.activation(out=B["msq"][:, :], in_=B["mrow"][:, :], func=AF.Square))
        msq_done = sact.n

        @block.vector
        def _(v, w=msq_done, wp=stats_done):
            v.wait_ge(sact.h, w)
            v.wait_ge(spe.h, wp)
            sdve.i(v.scalar_tensor_tensor(out=B["varrow"][:, :], in0=B["ps2"][:, :],
                                          scalar=1.0 / DM, in1=B["msq"][:, :],
                                          op0=OP.mult, op1=OP.subtract))
        var_done = sdve.n

        @block.scalar
        def _(sc, w=var_done):
            sc.wait_ge(sdve.h, w)
            sact.i(sc.activation(out=B["sdrow"][:, :], in_=B["varrow"][:, :],
                                 func=AF.Sqrt, bias=B["epsrow"][:, 0:1]))
        sd_done = sact.n

        @block.vector
        def _(v, w=sd_done):
            v.wait_ge(sact.h, w)
            sdve.i(v.reciprocal(out=B["rstd"][:, :], in_=B["sdrow"][:, :]))
        rstd_done = sdve.n

        @block.tensor
        def _(te, w=rstd_done, wm=m_done):
            te.wait_ge(sact.h, wm)
            te.wait_ge(sdve.h, w)
            te.matmul(out=B["pbm"][:, :], lhsT=B["onescf"][:, :], rhs=B["mrow"][:, :],
                      start=True, stop=True)
            spe.i(te.matmul(out=B["pbr"][:, :], lhsT=B["onescf"][:, :],
                            rhs=B["rstd"][:, :], start=True, stop=True))
        bc_done = spe.n

        @block.vector
        def _(v, w=bc_done):
            v.wait_ge(spe.h, w)
            v.tensor_copy(out=B["mbc"][:, :], in_=B["pbm"][:, :])
            sdve.i(v.tensor_copy(out=B["rbc"][:, :], in_=B["pbr"][:, :]))
        bc_sb = sdve.n

        xnt_busy = sm.get("_ln_xnt_busy")

        @block.vector
        def _(v, w=bc_sb, wb=xnt_busy):
            v.wait_ge(sdve.h, w)
            if wb is not None:
                v.wait_ge(sst.h, wb[0])
                v.wait_ge(spe.h, wb[1])
            for fc in range(FC):
                sdve.i(v.tensor_tensor(out=B["tmp"][:, :], in0=B["xt"][:, fc, :],
                                       in1=B["mbc"][:, :], op=OP.subtract))
                v.wait_ge(sdve.h, sdve.n)
                ins = v.tensor_tensor(out=B["xnt"][:, fc, :], in0=B["tmp"][:, :],
                                      in1=B["rbc"][:, :], op=OP.mult)
            sdve.i(ins)
        xn_done = sdve.n
        sm["_ln_xt_free"] = xn_done

        @block.tensor
        def _(te, w=xn_done):
            te.wait_ge(sdve.h, w)
            for fc in range(FC):
                ins = te.matmul(out=B["psx"][:, :], lhsT=B["onesb"][:, :],
                                rhs=B["xnt"][:, fc, :], start=(fc == 0), stop=(fc == FC - 1))
            spe.i(ins)
        sx_done = spe.n

        @block.vector
        def _(v, w=sx_done):
            v.wait_ge(spe.h, w)
            sdve.i(v.tensor_copy(out=B["sxrow"][:, :], in_=B["psx"][:, :]))
        sx_sb = sdve.n

        @block.gpsimd
        def _(gp, ts=ts, w=xn_done, w2=sx_sb):
            gp.wait_ge(sdve.h, w)
            sst.i(gp.dma_start(out=xnr[:, :, ts], in_=B["xnt"][:, :, :]), 16)
            gp.wait_ge(sdve.h, w2)
            sst.i(gp.dma_start(out=sxn[0:1, ts], in_=B["sxrow"][:, :]), 16)
        sm["_ln_xnt_busy"] = (sst.n, sx_done)


def ln_bufs(nc, stack, pfx=""):
    B = {}
    def sb(name, shape, dt):
        B[name] = stack.enter_context(nc.sbuf_tensor(pfx + name, shape, dt))
    def ps(name, shape, dt):
        B[name] = stack.enter_context(nc.psum_tensor(pfx + name, shape, dt))
    sb("xt", [128, FC, TN], F32)
    sb("xq", [128, FC, TN], F32)
    sb("xnt", [128, FC, TN], BF16)
    sb("tmp", [128, TN], F32)
    sb("onesf", [128, 1], F32)
    sb("onescf", [1, 128], F32)
    sb("onesb", [128, 1], BF16)
    sb("mrow", [1, TN], F32)
    sb("epsrow", [1, 1], F32)
    sb("msq", [1, TN], F32)
    sb("varrow", [1, TN], F32)
    sb("sdrow", [1, TN], F32)
    sb("rstd", [1, TN], F32)
    sb("mbc", [128, TN], F32)
    sb("rbc", [128, TN], F32)
    sb("sxrow", [1, TN], BF16)
    ps("ps1", [1, TN], F32)
    ps("ps2", [1, TN], F32)
    ps("psx", [1, TN], F32)
    ps("pbm", [128, TN], F32)
    ps("pbr", [128, TN], F32)
    return B


class Sems(dict):

    def __init__(self, nc, stack):
        super().__init__()
        self._nc = nc
        self._stack = stack
        for name in ["pe", "act", "dve", "gp", "cc"]:
            self[name] = S(stack.enter_context(nc.semaphore(name)))
        self._streams = {}

    def stream(self, name):
        if name not in self._streams:
            self._streams[name] = S(self._stack.enter_context(self._nc.semaphore(name)))
        return self._streams[name]

    def all_store_streams(self):
        return [s for n, s in self._streams.items() if n.startswith("st_")]


def make_sems(nc, stack):
    return Sems(nc, stack)


def finalize_store_wait(block, sm):
    @block.gpsimd
    def _(gp):
        for s in sm.all_store_streams():
            gp.wait_ge(s.h, s.n)



def build_ln_launch():
    nc = bass.Bass("TRN2", target_bir_lowering=False, debug=False)
    io = {
        "x": nc.dram_tensor("x", [DM, TLOC], F32, kind="ExternalInput").ap(),
        "xn": nc.dram_tensor("xn", [DM, TLOC], BF16, kind="ExternalOutput").ap(),
        "sxn": nc.dram_tensor("sxn", [1, TLOC], BF16, kind="ExternalOutput").ap(),
    }
    with contextlib.ExitStack() as stack:
        B = ln_bufs(nc, stack)
        sm = make_sems(nc, stack)
        blk = stack.enter_context(nc.Block())
        emit_ln(nc, blk, sm, B, io)
        finalize_store_wait(blk, sm)
    return nc




def emit_qkv(nc, block, sm, B, io, layer):
    sld = sm.stream("ld_qkvx")
    sldw = [sm.stream("ld_w0"), sm.stream("ld_w1")]
    sstq = [sm.stream("st_q0"), sm.stream("st_q1")]
    sstkv = sm.stream("st_kv")
    spe, sact, sdve = sm["pe"], sm["act"], sm["dve"]
    xnr = io["xn"].rearrange("(fc p) t -> p fc t", p=128)

    @block.sync
    def _(sy):
        sld.i(sy.dma_start(out=B["xn_sb"][:, :, :], in_=xnr[:, :, :]), 16)
        sld.i(sy.dma_start(out=B["eye_sb"][:, :], in_=io["eye"]), 16)
        if layer == 1:
            sld.i(sy.dma_start(out=B["bq_sb"][:, :], in_=io["bqT"]), 16)
            sld.i(sy.dma_start(out=B["bk_sb"][:, :], in_=io["bkT"]), 16)
            sld.i(sy.dma_start(out=B["bv_sb"][:, :], in_=io["bvT"]), 16)
    xn_ready = sld.n

    chunks = [("q", h) for h in range(NH)] + [("k", g) for g in range(NG)] \
        + [("v", g) for g in range(NG)]
    w_done_pe = {}

    for ci, (kind, idx) in enumerate(chunks):
        slot = ci % 2
        @block.gpsimd
        def _(gp, kind=kind, idx=idx, slot=slot, w=w_done_pe.get(slot)):
            if w is not None:
                gp.wait_ge(spe.h, w)
            if layer == 0:
                src = {"q": io["wq"], "k": io["wk"], "v": io["wv"]}[kind]
                ap = src.rearrange("(fc p) d -> p fc d", p=128)[:, :, idx * 128:(idx + 1) * 128]
            else:
                src = {"q": io["wq"], "k": io["wk"], "v": io["wv"]}[kind]
                ap = src[idx].rearrange("(fc p) d -> p fc d", p=128)[:, :, :]
            sldw[slot].i(gp.dma_start(out=B["wchunk"][slot][:, :, :], in_=ap), 16)
        w_ready = sldw[slot].n

        for t in range(TT):
            ts = slice(t * TN, (t + 1) * TN)
            pq = B["pq"][t % 2]
            pq_free = sm.get(("_qkv_pq", t % 2))

            @block.tensor
            def _(te, w=w_ready, pq=pq, ts=ts, slot=slot, pf=pq_free, xw=xn_ready):
                te.wait_ge(sld.h, xw)
                te.wait_ge(sldw[slot].h, w)
                if pf is not None:
                    te.wait_ge(sdve.h, pf)
                for fc in range(FC):
                    ins = te.matmul(out=pq[:, :], lhsT=B["wchunk"][slot][:, fc, :],
                                    rhs=B["xn_sb"][:, fc, ts],
                                    start=(fc == 0), stop=(fc == FC - 1))
                spe.i(ins)
            mm_done = spe.n

            if kind == "q":
                dst = B["qsb"][idx % 2]
                dsl = slice(t * TN, (t + 1) * TN)
            elif kind == "k":
                dst = B["kvsb"]
                dsl = slice(idx * 2048 + t * TN, idx * 2048 + (t + 1) * TN)
            else:
                dst = B["vtmp"]
                dsl = slice(0, TN)
            qst_wait = sm.get(("_qkv_qst", idx % 2)) if kind == "q" else None

            @block.vector
            def _(v, w=mm_done, pq=pq, kind=kind, idx=idx, dst=dst, dsl=dsl, qw=qst_wait):
                v.wait_ge(spe.h, w)
                if qw is not None and dsl.start == 0:
                    v.wait_ge(sstq[idx % 2].h, qw)
                if layer == 0:
                    ins = v.tensor_copy(out=dst[:, dsl], in_=pq[:, :])
                else:
                    bias = {"q": B["bq_sb"], "k": B["bk_sb"], "v": B["bv_sb"]}[kind]
                    ins = v.tensor_scalar(out=dst[:, dsl], in0=pq[:, :],
                                          scalar1=bias[:, idx:idx + 1], scalar2=None,
                                          op0=OP.add)
                sdve.i(ins)
            sm[("_qkv_pq", t % 2)] = sdve.n
            ev_done = sdve.n

            if kind == "v":
                for j in range(4):
                    kc = t * 4 + j

                    @block.tensor
                    def _(te, w=ev_done, j=j, kc=kc, idx=idx):
                        te.wait_ge(sdve.h, w)
                        if ("_qkv_pt",) in sm:
                            te.wait_ge(sdve.h, sm[("_qkv_pt",)])
                        spe.i(te.transpose(out=B["pt"][:, :],
                                           in_=B["vtmp"][:, j * 128:(j + 1) * 128],
                                           identity=B["eye_sb"][:, :]))
                    tr_done = spe.n

                    @block.vector
                    def _(v, w=tr_done, kc=kc, idx=idx):
                        v.wait_ge(spe.h, w)
                        col = 8192 + idx * 2048 + kc * 128
                        sdve.i(v.tensor_copy(out=B["kvsb"][:, col:col + 128],
                                             in_=B["pt"][:, :]))
                    sm[("_qkv_pt",)] = sdve.n

        if kind == "q":
            q_ev = sdve.n

            @block.gpsimd
            def _(gp, idx=idx, w=q_ev):
                gp.wait_ge(sdve.h, w)
                sstq[idx % 2].i(gp.dma_start(out=io["q_buf"][idx], in_=B["qsb"][idx % 2][:, :]), 16)
            sm[("_qkv_qst", idx % 2)] = sstq[idx % 2].n
        w_done_pe[slot] = spe.n

    @block.gpsimd
    def _(gp):
        gp.wait_ge(sdve.h, sdve.n)
        sstkv.i(gp.dma_start(out=io["kv_cc"], in_=B["kvsb"][:, :]), 16)


def qkv_bufs(nc, stack, layer, pfx=""):
    B = {}
    def sb(name, shape, dt):
        B[name] = stack.enter_context(nc.sbuf_tensor(pfx + name, shape, dt))
    def ps(name, shape, dt):
        B[name] = stack.enter_context(nc.psum_tensor(pfx + name, shape, dt))
    sb("xn_sb", [128, FC, TLOC], BF16)
    B["wchunk"] = [stack.enter_context(nc.sbuf_tensor(pfx + f"wchunk{i}", [128, FC, 128], BF16))
                   for i in range(2)]
    B["qsb"] = [stack.enter_context(nc.sbuf_tensor(pfx + f"qsb{i}", [128, TLOC], BF16))
                for i in range(2)]
    sb("kvsb", [128, 16384], BF16)
    sb("vtmp", [128, TN], BF16)
    sb("eye_sb", [128, 128], BF16)
    if layer == 1:
        sb("bq_sb", [128, NH], F32)
        sb("bk_sb", [128, NG], F32)
        sb("bv_sb", [128, NG], F32)
    B["pq"] = [stack.enter_context(nc.psum_tensor(pfx + f"pq{i}", [128, TN], F32))
               for i in range(2)]
    ps("pt", [128, 128], BF16)
    return B


def build_qkv_launch(layer):
    nc = bass.Bass("TRN2", target_bir_lowering=False, debug=False)
    io = {
        "xn": nc.dram_tensor("xn", [DM, TLOC], BF16, kind="ExternalInput").ap(),
        "eye": nc.dram_tensor("eye", [128, 128], BF16, kind="ExternalInput").ap(),
        "q_buf": nc.dram_tensor("q_buf", [NH, 128, TLOC], BF16, kind="ExternalOutput").ap(),
        "kv_cc": nc.dram_tensor("kv_cc", [128, 16384], BF16, kind="ExternalOutput").ap(),
    }
    if layer == 0:
        io["wq"] = nc.dram_tensor("wq", [DM, DM], F32, kind="ExternalInput").ap()
        io["wk"] = nc.dram_tensor("wk", [DM, 512], F32, kind="ExternalInput").ap()
        io["wv"] = nc.dram_tensor("wv", [DM, 512], F32, kind="ExternalInput").ap()
    else:
        io["wq"] = nc.dram_tensor("wq", [NH, DM, HD], F32, kind="ExternalInput").ap()
        io["wk"] = nc.dram_tensor("wk", [NG, DM, HD], F32, kind="ExternalInput").ap()
        io["wv"] = nc.dram_tensor("wv", [NG, DM, HD], F32, kind="ExternalInput").ap()
        io["bqT"] = nc.dram_tensor("bqT", [128, NH], F32, kind="ExternalInput").ap()
        io["bkT"] = nc.dram_tensor("bkT", [128, NG], F32, kind="ExternalInput").ap()
        io["bvT"] = nc.dram_tensor("bvT", [128, NG], F32, kind="ExternalInput").ap()
    with contextlib.ExitStack() as stack:
        B = qkv_bufs(nc, stack, layer)
        sm = make_sems(nc, stack)
        blk = stack.enter_context(nc.Block())
        emit_qkv(nc, blk, sm, B, io, layer)
        finalize_store_wait(blk, sm)
    return nc




def _mask_index():
    d = {}
    mi = 0
    for r in range(4):
        a, b = MSK_RANGE[r]
        for kc in range(a, b):
            d[(r, kc)] = mi
            mi += 1
    return d

MASK_IDX = _mask_index()


def emit_attn(nc, block, sm, B, io, layer):
    sldm = sm.stream("ld_msk")
    sldkv = sm.stream("ld_kv")
    sldq = [sm.stream("ld_aq0"), sm.stream("ld_aq1")]
    ssto = [sm.stream("st_ao0"), sm.stream("st_ao1")]
    spe, sact, sdve, sgp = sm["pe"], sm["act"], sm["dve"], sm["gp"]
    kvp = io["kv_pair"]

    @block.sync
    def _(sy):
        mr = io["amask"].rearrange("m p c -> p m c")
        sldm.i(sy.dma_start(out=B["msk"][:, :, :], in_=mr), 16)
    msk_ready = sldm.n

    @block.vector
    def _(v):
        v.memset(B["onesb2"][:, :], 1.0)
        sdve.i(v.memset(B["onescf2"][:, :], 1.0))
    ones_dve = sdve.n

    def emit_rsav(hi, kc, sslot, first, last, why):
        csl = slice(hi * TN, (hi + 1) * TN)

        @block.tensor
        def _(te, hi=hi, kc=kc, sslot=sslot, first=first, last=last, csl=csl,
              w=why):
            if w is not None:
                te.wait_ge(w[0], w[1])
            te.matmul(out=B["prs"][hi][0:1, :], lhsT=B["onesb2"][:, :],
                      rhs=B["p_sb"][sslot][:, csl], start=first, stop=last)
            ins = te.matmul(out=B["pav"][hi][:, :],
                            lhsT=B["vsb"][:, kc * 128:(kc + 1) * 128],
                            rhs=B["p_sb"][sslot][:, csl], start=first, stop=last)
            spe.i(ins)

    for g in range(NG):
        kv_free = sm.get("_att_kv_free")

        @block.sync
        def _(sy, g=g, w=kv_free, ccd=io.get("cc_dep")):
            if ccd is not None and g == 0:
                sy.wait_ge(ccd[0], ccd[1])
            if w is not None:
                sy.wait_ge(spe.h, w)
            c0 = g * 2048
            sldkv.i(sy.dma_start(out=B["ksb"][:, 0:1024], in_=kvp[0:128, c0:c0 + 1024]), 16)
            sldkv.i(sy.dma_start(out=B["ksb"][:, 1024:3072], in_=kvp[128:256, c0:c0 + 2048]), 16)
            sldkv.i(sy.dma_start(out=B["ksb"][:, 3072:4096], in_=kvp[0:128, c0 + 1024:c0 + 2048]), 16)
            v0 = 8192 + g * 2048
            sldkv.i(sy.dma_start(out=B["vsb"][:, 0:1024], in_=kvp[0:128, v0:v0 + 1024]), 16)
            sldkv.i(sy.dma_start(out=B["vsb"][:, 1024:3072], in_=kvp[128:256, v0:v0 + 2048]), 16)
            sldkv.i(sy.dma_start(out=B["vsb"][:, 3072:4096], in_=kvp[0:128, v0 + 1024:v0 + 2048]), 16)
        kv_ready = sldkv.n

        for hp in range(2):
            heads = (g * 4 + 2 * hp, g * 4 + 2 * hp + 1)
            q_ready = [None, None]
            for hi in (0, 1):
                q_free = sm.get(("_att_q_free", hi))

                @block.sync
                def _(sy, h=heads[hi], hi=hi, w=q_free):
                    if w is not None:
                        sy.wait_ge(spe.h, w)
                    sldq[hi].i(sy.dma_start(out=B["qsb2"][hi][:, :], in_=io["q_buf"][h]), 16)
                q_ready[hi] = sldq[hi].n

            for r in range(4):
                L = LHAT[r]
                qsl = slice(r * TN, (r + 1) * TN)
                avrs_free = [sm.get(("_att_avrs", hi)) for hi in (0, 1)]
                pend = None

                for kc in range(L):
                    sslot = kc % 2
                    psc_free = sm.get(("_att_psc", sslot))

                    @block.tensor
                    def _(te, kc=kc, sslot=sslot, qsl=qsl,
                          wk=kv_ready, wq=q_ready, wp=psc_free,
                          wav=(avrs_free if kc == 0 else None)):
                        te.wait_ge(sldq[0].h, wq[0])
                        te.wait_ge(sldq[1].h, wq[1])
                        if kc == 0:
                            te.wait_ge(sldkv.h, wk)
                        if wp is not None:
                            te.wait_ge(sact.h, wp)
                        if wav is not None:
                            for wv_ in wav:
                                if wv_ is not None:
                                    te.wait_ge(sdve.h, wv_)
                        for hi in (0, 1):
                            ins = te.matmul(out=B["psc"][sslot][:, hi * TN:(hi + 1) * TN],
                                            lhsT=B["ksb"][:, kc * 128:(kc + 1) * 128],
                                            rhs=B["qsb2"][hi][:, qsl],
                                            start=True, stop=True)
                        spe.i(ins)
                    sc_done = spe.n

                    pm_free = sm.get(("_att_pm", sslot))

                    @block.scalar
                    def _(sc, sslot=sslot, w=sc_done, wf=pm_free):
                        sc.wait_ge(spe.h, w)
                        if wf is not None:
                            sc.wait_ge(spe.h, wf)
                        sact.i(sc.activation(out=B["p_sb"][sslot][:, :],
                                             in_=B["psc"][sslot][:, :],
                                             func=AF.Exp, scale=SCALE))
                    sm[("_att_psc", sslot)] = sact.n
                    exp_done = sact.n

                    ready = (sact.h, exp_done)
                    if (r, kc) in MASK_IDX:
                        mi = MASK_IDX[(r, kc)]

                        @block.vector
                        def _(v, sslot=sslot, mi=mi, w=exp_done):
                            v.wait_ge(sact.h, w)
                            v.wait_ge(sldm.h, msk_ready)
                            v.wait_ge(sdve.h, ones_dve)
                            for hi in (0, 1):
                                ins = v.tensor_tensor(
                                    out=B["p_sb"][sslot][:, hi * TN:(hi + 1) * TN],
                                    in0=B["p_sb"][sslot][:, hi * TN:(hi + 1) * TN],
                                    in1=B["msk"][:, mi, :], op=OP.mult)
                            sdve.i(ins)
                        ready = (sdve.h, sdve.n)

                    if pend is not None:
                        pkc, pslot, pready = pend
                        for hi in (0, 1):
                            emit_rsav(hi, pkc, pslot, pkc == 0, False,
                                      pready if hi == 0 else None)
                        sm[("_att_pm", pslot)] = spe.n
                    pend = (kc, sslot, ready)

                pkc, pslot, pready = pend
                for hi in (0, 1):
                    emit_rsav(hi, pkc, pslot, pkc == 0, True,
                              pready if hi == 0 else None)
                sm[("_att_pm", pslot)] = spe.n
                av_done = spe.n

                @block.vector
                def _(v, w=av_done):
                    v.wait_ge(spe.h, w)
                    v.reciprocal(out=B["rr"][0][:, :], in_=B["prs"][0][0:1, :])
                    sdve.i(v.reciprocal(out=B["rr"][1][:, :], in_=B["prs"][1][0:1, :]))
                rec_done = sdve.n

                @block.tensor
                def _(te, w=rec_done):
                    te.wait_ge(sdve.h, w)
                    te.matmul(out=B["prs"][0][:, :], lhsT=B["onescf2"][:, :],
                              rhs=B["rr"][0][:, :], start=True, stop=True)
                    spe.i(te.matmul(out=B["prs"][1][:, :], lhsT=B["onescf2"][:, :],
                                    rhs=B["rr"][1][:, :], start=True, stop=True))
                bc_done = spe.n

                for hi in (0, 1):
                    oslot = hi
                    ost_free = sm.get(("_att_ost", oslot))

                    @block.vector
                    def _(v, hi=hi, oslot=oslot, w=bc_done, wo=ost_free):
                        v.wait_ge(spe.h, w)
                        if wo is not None:
                            v.wait_ge(ssto[oslot].h, wo)
                        sdve.i(v.tensor_copy(out=B["rn_sb"][hi][:, :], in_=B["prs"][hi][:, :]))
                        v.wait_ge(sdve.h, sdve.n)
                        sdve.i(v.tensor_tensor(out=B["osb"][oslot][:, :],
                                               in0=B["pav"][hi][:, :],
                                               in1=B["rn_sb"][hi][:, :], op=OP.mult))
                    sm[("_att_avrs", hi)] = sdve.n
                    ev_done = sdve.n

                    @block.gpsimd
                    def _(gp, h=heads[hi], r=r, oslot=oslot, w=ev_done):
                        gp.wait_ge(sdve.h, w)
                        ssto[oslot].i(gp.dma_start(out=io["attn_out"][h][:, r * TN:(r + 1) * TN],
                                                   in_=B["osb"][oslot][:, :]), 16)
                    sm[("_att_ost", oslot)] = ssto[oslot].n
            for hi in (0, 1):
                sm[("_att_q_free", hi)] = spe.n
        sm["_att_kv_free"] = spe.n

def attn_bufs(nc, stack, layer, pfx=""):
    B = {}
    def sb(name, shape, dt):
        B[name] = stack.enter_context(nc.sbuf_tensor(pfx + name, shape, dt))
    sb("ksb", [128, SEQ], BF16)
    sb("vsb", [128, SEQ], BF16)
    B["qsb2"] = [stack.enter_context(nc.sbuf_tensor(pfx + f"qsb2{i}", [128, TLOC], BF16))
                 for i in range(2)]
    sb("msk", [128, N_MSK, TN], BF16)
    B["p_sb"] = [stack.enter_context(nc.sbuf_tensor(pfx + f"p_sb{i}", [128, 2 * TN], BF16))
                 for i in range(2)]
    sb("onesb2", [128, 1], BF16)
    sb("onescf2", [1, 128], F32)
    B["rr"] = [stack.enter_context(nc.sbuf_tensor(pfx + f"rr{i}", [1, TN], F32))
               for i in range(2)]
    B["rn_sb"] = [stack.enter_context(nc.sbuf_tensor(pfx + f"rn_sb{i}", [128, TN], F32))
                  for i in range(2)]
    odt = F32 if layer == 1 else BF16
    B["osb"] = [stack.enter_context(nc.sbuf_tensor(pfx + f"osb{i}", [128, TN], odt))
                for i in range(2)]
    B["psc"] = [stack.enter_context(nc.psum_tensor(pfx + f"psc{i}", [128, 2 * TN], F32))
                for i in range(2)]
    B["pav"] = [stack.enter_context(nc.psum_tensor(pfx + f"pav{i}", [128, TN], F32))
                for i in range(2)]
    B["prs"] = [stack.enter_context(nc.psum_tensor(pfx + f"prs{i}", [128, TN], F32))
                for i in range(2)]
    return B


def build_attn_launch(layer):
    nc = bass.Bass("TRN2", target_bir_lowering=False, debug=False)
    odt = F32 if layer == 1 else BF16
    io = {
        "q_buf": nc.dram_tensor("q_buf", [NH, 128, TLOC], BF16, kind="ExternalInput").ap(),
        "kv_pair": nc.dram_tensor("kv_pair", [256, 16384], BF16, kind="ExternalInput").ap(),
        "amask": nc.dram_tensor("amask", [N_MSK, 128, TN], BF16, kind="ExternalInput").ap(),
        "attn_out": nc.dram_tensor("attn_out", [NH, 128, TLOC], odt, kind="ExternalOutput").ap(),
    }
    with contextlib.ExitStack() as stack:
        B = attn_bufs(nc, stack, layer)
        sm = make_sems(nc, stack)
        blk = stack.enter_context(nc.Block())
        emit_attn(nc, blk, sm, B, io, layer)
        finalize_store_wait(blk, sm)
    return nc




def emit_wo(nc, block, sm, B, io):
    slda = sm.stream("ld_woa")
    sldw = [sm.stream("ld_wow0"), sm.stream("ld_wow1")]
    ssto = [sm.stream("st_wo0"), sm.stream("st_wo1")]
    spe, sdve = sm["pe"], sm["dve"]

    @block.sync
    def _(sy):
        ar = io["attn_out"].rearrange("h p t -> p h t")
        slda.i(sy.dma_start(out=B["asb"][:, :, :], in_=ar), 16)
    a_ready = slda.n

    for oc in range(FC):
        slot = oc % 2
        w_free = sm.get(("_wo_wfree", slot))

        @block.gpsimd
        def _(gp, oc=oc, slot=slot, w=w_free):
            if w is not None:
                gp.wait_ge(spe.h, w)
            ap = io["wo"].rearrange("(ac p) o -> p ac o", p=128)[:, :, oc * 128:(oc + 1) * 128]
            sldw[slot].i(gp.dma_start(out=B["wosb"][slot][:, :, :], in_=ap), 16)
        w_ready = sldw[slot].n

        for t in range(TT):
            ts = slice(t * TN, (t + 1) * TN)
            pslot = t % 2
            p_free = sm.get(("_wo_pfree", pslot))

            @block.tensor
            def _(te, slot=slot, pslot=pslot, ts=ts, w=w_ready, pf=p_free, wa=a_ready):
                te.wait_ge(slda.h, wa)
                te.wait_ge(sldw[slot].h, w)
                if pf is not None:
                    te.wait_ge(sdve.h, pf)
                for ac in range(FC):
                    ins = te.matmul(out=B["pw"][pslot][:, :],
                                    lhsT=B["wosb"][slot][:, ac, :],
                                    rhs=B["asb"][:, ac, ts],
                                    start=(ac == 0), stop=(ac == FC - 1))
                spe.i(ins)
            mm_done = spe.n

            oslot = oc % 2
            st_free = sm.get(("_wo_sfree", oslot)) if t == 0 else None

            @block.vector
            def _(v, pslot=pslot, oslot=oslot, ts=ts, w=mm_done, sf=st_free):
                v.wait_ge(spe.h, w)
                if sf is not None:
                    v.wait_ge(ssto[oslot].h, sf)
                sdve.i(v.tensor_copy(out=B["wout"][oslot][:, ts], in_=B["pw"][pslot][:, :]))
            sm[("_wo_pfree", pslot)] = sdve.n
        ev_done = sdve.n

        @block.gpsimd
        def _(gp, oc=oc, w=ev_done):
            gp.wait_ge(sdve.h, w)
            ssto[oc % 2].i(gp.dma_start(out=io["wo_out"][oc * 128:(oc + 1) * 128, :],
                                        in_=B["wout"][oc % 2][:, :]), 16)
        sm[("_wo_sfree", oc % 2)] = ssto[oc % 2].n
        sm[("_wo_wfree", oc % 2)] = spe.n


def wo_bufs(nc, stack, pfx=""):
    B = {}
    B["asb"] = stack.enter_context(nc.sbuf_tensor(pfx + "asb", [128, FC, TLOC], BF16))
    B["wosb"] = [stack.enter_context(nc.sbuf_tensor(pfx + f"wosb{i}", [128, FC, 128], BF16))
                 for i in range(2)]
    B["wout"] = [stack.enter_context(nc.sbuf_tensor(pfx + f"wout{i}", [128, TLOC], F32))
                 for i in range(2)]
    B["pw"] = [stack.enter_context(nc.psum_tensor(pfx + f"pw{i}", [128, TN], F32))
               for i in range(2)]
    return B


def build_wo_launch():
    nc = bass.Bass("TRN2", target_bir_lowering=False, debug=False)
    io = {
        "attn_out": nc.dram_tensor("attn_out", [NH, 128, TLOC], BF16, kind="ExternalInput").ap(),
        "wo": nc.dram_tensor("wo", [DM, DM], F32, kind="ExternalInput").ap(),
        "wo_out": nc.dram_tensor("wo_out", [DM, TLOC], F32, kind="ExternalOutput").ap(),
    }
    with contextlib.ExitStack() as stack:
        B = wo_bufs(nc, stack)
        sm = make_sems(nc, stack)
        blk = stack.enter_context(nc.Block())
        emit_wo(nc, blk, sm, B, io)
        finalize_store_wait(blk, sm)
    return nc


def emit_ffn0_a(nc, block, sm, B, io):
    sldx = sm.stream("ld_f0x")
    sldw = [sm.stream("ld_f0w0"), sm.stream("ld_f0w1")]
    ssth = [sm.stream("st_f0h0"), sm.stream("st_f0h1")]
    spe, sact, sdve = sm["pe"], sm["act"], sm["dve"]
    xnr = io["xn"].rearrange("(fc p) t -> p fc t", p=128)
    hr = io["h"].rearrange("(ic p) t -> p ic t", p=128)

    @block.sync
    def _(sy):
        sldx.i(sy.dma_start(out=B["xn_sb"][:, :, :], in_=xnr), 16)
        sldx.i(sy.dma_start(out=B["sxn_sb"][:, :], in_=io["sxn"]), 16)
        sldx.i(sy.dma_start(out=B["sg_sb"][:, :], in_=io["sg"]), 16)
        sldx.i(sy.dma_start(out=B["su_sb"][:, :], in_=io["su"]), 16)
    x_ready = sldx.n

    @block.vector
    def _(v):
        v.memset(B["neg128"][:, :], -128.0)
        sdve.i(v.memset(B["onesb3"][:, :], 1.0))
    const_dve = sdve.n

    for ib in range(16):
        slot = ib % 2
        w_free = sm.get(("_f0a_wfree", slot))

        @block.gpsimd
        def _(gp, ib=ib, slot=slot, w=w_free):
            if w is not None:
                gp.wait_ge(spe.h, w)
            isl = slice(ib * 512, (ib + 1) * 512)
            gw = io["gate_wT"].rearrange("(fc p) i -> p fc i", p=128)[:, :, isl]
            uw = io["up_wT"].rearrange("(fc p) i -> p fc i", p=128)[:, :, isl]
            sldw[slot].i(gp.dma_start(out=B["gwsb"][slot][:, :, :], in_=gw), 16)
            sldw[slot].i(gp.dma_start(out=B["uwsb"][slot][:, :, :], in_=uw), 16)
        w_ready = sldw[slot].n

        for ic4 in range(4):
            i = ib * 4 + ic4
            hslot = i % 2
            h_free = sm.get(("_f0a_hfree", hslot))
            for t in range(TT):
                ts = slice(t * TN, (t + 1) * TN)
                pp = t % 2
                pg, pu = B["pg"][pp], B["pu"][pp]
                p_free = sm.get(("_f0a_pfree", pp))

                @block.tensor
                def _(te, ib=ib, ic4=ic4, t=t, ts=ts, slot=slot, pg=pg, pu=pu,
                      w=w_ready, xw=x_ready, pf=p_free, cd=const_dve):
                    te.wait_ge(sldx.h, xw)
                    te.wait_ge(sldw[slot].h, w)
                    te.wait_ge(sdve.h, cd)
                    if pf is not None:
                        te.wait_ge(sdve.h, pf)
                    csl = slice(ic4 * 128, (ic4 + 1) * 128)
                    for fc in range(FC):
                        te.matmul(out=pg[:, :], lhsT=B["gwsb"][slot][:, fc, csl],
                                  rhs=B["xn_sb"][:, fc, ts], start=(fc == 0), stop=False)
                    te.matmul(out=pg[:, :], lhsT=B["neg128"][:, :],
                              rhs=B["sxn_sb"][0:1, ts], start=False, stop=True)
                    for fc in range(FC):
                        te.matmul(out=pu[:, :], lhsT=B["uwsb"][slot][:, fc, csl],
                                  rhs=B["xn_sb"][:, fc, ts], start=(fc == 0), stop=False)
                    ins = te.matmul(out=pu[:, :], lhsT=B["neg128"][:, :],
                                    rhs=B["sxn_sb"][0:1, ts], start=False, stop=True)
                    spe.i(ins)
                mm_done = spe.n

                aslot = t % 2

                @block.scalar
                def _(sc, i=i, pg=pg, aslot=aslot, w=mm_done):
                    sc.wait_ge(spe.h, w)
                    af = sm.get(("_f0a_afree", aslot))
                    if af is not None:
                        sc.wait_ge(sdve.h, af)
                    sact.i(sc.activation(out=B["silu_sb"][aslot][:, :], in_=pg[:, :],
                                         func=AF.Silu, scale=B["sg_sb"][:, i:i + 1]))
                silu_done = sact.n

                @block.vector
                def _(v, i=i, t=t, ts=ts, pu=pu, aslot=aslot, hslot=hslot,
                      w=silu_done, hf=(h_free if t == 0 else None)):
                    v.wait_ge(sact.h, w)
                    if hf is not None:
                        v.wait_ge(ssth[hslot].h, hf[0])
                        v.wait_ge(spe.h, hf[1])
                    sdve.i(v.scalar_tensor_tensor(out=B["hsb"][hslot][:, ts], in0=pu[:, :],
                                                  scalar=B["su_sb"][:, i:i + 1],
                                                  in1=B["silu_sb"][aslot][:, :],
                                                  op0=OP.mult, op1=OP.mult))
                sm[("_f0a_afree", aslot)] = sdve.n
                sm[("_f0a_pfree", pp)] = sdve.n
                h_done = sdve.n

                @block.tensor
                def _(te, i=i, t=t, ts=ts, hslot=hslot, w=h_done):
                    te.wait_ge(sdve.h, w)
                    ins = te.matmul(out=B["phrs"][t][:, :], lhsT=B["onesb3"][:, :],
                                    rhs=B["hsb"][hslot][:, ts],
                                    start=(i == 0), stop=(i == 63))
                    spe.i(ins)
            hrs_done = spe.n

            @block.gpsimd
            def _(gp, i=i, hslot=hslot, w=hrs_done):
                gp.wait_ge(spe.h, w)
                ssth[hslot].i(gp.dma_start(out=hr[:, i, :], in_=B["hsb"][hslot][:, :]), 16)
            sm[("_f0a_hfree", hslot)] = (ssth[hslot].n, spe.n)
        sm[("_f0a_wfree", slot)] = spe.n

    @block.vector
    def _(v):
        v.wait_ge(spe.h, spe.n)
        for t in range(TT):
            ins = v.tensor_copy(out=B["hrs_sb"][0:1, t, :], in_=B["phrs"][t][0:1, :])
        sdve.i(ins)
    sm["_f0_hrs_ready"] = sdve.n


def emit_ffn0_b(nc, block, sm, B, io):
    sldw = [sm.stream("ld_f0dw0"), sm.stream("ld_f0dw1")]
    sldh = [sm.stream("ld_f0h0"), sm.stream("ld_f0h1")]
    sldx = [sm.stream("ld_f0r0"), sm.stream("ld_f0r1")]
    sstx = [sm.stream("st_f0x0"), sm.stream("st_f0x1")]
    sldc = sm.stream("ld_f0c")
    spe, sact, sdve = sm["pe"], sm["act"], sm["dve"]
    hr = io["h"].rearrange("(ic p) t -> p ic t", p=128)
    dwr = io["down_wT"].rearrange("(ic p) o -> p ic o", p=128)
    x0r = io["x0"].rearrange("(oc p) t -> p oc t", p=128)
    wor = io["wo_out"].rearrange("(oc p) t -> p oc t", p=128)
    x1r = io["x1"].rearrange("(oc p) t -> p oc t", p=128)
    hrs_ready = sm["_f0_hrs_ready"]

    @block.sync
    def _(sy):
        sldc.i(sy.dma_start(out=B["sd_sb"][:, :], in_=io["sd"]), 16)
    sd_ready = sldc.n

    for mt in range(4):
        osl = slice(mt * 512, (mt + 1) * 512)
        dw_ready = [None, None]
        for half in range(2):
            dw_free = sm.get(("_f0b_dwfree", half))

            @block.gpsimd
            def _(gp, mt=mt, half=half, osl=osl, w=dw_free):
                if mt == 0:
                    gp.wait_ge(sdve.h, hrs_ready)
                if w is not None:
                    gp.wait_ge(spe.h, w)
                sldw[half].i(gp.dma_start(out=B["dwsb"][half][:, :, :],
                                          in_=dwr[:, half * 32:(half + 1) * 32, osl]), 16)
            dw_ready[half] = sldw[half].n

        for t in range(TT):
            ts = slice(t * TN, (t + 1) * TN)
            ht_ready = [None, None]
            for half in range(2):
                ht_free = sm.get(("_f0b_htfree", half))

                @block.sync
                def _(sy, t=t, ts=ts, half=half, mt=mt, w=ht_free):
                    if mt == 0 and t == 0:
                        sy.wait_ge(sdve.h, hrs_ready)
                    if w is not None:
                        sy.wait_ge(spe.h, w)
                    sldh[half].i(sy.dma_start(out=B["htsb"][half][:, :, :],
                                              in_=hr[:, half * 32:(half + 1) * 32, ts]), 16)
                ht_ready[half] = sldh[half].n

            for oc4 in range(4):
                o = mt * 4 + oc4
                pp = oc4 % 2
                p_free = sm.get(("_f0b_pfree", pp))
                rslot = oc4 % 2
                r_free = sm.get(("_f0b_rfree", rslot))

                @block.sync
                def _(sy, o=o, ts=ts, rslot=rslot, w=r_free,
                      first=(mt == 0 and t == 0 and oc4 < 2)):
                    if first:
                        sy.wait_ge(sdve.h, hrs_ready)
                    if w is not None:
                        sy.wait_ge(sdve.h, w)
                    sldx[rslot].i(sy.dma_start(out=B["x0t"][rslot][:, :], in_=x0r[:, o, ts]), 16)
                    sldx[rslot].i(sy.dma_start(out=B["wot"][rslot][:, :], in_=wor[:, o, ts]), 16)
                r_ready = sldx[rslot].n

                @block.tensor
                def _(te, o=o, t=t, pp=pp, wd=dw_ready, wh=ht_ready, pf=p_free,
                      whr=hrs_ready):
                    te.wait_ge(sdve.h, whr)
                    if pf is not None:
                        te.wait_ge(sdve.h, pf)
                    csl = slice((o % 4) * 128, (o % 4 + 1) * 128)
                    for half in range(2):
                        te.wait_ge(sldw[half].h, wd[half])
                        te.wait_ge(sldh[half].h, wh[half])
                        for ic in range(32):
                            te.matmul(out=B["pd"][pp][:, :],
                                      lhsT=B["dwsb"][half][:, ic, csl],
                                      rhs=B["htsb"][half][:, ic, :],
                                      start=(half == 0 and ic == 0), stop=False)
                    ins = te.matmul(out=B["pd"][pp][:, :], lhsT=B["neg128"][:, :],
                                    rhs=B["hrs_sb"][0:1, t, :], start=False, stop=True)
                    spe.i(ins)
                mm_done = spe.n

                xslot = (mt * 16 + t * 4 + oc4) % 2
                xs_free = sm.get(("_f0b_xsfree", xslot))

                @block.vector
                def _(v, o=o, pp=pp, rslot=rslot, xslot=xslot, w=mm_done,
                      wr=r_ready, xf=xs_free, wsd=sd_ready):
                    v.wait_ge(spe.h, w)
                    v.wait_ge(sldx[rslot].h, wr)
                    v.wait_ge(sldc.h, wsd)
                    if xf is not None:
                        v.wait_ge(sstx[xslot].h, xf)
                    sdve.i(v.scalar_tensor_tensor(out=B["tmpb"][:, :], in0=B["pd"][pp][:, :],
                                                  scalar=B["sd_sb"][:, o:o + 1],
                                                  in1=B["wot"][rslot][:, :],
                                                  op0=OP.mult, op1=OP.add))
                    v.wait_ge(sdve.h, sdve.n)
                    sdve.i(v.tensor_tensor(out=B["x1t"][xslot][:, :], in0=B["tmpb"][:, :],
                                           in1=B["x0t"][rslot][:, :], op=OP.add))
                sm[("_f0b_pfree", pp)] = sdve.n
                sm[("_f0b_rfree", rslot)] = sdve.n
                ev_done = sdve.n

                @block.gpsimd
                def _(gp, o=o, ts=ts, xslot=xslot, w=ev_done):
                    gp.wait_ge(sdve.h, w)
                    sstx[xslot].i(gp.dma_start(out=x1r[:, o, ts], in_=B["x1t"][xslot][:, :]), 16)
                sm[("_f0b_xsfree", xslot)] = sstx[xslot].n
            for half in range(2):
                sm[("_f0b_htfree", half)] = spe.n
        for half in range(2):
            sm[("_f0b_dwfree", half)] = spe.n

def ffn0_small_bufs(nc, stack, pfx=""):
    B = {}
    B["neg128"] = stack.enter_context(nc.sbuf_tensor(pfx + "neg128", [1, 128], BF16))
    B["onesb3"] = stack.enter_context(nc.sbuf_tensor(pfx + "onesb3", [128, 1], BF16))
    B["hrs_sb"] = stack.enter_context(nc.sbuf_tensor(pfx + "hrs_sb", [1, TT, TN], BF16))
    B["sxn_sb"] = stack.enter_context(nc.sbuf_tensor(pfx + "sxn_sb", [1, TLOC], BF16))
    B["sg_sb"] = stack.enter_context(nc.sbuf_tensor(pfx + "sg_sb", [128, 64], F32))
    B["su_sb"] = stack.enter_context(nc.sbuf_tensor(pfx + "su_sb", [128, 64], F32))
    B["sd_sb"] = stack.enter_context(nc.sbuf_tensor(pfx + "sd_sb", [128, 16], F32))
    return B


def ffn0_a_bufs(nc, sa, B, pfx=""):
    B["xn_sb"] = sa.enter_context(nc.sbuf_tensor(pfx + "xn_sb", [128, FC, TLOC], BF16))
    B["gwsb"] = [sa.enter_context(nc.sbuf_tensor(pfx + f"gwsb{i}", [128, FC, 512], BF16))
                 for i in range(2)]
    B["uwsb"] = [sa.enter_context(nc.sbuf_tensor(pfx + f"uwsb{i}", [128, FC, 512], BF16))
                 for i in range(2)]
    B["hsb"] = [sa.enter_context(nc.sbuf_tensor(pfx + f"hsb{i}", [128, TLOC], BF16))
                for i in range(2)]
    B["silu_sb"] = [sa.enter_context(nc.sbuf_tensor(pfx + f"silu{i}", [128, TN], F32))
                    for i in range(2)]
    B["pg"] = [sa.enter_context(nc.psum_tensor(pfx + f"pg{i}", [128, TN], F32))
               for i in range(2)]
    B["pu"] = [sa.enter_context(nc.psum_tensor(pfx + f"pu{i}", [128, TN], F32))
               for i in range(2)]
    B["phrs"] = [sa.enter_context(nc.psum_tensor(pfx + f"phrs{i}", [1, TN], F32))
                 for i in range(TT)]
    return B


def ffn0_b_bufs(nc, sb_, B, pfx=""):
    B["dwsb"] = [sb_.enter_context(nc.sbuf_tensor(pfx + f"dwsb{i}", [128, 32, 512], BF16))
                 for i in range(2)]
    B["htsb"] = [sb_.enter_context(nc.sbuf_tensor(pfx + f"htsb{i}", [128, 32, TN], BF16))
                 for i in range(2)]
    B["x0t"] = [sb_.enter_context(nc.sbuf_tensor(pfx + f"x0t{i}", [128, TN], F32))
                for i in range(2)]
    B["wot"] = [sb_.enter_context(nc.sbuf_tensor(pfx + f"wot{i}", [128, TN], F32))
                for i in range(2)]
    B["x1t"] = [sb_.enter_context(nc.sbuf_tensor(pfx + f"x1t{i}", [128, TN], F32))
                for i in range(2)]
    B["tmpb"] = sb_.enter_context(nc.sbuf_tensor(pfx + "tmpb", [128, TN], F32))
    B["pd"] = [sb_.enter_context(nc.psum_tensor(pfx + f"pd{i}", [128, TN], F32))
               for i in range(2)]
    return B


def moe_router_scoped(nc, rstack, B, pfx=""):
    B["pl_t"] = rstack.enter_context(nc.psum_tensor(pfx + "pl", [128, TN], F32))
    B["pw8_t"] = rstack.enter_context(nc.psum_tensor(pfx + "pw8", [128, TLOC], F32))
    return B


def moe_expert_psums(nc, estack, B, pfx=""):
    B["pwb"] = estack.enter_context(nc.psum_tensor(pfx + "pwb", [128, TN], F32))
    B["pg2"] = [estack.enter_context(nc.psum_tensor(pfx + f"pg2{i}", [128, TN], F32))
                for i in range(2)]
    B["pu2"] = [estack.enter_context(nc.psum_tensor(pfx + f"pu2{i}", [128, TN], F32))
                for i in range(2)]
    B["pd2"] = [estack.enter_context(nc.psum_tensor(pfx + f"pd2{i}", [128, TN], F32))
                for i in range(2)]
    return B


def build_ffn0_launch():
    nc = bass.Bass("TRN2", target_bir_lowering=False, debug=False)
    io = {
        "xn": nc.dram_tensor("xn", [DM, TLOC], BF16, kind="ExternalInput").ap(),
        "sxn": nc.dram_tensor("sxn", [1, TLOC], BF16, kind="ExternalInput").ap(),
        "gate_wT": nc.dram_tensor("gate_wT", [DM, I0], I32, kind="ExternalInput").ap(),
        "up_wT": nc.dram_tensor("up_wT", [DM, I0], I32, kind="ExternalInput").ap(),
        "down_wT": nc.dram_tensor("down_wT", [I0, DM], I32, kind="ExternalInput").ap(),
        "sg": nc.dram_tensor("sg", [128, 64], F32, kind="ExternalInput").ap(),
        "su": nc.dram_tensor("su", [128, 64], F32, kind="ExternalInput").ap(),
        "sd": nc.dram_tensor("sd", [128, 16], F32, kind="ExternalInput").ap(),
        "x0": nc.dram_tensor("x0", [DM, TLOC], F32, kind="ExternalInput").ap(),
        "wo_out": nc.dram_tensor("wo_out", [DM, TLOC], F32, kind="ExternalInput").ap(),
        "x1": nc.dram_tensor("x1", [DM, TLOC], F32, kind="ExternalOutput").ap(),
        "h": nc.dram_tensor("h", [I0, TLOC], BF16, kind="Internal").ap(),
    }
    with contextlib.ExitStack() as stack:
        B = ffn0_small_bufs(nc, stack, "")
        sm = make_sems(nc, stack)
        blk = stack.enter_context(nc.Block())
        with contextlib.ExitStack() as sa:
            ffn0_a_bufs(nc, sa, B, "")
            emit_ffn0_a(nc, blk, sm, B, io)
        with contextlib.ExitStack() as sb_:
            ffn0_b_bufs(nc, sb_, B, "")
            emit_ffn0_b(nc, blk, sm, B, io)
            finalize_store_wait(blk, sm)
    return nc
def emit_router(nc, block, sm, B, io):
    sldx = sm.stream("ld_rtx")
    sldr = sm.stream("ld_rtw")
    spe, sact, sdve = sm["pe"], sm["act"], sm["dve"]
    xnr = io["xn"].rearrange("(fc p) t -> p fc t", p=128)

    @block.gpsimd
    def _(gp):
        rr = io["routerT"].rearrange("(fc p) e -> p fc e", p=128)
        sldr.i(gp.dma_start(out=B["rt_sb"][:, :, :], in_=rr), 16)
        sldr.i(gp.dma_start(out=B["eyef_sb"][:, :], in_=io["eyef"]), 16)
        sldr.i(gp.dma_start(out=B["sel_sb"][:, :], in_=io["sel8"]), 16)
    r_ready = sldr.n

    for tc in range(16):
        xslot = tc % 2
        x_free = sm.get(("_rt_xfree", xslot))

        @block.sync
        def _(sy, tc=tc, xslot=xslot, w=x_free):
            if w is not None:
                sy.wait_ge(spe.h, w)
            sldx.i(sy.dma_start(out=B["gslice"][xslot][:, :, :],
                                in_=xnr[:, :, tc * 128:(tc + 1) * 128]), 16)
        x_ready = sldx.n

        pl_free = sm.get("_rt_plfree")

        @block.tensor
        def _(te, xslot=xslot, w=x_ready, wr=r_ready, pf=pl_free):
            te.wait_ge(sldx.h, w)
            te.wait_ge(sldr.h, wr)
            if pf is not None:
                te.wait_ge(sdve.h, pf)
            for fc in range(FC):
                ins = te.matmul(out=B["pl_t"][:, 0:NE], lhsT=B["gslice"][xslot][:, fc, :],
                                rhs=B["rt_sb"][:, fc, :], start=(fc == 0), stop=(fc == FC - 1))
            spe.i(ins)
        sm[("_rt_xfree", xslot)] = spe.n
        lg_done = spe.n

        @block.vector
        def _(v, w=lg_done):
            v.wait_ge(spe.h, w)
            sdve.i(v.tensor_reduce(out=B["m1"][:, :], in_=B["pl_t"][:, 0:NE], axis=AX.X, op=OP.max))
            v.wait_ge(sdve.h, sdve.n)
            sdve.i(v.tensor_scalar(out=B["eq1"][:, :], in0=B["pl_t"][:, 0:NE],
                                   scalar1=B["m1"][:, 0:1], scalar2=None, op0=OP.is_equal))
            v.wait_ge(sdve.h, sdve.n)
            sdve.i(v.scalar_tensor_tensor(out=B["l2"][:, :], in0=B["eq1"][:, :], scalar=-1e30,
                                          in1=B["pl_t"][:, 0:NE], op0=OP.mult, op1=OP.add))
            v.wait_ge(sdve.h, sdve.n)
            sdve.i(v.tensor_reduce(out=B["m2"][:, :], in_=B["l2"][:, :], axis=AX.X, op=OP.max))
            v.wait_ge(sdve.h, sdve.n)
            sdve.i(v.tensor_tensor(out=B["dd"][:, :], in0=B["m1"][:, :],
                                   in1=B["m2"][:, :], op=OP.subtract))
        sm["_rt_plfree"] = sdve.n
        mx_done = sdve.n

        @block.scalar
        def _(sc, w=mx_done):
            sc.wait_ge(sdve.h, w)
            sact.i(sc.activation(out=B["s1"][:, :], in_=B["dd"][:, :], func=AF.Sigmoid))
        s1_done = sact.n

        @block.vector
        def _(v, tc=tc, w=s1_done):
            v.wait_ge(sact.h, w)
            v.tensor_scalar(out=B["s2"][:, :], in0=B["s1"][:, :], scalar1=-1.0,
                            scalar2=1.0, op0=OP.mult, op1=OP.add)
            v.tensor_scalar(out=B["eq2"][:, :], in0=B["l2"][:, :],
                            scalar1=B["m2"][:, 0:1], scalar2=None, op0=OP.is_equal)
            sdve.i(v.tensor_scalar(out=B["w1"][:, :], in0=B["eq1"][:, :],
                                   scalar1=B["s1"][:, 0:1], scalar2=None, op0=OP.mult))
            v.wait_ge(sdve.h, sdve.n)
            sdve.i(v.scalar_tensor_tensor(out=B["wsb"][:, tc, :], in0=B["eq2"][:, :],
                                          scalar=B["s2"][:, 0:1], in1=B["w1"][:, :],
                                          op0=OP.mult, op1=OP.add))
        w_done = sdve.n

    all_w = sdve.n

    @block.tensor
    def _(te, w=all_w):
        te.wait_ge(sdve.h, w)
        for tc in range(16):
            ins = te.matmul(out=B["pw8_t"][0:NE, tc * 128:(tc + 1) * 128],
                            lhsT=B["wsb"][:, tc, :],
                            rhs=B["eyef_sb"][:, :], start=True, stop=True)
        spe.i(ins)
    tr_done = spe.n

    @block.vector
    def _(v, w=tr_done):
        v.wait_ge(spe.h, w)
        sdve.i(v.tensor_copy(out=B["w8sb"][0:8, :], in_=B["pw8_t"][0:NE, 0:TLOC]))
    sm["_rt_done"] = (sdve.n, spe.n)



def emit_moe_prepass(nc, block, sm, B, io):
    stmo = [sm.stream(f"st_mo{t}") for t in range(TT)]
    sdve = sm["dve"]
    a1r = io["attn1"].rearrange("h p t -> (h p) t")
    rt_done = sm["_rt_done"]

    @block.gpsimd
    def _(gp):
        gp.wait_ge(sdve.h, rt_done[0])
        for t in range(TT):
            ts = slice(t * TN, (t + 1) * TN)
            stmo[t].i(gp.dma_start(out=io["out"][:, ts], in_=io["x1"][:, ts]), 16)
        for t in range(TT):
            ts = slice(t * TN, (t + 1) * TN)
            gp.wait_ge(stmo[t].h, stmo[t].n)
            stmo[t].i(gp.dma_start(out=io["out"][:, ts], in_=a1r[:, ts],
                                   accum_op=OP.add), 16)
            sm[("_me_posb", t)] = (stmo[t].n, t)
    sm["_me_prepass_done"] = sdve.n


def emit_moe(nc, block, sm, B, io):
    sldx = sm.stream("ld_mex")
    sldg = [sm.stream("ld_meg0"), sm.stream("ld_meg1")]
    sldd2 = [sm.stream("ld_med0"), sm.stream("ld_med1")]
    stmo = [sm.stream(f"st_mo{t}") for t in range(TT)]
    spe, sact, sdve = sm["pe"], sm["act"], sm["dve"]
    xnr = io["xn"].rearrange("(fc p) t -> p fc t", p=128)
    outr = io["out"].rearrange("(oc p) t -> p oc t", p=128)
    rt_done = sm["_rt_done"]
    prepass_done = sm["_me_prepass_done"]
    prepass_keys = [(t, sm[("_me_posb", t)][0]) for t in range(TT)
                    if ("_me_posb", t) in sm]

    @block.sync
    def _(sy):
        sldx.i(sy.dma_start(out=B["xn_sb2"][:, :, :], in_=xnr), 16)
    xn_ready = sldx.n

    for e in range(NE):
        hslot = e % 2
        h_free = sm.get(("_me_hfree", hslot))
        for ic in range(8):
            wslot = ic % 2
            w_free = sm.get(("_me_wfree", wslot))

            @block.gpsimd
            def _(gp, e=e, ic=ic, wslot=wslot, w=w_free, first=(e == 0 and ic < 2)):
                if first:
                    gp.wait_ge(sdve.h, max(rt_done[0], prepass_done))
                if w is not None:
                    gp.wait_ge(spe.h, w)
                gu = io["egu"][e].rearrange("(fc p) o -> p fc o", p=128)
                sldg[wslot].i(gp.dma_start(
                    out=B["gslice"][wslot][:, :, :],
                    in_=gu[:, :, ic * 128:(ic + 1) * 128]), 16)
                sldg[wslot].i(gp.dma_start(
                    out=B["uslice"][wslot][:, :, :],
                    in_=gu[:, :, 1024 + ic * 128:1024 + (ic + 1) * 128]), 16)
            w_ready = sldg[wslot].n

            for t in range(TT):
                ts = slice(t * TN, (t + 1) * TN)
                pp = (ic * TT + t) % 2
                p_free = sm.get(("_me_pgu", pp))

                @block.tensor
                def _(te, ts=ts, pp=pp, wslot=wslot, w=w_ready, xw=xn_ready,
                      pf=p_free, hf=(h_free if ic == 0 and t == 0 else None)):
                    te.wait_ge(sldx.h, xw)
                    te.wait_ge(sldg[wslot].h, w)
                    if pf is not None:
                        te.wait_ge(sdve.h, pf)
                    if hf is not None:
                        te.wait_ge(spe.h, hf)
                    for fc in range(FC):
                        te.matmul(out=B["pg2"][pp][:, :], lhsT=B["gslice"][wslot][:, fc, :],
                                  rhs=B["xn_sb2"][:, fc, ts], start=(fc == 0), stop=(fc == FC - 1))
                    for fc in range(FC):
                        ins = te.matmul(out=B["pu2"][pp][:, :], lhsT=B["uslice"][wslot][:, fc, :],
                                        rhs=B["xn_sb2"][:, fc, ts], start=(fc == 0), stop=(fc == FC - 1))
                    spe.i(ins)
                mm_done = spe.n

                aslot = (ic * TT + t) % 2

                @block.scalar
                def _(sc, pp=pp, aslot=aslot, w=mm_done):
                    sc.wait_ge(spe.h, w)
                    af = sm.get(("_me_afree", aslot))
                    if af is not None:
                        sc.wait_ge(sdve.h, af)
                    sact.i(sc.activation(out=B["silu2"][aslot][:, :], in_=B["pg2"][pp][:, :],
                                         func=AF.Silu))
                silu_done = sact.n

                @block.vector
                def _(v, ic=ic, t=t, ts=ts, pp=pp, aslot=aslot, hslot=hslot, w=silu_done):
                    v.wait_ge(sact.h, w)
                    sdve.i(v.scalar_tensor_tensor(out=B["h_et"][hslot][:, ic, ts],
                                                  in0=B["pu2"][pp][:, :], scalar=1.0,
                                                  in1=B["silu2"][aslot][:, :],
                                                  op0=OP.mult, op1=OP.mult))
                sm[("_me_afree", aslot)] = sdve.n
                sm[("_me_pgu", pp)] = sdve.n
            sm[("_me_wfree", wslot)] = spe.n
        h_done = sdve.n

        for t in range(TT):
            ts = slice(t * TN, (t + 1) * TN)
            pwb_free = sm.get("_me_pwbfree")

            @block.tensor
            def _(te, e=e, ts=ts, w=rt_done, pf=pwb_free):
                te.wait_ge(sdve.h, w[0])
                if pf is not None:
                    te.wait_ge(sdve.h, pf)
                spe.i(te.matmul(out=B["pwb"][:, :], lhsT=B["sel_sb"][:, e * 128:(e + 1) * 128],
                                rhs=B["w8sb"][0:8, ts], start=True, stop=True))
            wbmm = spe.n

            @block.vector
            def _(v, t=t, w=wbmm, wp=sm.get("_me_wbfree")):
                v.wait_ge(spe.h, w)
                if wp is not None:
                    v.wait_ge(sdve.h, wp)
                sdve.i(v.tensor_copy(out=B["wbcs4"][:, t, :], in_=B["pwb"][:, :]))
                v.wait_ge(sdve.h, sdve.n)
            sm["_me_pwbfree"] = sdve.n
        wbc_done = sdve.n

        for oc in range(16):
            dslot = oc % 2
            ds_free = sm.get(("_me_dsfree", dslot))

            @block.gpsimd
            def _(gp, e=e, oc=oc, dslot=dslot, w=ds_free,
                  first=(e == 0 and oc < 2)):
                if first:
                    gp.wait_ge(sdve.h, max(rt_done[0], prepass_done))
                if w is not None:
                    gp.wait_ge(spe.h, w)
                dw = io["edown"][e].rearrange("(ic p) o -> p ic o", p=128)
                sldd2[dslot].i(gp.dma_start(
                    out=B["dslice"][dslot][:, :, :],
                    in_=dw[:, :, oc * 128:(oc + 1) * 128]), 16)
            d_rdy = sldd2[dslot].n

            mslot = oc % 2
            m_free = sm.get(("_me_mo", mslot))
            for t in range(TT):
                ts = slice(t * TN, (t + 1) * TN)
                pp = (oc * TT + t) % 2
                pd_free = sm.get(("_me_pd", pp))

                @block.tensor
                def _(te, ts=ts, pp=pp, dslot=dslot, hslot=hslot,
                      w=h_done, wd=d_rdy, pf=pd_free):
                    te.wait_ge(sdve.h, w)
                    te.wait_ge(sldd2[dslot].h, wd)
                    if pf is not None:
                        te.wait_ge(sdve.h, pf)
                    for ic in range(8):
                        ins = te.matmul(out=B["pd2"][pp][:, :],
                                        lhsT=B["dslice"][dslot][:, ic, :],
                                        rhs=B["h_et"][hslot][:, ic, ts],
                                        start=(ic == 0), stop=(ic == 7))
                    spe.i(ins)
                dmm_done = spe.n

                @block.vector
                def _(v, t=t, ts=ts, pp=pp, mslot=mslot, w=dmm_done,
                      wb=wbc_done, mf=(m_free if t == 0 else None)):
                    v.wait_ge(spe.h, w)
                    v.wait_ge(sdve.h, wb)
                    if mf is not None:
                        v.wait_ge(stmo[mf[1]].h, mf[0])
                    sdve.i(v.tensor_tensor(out=B["mo_oc"][mslot][:, ts],
                                           in0=B["pd2"][pp][:, :],
                                           in1=B["wbcs4"][:, t, :], op=OP.mult))
                sm[("_me_pd", pp)] = sdve.n
            ev_done = sdve.n
            sm[("_me_dsfree", dslot)] = spe.n

            @block.gpsimd
            def _(gp, e=e, oc=oc, mslot=mslot, w=ev_done, pp_done=prepass_keys):
                gp.wait_ge(sdve.h, w)
                st = stmo[oc % 4]
                if e == 0:
                    for tt_, cnt in pp_done:
                        gp.wait_ge(stmo[tt_].h, cnt)
                gp.wait_ge(st.h, st.n)
                st.i(gp.dma_start(out=outr[:, oc, :], in_=B["mo_oc"][mslot][:, :],
                                  accum_op=OP.add), 16)
            sm[("_me_mo", mslot)] = (stmo[oc % 4].n, oc % 4)
        sm["_me_wbfree"] = sdve.n
        sm[("_me_hfree", hslot)] = spe.n
        sm["_me_dfree"] = spe.n


def build_moe_launch():
    pfx = ""
    nc = bass.Bass("TRN2", target_bir_lowering=False, debug=False)
    io = {
        "xn": nc.dram_tensor("xn", [DM, TLOC], BF16, kind="ExternalInput").ap(),
        "routerT": nc.dram_tensor("routerT", [DM, NE], F32, kind="ExternalInput").ap(),
        "egu": nc.dram_tensor("egu", [NE, DM, DM], F32, kind="ExternalInput").ap(),
        "edown": nc.dram_tensor("edown", [NE, IM, DM], F32, kind="ExternalInput").ap(),
        "sel8": nc.dram_tensor("sel8", [NE, NE * 128], F32, kind="ExternalInput").ap(),
        "eyef": nc.dram_tensor("eyef", [128, 128], F32, kind="ExternalInput").ap(),
        "x1": nc.dram_tensor("x1", [DM, TLOC], F32, kind="ExternalInput").ap(),
        "attn1": nc.dram_tensor("attn1", [NH, 128, TLOC], F32, kind="ExternalInput").ap(),
        "out": nc.dram_tensor("out", [DM, TLOC], F32, kind="ExternalOutput").ap(),
    }
    with contextlib.ExitStack() as stack:
        B = moe_bufs(nc, stack)
        sm = make_sems(nc, stack)
        blk = stack.enter_context(nc.Block())
        with contextlib.ExitStack() as rstack:
            moe_router_scoped(nc, rstack, B, "")
            emit_router(nc, blk, sm, B, io)
            emit_moe_prepass(nc, blk, sm, B, io)
        with contextlib.ExitStack() as estack:
            B["pwb"] = estack.enter_context(nc.psum_tensor("pwb", [128, TN], F32))
            B["pg2"] = [estack.enter_context(nc.psum_tensor(f"pg2{i}", [128, TN], F32))
                        for i in range(2)]
            B["pu2"] = [estack.enter_context(nc.psum_tensor(f"pu2{i}", [128, TN], F32))
                        for i in range(2)]
            B["pd2"] = [estack.enter_context(nc.psum_tensor(f"pd2{i}", [128, TN], F32))
                        for i in range(2)]
            emit_moe(nc, blk, sm, B, io)
            finalize_store_wait(blk, sm)
    return nc


def moe_bufs(nc, stack, pfx=""):
    B = {}
    def sb(name, shape, dt):
        B[name] = stack.enter_context(nc.sbuf_tensor(pfx + name, shape, dt))
    sb("rt_sb", [128, FC, NE], BF16)
    sb("eyef_sb", [128, 128], F32)
    sb("sel_sb", [NE, NE * 128], F32)
    sb("m1", [128, 1], F32)
    sb("m2", [128, 1], F32)
    sb("dd", [128, 1], F32)
    sb("s1", [128, 1], F32)
    sb("s2", [128, 1], F32)
    sb("eq1", [128, NE], F32)
    sb("eq2", [128, NE], F32)
    sb("l2", [128, NE], F32)
    sb("w1", [128, NE], F32)
    sb("wsb", [128, FC, NE], F32)
    sb("w8sb", [NE, TLOC], F32)
    sb("xn_sb2", [128, FC, TLOC], BF16)
    B["gslice"] = [stack.enter_context(nc.sbuf_tensor(pfx + f"gslice{i}", [128, FC, 128], BF16))
                   for i in range(2)]
    B["uslice"] = [stack.enter_context(nc.sbuf_tensor(pfx + f"uslice{i}", [128, FC, 128], BF16))
                   for i in range(2)]
    B["dslice"] = [stack.enter_context(nc.sbuf_tensor(pfx + f"dslice{i}", [128, 8, 128], BF16))
                   for i in range(2)]
    B["h_et"] = [stack.enter_context(nc.sbuf_tensor(pfx + f"h_et{i}", [128, 8, TLOC], BF16))
                 for i in range(2)]
    B["silu2"] = [stack.enter_context(nc.sbuf_tensor(pfx + f"silu2{i}", [128, TN], F32))
                  for i in range(2)]
    sb("wbcs4", [128, TT, TN], F32)
    B["mo_oc"] = [stack.enter_context(nc.sbuf_tensor(pfx + f"mo_oc{i}", [128, TLOC], F32))
                  for i in range(2)]
    return B




def emit_kv_collective(nc, block, sm, cc_in, cc_out, kv_store_stream):
    scc = sm["cc"]

    @block.gpsimd
    def _(gp):
        gp.wait_ge(kv_store_stream.h, kv_store_stream.n)
        scc.i(gp.collective_compute(
            "AllGather", OP.bypass,
            replica_groups=[[0, 1], [2, 3], [4, 5], [6, 7]],
            ins=[cc_in.opt()], outs=[cc_out.opt()]))
    return scc.n


def build_fused():
    nc = bass.Bass("TRN2", target_bir_lowering=False, debug=False)
    EI, EO, IN = "ExternalInput", "ExternalOutput", "Internal"

    def dt_(name, shape, dt, kind):
        return nc.dram_tensor(name, shape, dt, kind=kind).ap()

    io = {
        "x0": dt_("x0", [DM, TLOC], F32, EI),
        "amask": dt_("amask", [N_MSK, 128, TN], BF16, EI),
        "eye": dt_("eye", [128, 128], BF16, EI),
        "eyef": dt_("eyef", [128, 128], F32, EI),
        "sel8": dt_("sel8", [NE, NE * 128], F32, EI),
        "wq0": dt_("wq0", [DM, DM], F32, EI),
        "wk0": dt_("wk0", [DM, 512], F32, EI),
        "wv0": dt_("wv0", [DM, 512], F32, EI),
        "wo_w": dt_("wo_w", [DM, DM], F32, EI),
        "gate_wT": dt_("gate_wT", [DM, I0], I32, EI),
        "up_wT": dt_("up_wT", [DM, I0], I32, EI),
        "down_wT": dt_("down_wT", [I0, DM], I32, EI),
        "sg": dt_("sg", [128, 64], F32, EI),
        "su": dt_("su", [128, 64], F32, EI),
        "sd": dt_("sd", [128, 16], F32, EI),
        "wq1": dt_("wq1", [NH, DM, HD], F32, EI),
        "wk1": dt_("wk1", [NG, DM, HD], F32, EI),
        "wv1": dt_("wv1", [NG, DM, HD], F32, EI),
        "bqT": dt_("bqT", [128, NH], F32, EI),
        "bkT": dt_("bkT", [128, NG], F32, EI),
        "bvT": dt_("bvT", [128, NG], F32, EI),
        "routerT": dt_("routerT", [DM, NE], F32, EI),
        "egu": dt_("egu", [NE, DM, DM], F32, EI),
        "edown": dt_("edown", [NE, IM, DM], F32, EI),
        "out": dt_("out", [DM, TLOC], F32, EO),
        "xn0": dt_("xn0", [DM, TLOC], BF16, IN),
        "sxn0": dt_("sxn0", [1, TLOC], BF16, IN),
        "xn1": dt_("xn1", [DM, TLOC], BF16, IN),
        "sxn1": dt_("sxn1", [1, TLOC], BF16, IN),
        "q0": dt_("q0", [NH, 128, TLOC], BF16, IN),
        "q1": dt_("q1", [NH, 128, TLOC], BF16, IN),
        "cc_in0": dt_("cc_in0", [128, 16384], BF16, IN),
        "cc_in1": dt_("cc_in1", [128, 16384], BF16, IN),
        "cc_out0": dt_("cc_out0", [256, 16384], BF16, IN),
        "cc_out1": dt_("cc_out1", [256, 16384], BF16, IN),
        "attn0": dt_("attn0", [NH, 128, TLOC], BF16, IN),
        "attn1": dt_("attn1", [NH, 128, TLOC], F32, IN),
        "wo_out": dt_("wo_out", [DM, TLOC], F32, IN),
        "h": dt_("h", [I0, TLOC], BF16, IN),
        "x1": dt_("x1", [DM, TLOC], F32, IN),
    }
    with contextlib.ExitStack() as S0:
        sm = make_sems(nc, S0)
        blk = S0.enter_context(nc.Block())

        def barrier():
            cnts = ([(sm[e].h, sm[e].n) for e in ("pe", "act", "dve", "gp")]
                    + [(s.h, s.n) for s in sm._streams.values()])
            cnts = [(hh, n) for hh, n in cnts if n > 0]
            for ename in ("tensor", "scalar", "vector", "gpsimd", "sync"):
                def f(eng, cnts=cnts):
                    for hh, n in cnts:
                        eng.wait_ge(hh, n)
                getattr(blk, ename)(f)

        for layer in (0, 1):
            P = f"l{layer}"
            xsrc = io["x0"] if layer == 0 else io["x1"]
            xn, sxn = io[f"xn{layer}"], io[f"sxn{layer}"]
            barrier()
            with contextlib.ExitStack() as s:
                B = ln_bufs(nc, s, P + "ln")
                emit_ln(nc, blk, sm, B, {"x": xsrc, "xn": xn, "sxn": sxn})
            barrier()
            with contextlib.ExitStack() as s:
                B = qkv_bufs(nc, s, layer, P + "qk")
                ioq = {"xn": xn, "eye": io["eye"], "q_buf": io[f"q{layer}"],
                       "kv_cc": io[f"cc_in{layer}"]}
                if layer == 0:
                    ioq.update(wq=io["wq0"], wk=io["wk0"], wv=io["wv0"])
                else:
                    ioq.update(wq=io["wq1"], wk=io["wk1"], wv=io["wv1"],
                               bqT=io["bqT"], bkT=io["bkT"], bvT=io["bvT"])
                emit_qkv(nc, blk, sm, B, ioq, layer)
                cc_n = emit_kv_collective(nc, blk, sm, io[f"cc_in{layer}"],
                                          io[f"cc_out{layer}"], sm.stream("st_kv"))
            barrier()
            with contextlib.ExitStack() as s:
                B = attn_bufs(nc, s, layer, P + "at")
                ioa = {"q_buf": io[f"q{layer}"], "kv_pair": io[f"cc_out{layer}"],
                       "amask": io["amask"], "attn_out": io[f"attn{layer}"],
                       "cc_dep": (sm["cc"].h, cc_n)}
                emit_attn(nc, blk, sm, B, ioa, layer)
            if layer == 0:
                barrier()
                with contextlib.ExitStack() as s:
                    B = wo_bufs(nc, s, "wo")
                    emit_wo(nc, blk, sm, B, {"attn_out": io["attn0"],
                                             "wo": io["wo_w"], "wo_out": io["wo_out"]})
                barrier()
                with contextlib.ExitStack() as s:
                    B = ffn0_small_bufs(nc, s, "f0")
                    iof = {"xn": xn, "sxn": sxn, "gate_wT": io["gate_wT"],
                           "up_wT": io["up_wT"], "down_wT": io["down_wT"],
                           "sg": io["sg"], "su": io["su"], "sd": io["sd"],
                           "x0": io["x0"], "wo_out": io["wo_out"], "x1": io["x1"],
                           "h": io["h"]}
                    with contextlib.ExitStack() as sa:
                        ffn0_a_bufs(nc, sa, B, "f0a")
                        emit_ffn0_a(nc, blk, sm, B, iof)
                    with contextlib.ExitStack() as sb_:
                        ffn0_b_bufs(nc, sb_, B, "f0b")
                        emit_ffn0_b(nc, blk, sm, B, iof)
            else:
                barrier()
                with contextlib.ExitStack() as s:
                    B = moe_bufs(nc, s, "me")
                    iom = {"xn": xn, "routerT": io["routerT"], "egu": io["egu"],
                           "edown": io["edown"], "sel8": io["sel8"],
                           "eyef": io["eyef"], "x1": io["x1"],
                           "attn1": io["attn1"], "out": io["out"]}
                    with contextlib.ExitStack() as rs:
                        moe_router_scoped(nc, rs, B, "me")
                        emit_router(nc, blk, sm, B, iom)
                        emit_moe_prepass(nc, blk, sm, B, iom)
                    with contextlib.ExitStack() as es:
                        moe_expert_psums(nc, es, B, "me")
                        emit_moe(nc, blk, sm, B, iom)
        finalize_store_wait(blk, sm)

        @blk.gpsimd
        def _(gp):
            gp.wait_ge(sm["cc"].h, sm["cc"].n)
    return nc


def kernel_fused(**inputs):
    g, per = _prep_inputs(inputs)
    nc = _get_nc("fused", build_fused)
    maps = []
    for c in range(8):
        m = {"x0": per[c]["x0"], "amask": per[c]["amask"],
             "eye": g["eye"], "eyef": g["eyef"], "sel8": g["sel8"],
             "wq0": g["l0_wq"], "wk0": g["l0_wk"], "wv0": g["l0_wv"],
             "wo_w": g["l0_wo"], "gate_wT": g["gate_wT"], "up_wT": g["up_wT"],
             "down_wT": g["down_wT"], "sg": g["sg"], "su": g["su"], "sd": g["sd"],
             "wq1": g["l1_wq"], "wk1": g["l1_wk"], "wv1": g["l1_wv"],
             "bqT": g["l1_bq"], "bkT": g["l1_bk"], "bvT": g["l1_bv"],
             "routerT": g["routerT"], "egu": g["egu"], "edown": g["edown"]}
        maps.append(m)
    res = run_bass_kernel_spmd(nc, maps, core_ids=list(range(8)))
    return _unprep_output([res.results[c]["out"] for c in range(8)])


_NC_CACHE = {}


def _get_nc(name, builder):
    if name not in _NC_CACHE:
        _NC_CACHE[name] = builder()
    return _NC_CACHE[name]


def kernel(**inputs):
    return kernel_fused(**inputs)


def kernel_unfused(**inputs):
    g, per = _prep_inputs(inputs)
    C = list(range(8))

    def run(nc, maps):
        return run_bass_kernel_spmd(nc, maps, core_ids=C).results

    out_final = None
    xcur = [per[c]["x0"] for c in C]
    for layer in (0, 1):
        r = run(_get_nc("ln", build_ln_launch), [{"x": xcur[c]} for c in C])
        xn = [r[c]["xn"] for c in C]
        sxn = [r[c]["sxn"] for c in C]
        if layer == 0:
            maps = [{"xn": xn[c], "eye": g["eye"], "wq": g["l0_wq"],
                     "wk": g["l0_wk"], "wv": g["l0_wv"]} for c in C]
            r = run(_get_nc("qkv0", lambda: build_qkv_launch(0)), maps)
        else:
            maps = [{"xn": xn[c], "eye": g["eye"], "wq": g["l1_wq"],
                     "wk": g["l1_wk"], "wv": g["l1_wv"], "bqT": g["l1_bq"],
                     "bkT": g["l1_bk"], "bvT": g["l1_bv"]} for c in C]
            r = run(_get_nc("qkv1", lambda: build_qkv_launch(1)), maps)
        q = [r[c]["q_buf"] for c in C]
        kv = [r[c]["kv_cc"] for c in C]
        kvp = [np.concatenate([kv[2 * (c // 2)], kv[2 * (c // 2) + 1]], axis=0)
               for c in C]
        maps = [{"q_buf": q[c], "kv_pair": kvp[c], "amask": per[c]["amask"]}
                for c in C]
        if layer == 0:
            r = run(_get_nc("att0", lambda: build_attn_launch(0)), maps)
            attn = [r[c]["attn_out"] for c in C]
            r = run(_get_nc("wo", build_wo_launch),
                    [{"attn_out": attn[c], "wo": g["l0_wo"]} for c in C])
            wo_out = [r[c]["wo_out"] for c in C]
            maps = [{"xn": xn[c], "sxn": sxn[c], "gate_wT": g["gate_wT"],
                     "up_wT": g["up_wT"], "down_wT": g["down_wT"], "sg": g["sg"],
                     "su": g["su"], "sd": g["sd"], "x0": xcur[c],
                     "wo_out": wo_out[c]} for c in C]
            r = run(_get_nc("ffn0", build_ffn0_launch), maps)
            xcur = [r[c]["x1"] for c in C]
        else:
            r = run(_get_nc("att1", lambda: build_attn_launch(1)), maps)
            attn = [r[c]["attn_out"] for c in C]
            maps = [{"xn": xn[c], "routerT": g["routerT"], "egu": g["egu"],
                     "edown": g["edown"], "sel8": g["sel8"], "eyef": g["eyef"],
                     "x1": xcur[c], "attn1": attn[c]} for c in C]
            r = run(_get_nc("moe", build_moe_launch), maps)
            out_final = [r[c]["out"] for c in C]
    return _unprep_output(out_final)


# version 07 (final)
# speedup vs baseline: 1.3364x; 1.0546x over previous
import contextlib
import numpy as np
import ml_dtypes
import concourse.bass as bass
import concourse.mybir as mybir
from concourse.bass_utils import run_bass_kernel_spmd

F32 = mybir.dt.float32
BF16 = mybir.dt.bfloat16
I32 = mybir.dt.int32
AF = mybir.ActivationFunctionType
OP = mybir.AluOpType
AX = mybir.AxisListType

BF = ml_dtypes.bfloat16

DM = 2048
FC = 16
TLOC = 2048
TT = 4
TN = 512
NH = 16
NG = 4
HD = 128
SEQ = 4096
I0 = 8192
NE = 8
IM = 1024
LHAT = (12, 16, 28, 32)
MSK_RANGE = ((0, 12), (0, 16), (16, 28), (20, 32))
N_MSK = sum(b - a for a, b in MSK_RANGE)
SCALE = 1.0 / np.sqrt(128.0)


class S:

    def __init__(self, h):
        self.h = h
        self.n = 0

    def i(self, instr, k=1):
        instr.then_inc(self.h, k)
        self.n += k
        return self.n


def _core_tokens(core):
    s, half = core // 2, core % 2
    if half == 0:
        return s, [(0, 1024), (3072, 4096)]
    return s, [(1024, 3072)]


GRPS_A = (0, 1, 6, 7)
GRPS_B = (2, 3, 4, 5)


def _host_masks(core):
    half = core % 2
    grps = GRPS_A if half == 0 else GRPS_B
    tiles = []
    for r in range(4):
        g = grps[r]
        q_glob = np.arange(g * 512, (g + 1) * 512)
        a, b = MSK_RANGE[r]
        for kc in range(a, b):
            k_glob = np.arange(kc * 128, (kc + 1) * 128)
            tiles.append((k_glob[:, None] <= q_glob[None, :]).astype(BF))
    return np.stack(tiles)


def _prep_inputs(inputs):
    g = {}
    g["l0_wq"] = np.ascontiguousarray(inputs["L0_attn_Wq"])
    g["l0_wk"] = np.ascontiguousarray(inputs["L0_attn_Wk"])
    g["l0_wv"] = np.ascontiguousarray(inputs["L0_attn_Wv"])
    g["l0_wo"] = np.ascontiguousarray(inputs["L0_attn_Wo"])
    g["gate_wT"] = np.ascontiguousarray(inputs["L0_ffn_gate_w"].T)
    g["up_wT"] = np.ascontiguousarray(inputs["L0_ffn_up_w"].T)
    g["down_wT"] = np.ascontiguousarray(inputs["L0_ffn_down_w"].T)
    g["sg"] = np.ascontiguousarray(inputs["L0_ffn_gate_scale"].reshape(64, 128).T)
    g["su"] = np.ascontiguousarray(inputs["L0_ffn_up_scale"].reshape(64, 128).T)
    g["sd"] = np.ascontiguousarray(inputs["L0_ffn_down_scale"].reshape(16, 128).T)
    g["l1_wq"] = np.ascontiguousarray(inputs["L1_attn_Wq"])
    g["l1_wk"] = np.ascontiguousarray(inputs["L1_attn_Wk"])
    g["l1_wv"] = np.ascontiguousarray(inputs["L1_attn_Wv"])
    g["l1_bq"] = np.ascontiguousarray(inputs["L1_attn_bq"].T)
    g["l1_bk"] = np.ascontiguousarray(inputs["L1_attn_bk"].T)
    g["l1_bv"] = np.ascontiguousarray(inputs["L1_attn_bv"].T)
    g["routerT"] = np.ascontiguousarray(inputs["L1_ffn_router_gate"].T)
    egu = np.asarray(inputs["L1_ffn_expert_gate_up"])
    g["egu"] = np.ascontiguousarray(
        np.concatenate([egu[..., 0::2], egu[..., 1::2]], axis=-1))
    g["edown"] = np.ascontiguousarray(inputs["L1_ffn_expert_down"])
    sel = np.zeros((8, 8 * 128), dtype=np.float32)
    for e in range(8):
        sel[e, e * 128:(e + 1) * 128] = 1.0
    g["sel8"] = sel
    g["eye"] = np.eye(128, dtype=BF)
    g["eyef"] = np.eye(128, dtype=np.float32)
    x0 = np.asarray(inputs["L0_attn_x"])
    per = []
    for c in range(8):
        s, spans = _core_tokens(c)
        tok = np.concatenate([x0[s, a:b] for a, b in spans], axis=0)
        d = {
            "x0": np.ascontiguousarray(tok.T),
            "amask": _host_masks(c),
        }
        per.append(d)
    return g, per


def _unprep_output(outs):
    res = np.empty((4, 4096, 2048), dtype=np.float32)
    for c in range(8):
        s, spans = _core_tokens(c)
        tm = outs[c].T
        off = 0
        for a, b in spans:
            res[s, a:b] = tm[off:off + (b - a)]
            off += b - a
    return res



def emit_ln(nc, block, sm, B, io):
    x, xn, sxn = io["x"], io["xn"], io["sxn"]
    xr = x.rearrange("(fc p) t -> p fc t", p=128)
    xnr = xn.rearrange("(fc p) t -> p fc t", p=128)
    sld = sm.stream("ld_lnx")
    sst = sm.stream("st_ln")
    spe, sact, sdve = sm["pe"], sm["act"], sm["dve"]

    @block.vector
    def _(v):
        v.memset(B["onesf"][:, :], 1.0)
        v.memset(B["onescf"][:, :], 1.0)
        mi = v.memset(B["epsrow"][:, :], 1e-5)
        sdve.i(mi)

    @block.scalar
    def _(sc):
        sc.wait_ge(sdve.h, sdve.n)
        sact.i(sc.copy(out=B["onesb"][:, :], in_=B["onesf"][:, :]))

    for t in range(TT):
        ts = slice(t * TN, (t + 1) * TN)
        xt_free_at = sm.get("_ln_xt_free", 0)

        @block.sync
        def _(sy, t=t, ts=ts, w=xt_free_at):
            if t > 0:
                sy.wait_ge(sdve.h, w)
            sld.i(sy.dma_start(out=B["xt"][:, :, :], in_=xr[:, :, ts]), 16)
        ld_done = sld.n

        @block.scalar
        def _(sc, w=ld_done):
            sc.wait_ge(sld.h, w)
            for fc in range(FC):
                ins = sc.activation(out=B["xq"][:, fc, :], in_=B["xt"][:, fc, :],
                                    func=AF.Square)
                if fc == FC - 1:
                    sact.i(ins)
        sq_done = sact.n

        @block.tensor
        def _(te, w=ld_done, wq=sq_done):
            te.wait_ge(sld.h, w)
            for fc in range(FC):
                ins = te.matmul(out=B["ps1"][:, :], lhsT=B["onesf"][:, :],
                                rhs=B["xt"][:, fc, :], start=(fc == 0), stop=(fc == FC - 1))
            spe.i(ins)
            te.wait_ge(sact.h, wq)
            for fc in range(FC):
                ins = te.matmul(out=B["ps2"][:, :], lhsT=B["onesf"][:, :],
                                rhs=B["xq"][:, fc, :], start=(fc == 0), stop=(fc == FC - 1))
            spe.i(ins)
        stats_done = spe.n

        @block.scalar
        def _(sc, w=stats_done):
            sc.wait_ge(spe.h, w)
            sact.i(sc.activation(out=B["mrow"][:, :], in_=B["ps1"][:, :],
                                 func=AF.Copy, scale=1.0 / DM))
        m_done = sact.n

        @block.scalar
        def _(sc, w=m_done):
            sc.wait_ge(sact.h, w)
            sact.i(sc.activation(out=B["msq"][:, :], in_=B["mrow"][:, :], func=AF.Square))
        msq_done = sact.n

        @block.vector
        def _(v, w=msq_done, wp=stats_done):
            v.wait_ge(sact.h, w)
            v.wait_ge(spe.h, wp)
            sdve.i(v.scalar_tensor_tensor(out=B["varrow"][:, :], in0=B["ps2"][:, :],
                                          scalar=1.0 / DM, in1=B["msq"][:, :],
                                          op0=OP.mult, op1=OP.subtract))
        var_done = sdve.n

        @block.scalar
        def _(sc, w=var_done):
            sc.wait_ge(sdve.h, w)
            sact.i(sc.activation(out=B["sdrow"][:, :], in_=B["varrow"][:, :],
                                 func=AF.Sqrt, bias=B["epsrow"][:, 0:1]))
        sd_done = sact.n

        @block.vector
        def _(v, w=sd_done):
            v.wait_ge(sact.h, w)
            sdve.i(v.reciprocal(out=B["rstd"][:, :], in_=B["sdrow"][:, :]))
        rstd_done = sdve.n

        @block.tensor
        def _(te, w=rstd_done, wm=m_done):
            te.wait_ge(sact.h, wm)
            te.wait_ge(sdve.h, w)
            te.matmul(out=B["pbm"][:, :], lhsT=B["onescf"][:, :], rhs=B["mrow"][:, :],
                      start=True, stop=True)
            spe.i(te.matmul(out=B["pbr"][:, :], lhsT=B["onescf"][:, :],
                            rhs=B["rstd"][:, :], start=True, stop=True))
        bc_done = spe.n

        @block.vector
        def _(v, w=bc_done):
            v.wait_ge(spe.h, w)
            v.tensor_copy(out=B["mbc"][:, :], in_=B["pbm"][:, :])
            sdve.i(v.tensor_copy(out=B["rbc"][:, :], in_=B["pbr"][:, :]))
        bc_sb = sdve.n

        xnt_busy = sm.get("_ln_xnt_busy")

        @block.vector
        def _(v, w=bc_sb, wb=xnt_busy):
            v.wait_ge(sdve.h, w)
            if wb is not None:
                v.wait_ge(sst.h, wb[0])
                v.wait_ge(spe.h, wb[1])
            for fc in range(FC):
                sdve.i(v.tensor_tensor(out=B["tmp"][:, :], in0=B["xt"][:, fc, :],
                                       in1=B["mbc"][:, :], op=OP.subtract))
                v.wait_ge(sdve.h, sdve.n)
                ins = v.tensor_tensor(out=B["xnt"][:, fc, :], in0=B["tmp"][:, :],
                                      in1=B["rbc"][:, :], op=OP.mult)
            sdve.i(ins)
        xn_done = sdve.n
        sm["_ln_xt_free"] = xn_done

        @block.tensor
        def _(te, w=xn_done):
            te.wait_ge(sdve.h, w)
            for fc in range(FC):
                ins = te.matmul(out=B["psx"][:, :], lhsT=B["onesb"][:, :],
                                rhs=B["xnt"][:, fc, :], start=(fc == 0), stop=(fc == FC - 1))
            spe.i(ins)
        sx_done = spe.n

        @block.vector
        def _(v, w=sx_done):
            v.wait_ge(spe.h, w)
            sdve.i(v.tensor_copy(out=B["sxrow"][:, :], in_=B["psx"][:, :]))
        sx_sb = sdve.n

        @block.gpsimd
        def _(gp, ts=ts, w=xn_done, w2=sx_sb):
            gp.wait_ge(sdve.h, w)
            sst.i(gp.dma_start(out=xnr[:, :, ts], in_=B["xnt"][:, :, :]), 16)
            gp.wait_ge(sdve.h, w2)
            sst.i(gp.dma_start(out=sxn[0:1, ts], in_=B["sxrow"][:, :]), 16)
        sm["_ln_xnt_busy"] = (sst.n, sx_done)


def ln_bufs(nc, stack, pfx=""):
    B = {}
    def sb(name, shape, dt):
        B[name] = stack.enter_context(nc.sbuf_tensor(pfx + name, shape, dt))
    def ps(name, shape, dt):
        B[name] = stack.enter_context(nc.psum_tensor(pfx + name, shape, dt))
    sb("xt", [128, FC, TN], F32)
    sb("xq", [128, FC, TN], F32)
    sb("xnt", [128, FC, TN], BF16)
    sb("tmp", [128, TN], F32)
    sb("onesf", [128, 1], F32)
    sb("onescf", [1, 128], F32)
    sb("onesb", [128, 1], BF16)
    sb("mrow", [1, TN], F32)
    sb("epsrow", [1, 1], F32)
    sb("msq", [1, TN], F32)
    sb("varrow", [1, TN], F32)
    sb("sdrow", [1, TN], F32)
    sb("rstd", [1, TN], F32)
    sb("mbc", [128, TN], F32)
    sb("rbc", [128, TN], F32)
    sb("sxrow", [1, TN], BF16)
    ps("ps1", [1, TN], F32)
    ps("ps2", [1, TN], F32)
    ps("psx", [1, TN], F32)
    ps("pbm", [128, TN], F32)
    ps("pbr", [128, TN], F32)
    return B


class Sems(dict):

    def __init__(self, nc, stack):
        super().__init__()
        self._nc = nc
        self._stack = stack
        for name in ["pe", "act", "dve", "gp", "cc"]:
            self[name] = S(stack.enter_context(nc.semaphore(name)))
        self._streams = {}

    def stream(self, name):
        if name not in self._streams:
            self._streams[name] = S(self._stack.enter_context(self._nc.semaphore(name)))
        return self._streams[name]

    def all_store_streams(self):
        return [s for n, s in self._streams.items() if n.startswith("st_")]


def make_sems(nc, stack):
    return Sems(nc, stack)


def finalize_store_wait(block, sm):
    @block.gpsimd
    def _(gp):
        for s in sm.all_store_streams():
            gp.wait_ge(s.h, s.n)



def build_ln_launch():
    nc = bass.Bass("TRN2", target_bir_lowering=False, debug=False)
    io = {
        "x": nc.dram_tensor("x", [DM, TLOC], F32, kind="ExternalInput").ap(),
        "xn": nc.dram_tensor("xn", [DM, TLOC], BF16, kind="ExternalOutput").ap(),
        "sxn": nc.dram_tensor("sxn", [1, TLOC], BF16, kind="ExternalOutput").ap(),
    }
    with contextlib.ExitStack() as stack:
        B = ln_bufs(nc, stack)
        sm = make_sems(nc, stack)
        blk = stack.enter_context(nc.Block())
        emit_ln(nc, blk, sm, B, io)
        finalize_store_wait(blk, sm)
    return nc




def emit_qkv(nc, block, sm, B, io, layer):
    sld = sm.stream("ld_qkvx")
    sldw = [sm.stream("ld_w0"), sm.stream("ld_w1")]
    sstq = [sm.stream("st_q0"), sm.stream("st_q1")]
    sstkv = sm.stream("st_kv")
    spe, sact, sdve = sm["pe"], sm["act"], sm["dve"]
    xnr = io["xn"].rearrange("(fc p) t -> p fc t", p=128)

    @block.sync
    def _(sy):
        sld.i(sy.dma_start(out=B["xn_sb"][:, :, :], in_=xnr[:, :, :]), 16)
        sld.i(sy.dma_start(out=B["eye_sb"][:, :], in_=io["eye"]), 16)
        if layer == 1:
            sld.i(sy.dma_start(out=B["bq_sb"][:, :], in_=io["bqT"]), 16)
            sld.i(sy.dma_start(out=B["bk_sb"][:, :], in_=io["bkT"]), 16)
            sld.i(sy.dma_start(out=B["bv_sb"][:, :], in_=io["bvT"]), 16)
    xn_ready = sld.n

    chunks = [("q", h) for h in range(NH)] + [("k", g) for g in range(NG)] \
        + [("v", g) for g in range(NG)]
    w_done_pe = {}

    for ci, (kind, idx) in enumerate(chunks):
        slot = ci % 2
        @block.gpsimd
        def _(gp, kind=kind, idx=idx, slot=slot, w=w_done_pe.get(slot)):
            if w is not None:
                gp.wait_ge(spe.h, w)
            if layer == 0:
                src = {"q": io["wq"], "k": io["wk"], "v": io["wv"]}[kind]
                ap = src.rearrange("(fc p) d -> p fc d", p=128)[:, :, idx * 128:(idx + 1) * 128]
            else:
                src = {"q": io["wq"], "k": io["wk"], "v": io["wv"]}[kind]
                ap = src[idx].rearrange("(fc p) d -> p fc d", p=128)[:, :, :]
            sldw[slot].i(gp.dma_start(out=B["wchunk"][slot][:, :, :], in_=ap), 16)
        w_ready = sldw[slot].n

        for t in range(TT):
            ts = slice(t * TN, (t + 1) * TN)
            pq = B["pq"][t % 2]
            pq_free = sm.get(("_qkv_pq", t % 2))

            @block.tensor
            def _(te, w=w_ready, pq=pq, ts=ts, slot=slot, pf=pq_free, xw=xn_ready):
                te.wait_ge(sld.h, xw)
                te.wait_ge(sldw[slot].h, w)
                if pf is not None:
                    te.wait_ge(sdve.h, pf)
                for fc in range(FC):
                    ins = te.matmul(out=pq[:, :], lhsT=B["wchunk"][slot][:, fc, :],
                                    rhs=B["xn_sb"][:, fc, ts],
                                    start=(fc == 0), stop=(fc == FC - 1))
                spe.i(ins)
            mm_done = spe.n

            if kind == "q":
                dst = B["qsb"][idx % 2]
                dsl = slice(t * TN, (t + 1) * TN)
            elif kind == "k":
                dst = B["kvsb"]
                dsl = slice(idx * 2048 + t * TN, idx * 2048 + (t + 1) * TN)
            else:
                dst = B["vtmp"]
                dsl = slice(0, TN)
            qst_wait = sm.get(("_qkv_qst", idx % 2)) if kind == "q" else None

            @block.vector
            def _(v, w=mm_done, pq=pq, kind=kind, idx=idx, dst=dst, dsl=dsl, qw=qst_wait):
                v.wait_ge(spe.h, w)
                if qw is not None and dsl.start == 0:
                    v.wait_ge(sstq[idx % 2].h, qw)
                if layer == 0:
                    ins = v.tensor_copy(out=dst[:, dsl], in_=pq[:, :])
                else:
                    bias = {"q": B["bq_sb"], "k": B["bk_sb"], "v": B["bv_sb"]}[kind]
                    ins = v.tensor_scalar(out=dst[:, dsl], in0=pq[:, :],
                                          scalar1=bias[:, idx:idx + 1], scalar2=None,
                                          op0=OP.add)
                sdve.i(ins)
            sm[("_qkv_pq", t % 2)] = sdve.n
            ev_done = sdve.n

            if kind == "v":
                for j in range(4):
                    kc = t * 4 + j

                    @block.tensor
                    def _(te, w=ev_done, j=j, kc=kc, idx=idx):
                        te.wait_ge(sdve.h, w)
                        if ("_qkv_pt",) in sm:
                            te.wait_ge(sdve.h, sm[("_qkv_pt",)])
                        spe.i(te.transpose(out=B["pt"][:, :],
                                           in_=B["vtmp"][:, j * 128:(j + 1) * 128],
                                           identity=B["eye_sb"][:, :]))
                    tr_done = spe.n

                    @block.vector
                    def _(v, w=tr_done, kc=kc, idx=idx):
                        v.wait_ge(spe.h, w)
                        col = 8192 + idx * 2048 + kc * 128
                        sdve.i(v.tensor_copy(out=B["kvsb"][:, col:col + 128],
                                             in_=B["pt"][:, :]))
                    sm[("_qkv_pt",)] = sdve.n

        if kind == "q":
            q_ev = sdve.n

            @block.gpsimd
            def _(gp, idx=idx, w=q_ev):
                gp.wait_ge(sdve.h, w)
                sstq[idx % 2].i(gp.dma_start(out=io["q_buf"][idx], in_=B["qsb"][idx % 2][:, :]), 16)
            sm[("_qkv_qst", idx % 2)] = sstq[idx % 2].n
        w_done_pe[slot] = spe.n

    @block.gpsimd
    def _(gp):
        gp.wait_ge(sdve.h, sdve.n)
        sstkv.i(gp.dma_start(out=io["kv_cc"], in_=B["kvsb"][:, :]), 16)


def qkv_bufs(nc, stack, layer, pfx=""):
    B = {}
    def sb(name, shape, dt):
        B[name] = stack.enter_context(nc.sbuf_tensor(pfx + name, shape, dt))
    def ps(name, shape, dt):
        B[name] = stack.enter_context(nc.psum_tensor(pfx + name, shape, dt))
    sb("xn_sb", [128, FC, TLOC], BF16)
    B["wchunk"] = [stack.enter_context(nc.sbuf_tensor(pfx + f"wchunk{i}", [128, FC, 128], BF16))
                   for i in range(2)]
    B["qsb"] = [stack.enter_context(nc.sbuf_tensor(pfx + f"qsb{i}", [128, TLOC], BF16))
                for i in range(2)]
    sb("kvsb", [128, 16384], BF16)
    sb("vtmp", [128, TN], BF16)
    sb("eye_sb", [128, 128], BF16)
    if layer == 1:
        sb("bq_sb", [128, NH], F32)
        sb("bk_sb", [128, NG], F32)
        sb("bv_sb", [128, NG], F32)
    B["pq"] = [stack.enter_context(nc.psum_tensor(pfx + f"pq{i}", [128, TN], F32))
               for i in range(2)]
    ps("pt", [128, 128], BF16)
    return B


def build_qkv_launch(layer):
    nc = bass.Bass("TRN2", target_bir_lowering=False, debug=False)
    io = {
        "xn": nc.dram_tensor("xn", [DM, TLOC], BF16, kind="ExternalInput").ap(),
        "eye": nc.dram_tensor("eye", [128, 128], BF16, kind="ExternalInput").ap(),
        "q_buf": nc.dram_tensor("q_buf", [NH, 128, TLOC], BF16, kind="ExternalOutput").ap(),
        "kv_cc": nc.dram_tensor("kv_cc", [128, 16384], BF16, kind="ExternalOutput").ap(),
    }
    if layer == 0:
        io["wq"] = nc.dram_tensor("wq", [DM, DM], F32, kind="ExternalInput").ap()
        io["wk"] = nc.dram_tensor("wk", [DM, 512], F32, kind="ExternalInput").ap()
        io["wv"] = nc.dram_tensor("wv", [DM, 512], F32, kind="ExternalInput").ap()
    else:
        io["wq"] = nc.dram_tensor("wq", [NH, DM, HD], F32, kind="ExternalInput").ap()
        io["wk"] = nc.dram_tensor("wk", [NG, DM, HD], F32, kind="ExternalInput").ap()
        io["wv"] = nc.dram_tensor("wv", [NG, DM, HD], F32, kind="ExternalInput").ap()
        io["bqT"] = nc.dram_tensor("bqT", [128, NH], F32, kind="ExternalInput").ap()
        io["bkT"] = nc.dram_tensor("bkT", [128, NG], F32, kind="ExternalInput").ap()
        io["bvT"] = nc.dram_tensor("bvT", [128, NG], F32, kind="ExternalInput").ap()
    with contextlib.ExitStack() as stack:
        B = qkv_bufs(nc, stack, layer)
        sm = make_sems(nc, stack)
        blk = stack.enter_context(nc.Block())
        emit_qkv(nc, blk, sm, B, io, layer)
        finalize_store_wait(blk, sm)
    return nc




def _mask_index():
    d = {}
    mi = 0
    for r in range(4):
        a, b = MSK_RANGE[r]
        for kc in range(a, b):
            d[(r, kc)] = mi
            mi += 1
    return d

MASK_IDX = _mask_index()


def emit_attn(nc, block, sm, B, io, layer):
    sldm = sm.stream("ld_msk")
    sldkv = sm.stream("ld_kv")
    sldq = [sm.stream("ld_aq0"), sm.stream("ld_aq1")]
    ssto = [sm.stream("st_ao0"), sm.stream("st_ao1")]
    spe, sact, sdve, sgp = sm["pe"], sm["act"], sm["dve"], sm["gp"]
    kvp = io["kv_pair"]

    @block.sync
    def _(sy):
        mr = io["amask"].rearrange("m p c -> p m c")
        sldm.i(sy.dma_start(out=B["msk"][:, :, :], in_=mr), 16)
    msk_ready = sldm.n

    @block.vector
    def _(v):
        v.memset(B["onesb2"][:, :], 1.0)
        sdve.i(v.memset(B["onescf2"][:, :], 1.0))
    ones_dve = sdve.n

    def emit_rsav(hi, kc, sslot, first, last, why):
        csl = slice(hi * TN, (hi + 1) * TN)

        @block.tensor
        def _(te, hi=hi, kc=kc, sslot=sslot, first=first, last=last, csl=csl,
              w=why):
            if w is not None:
                te.wait_ge(w[0], w[1])
            te.matmul(out=B["prs"][hi][0:1, :], lhsT=B["onesb2"][:, :],
                      rhs=B["p_sb"][sslot][:, csl], start=first, stop=last)
            ins = te.matmul(out=B["pav"][hi][:, :],
                            lhsT=B["vsb"][:, kc * 128:(kc + 1) * 128],
                            rhs=B["p_sb"][sslot][:, csl], start=first, stop=last)
            spe.i(ins)

    for g in range(NG):
        kv_free = sm.get("_att_kv_free")

        @block.sync
        def _(sy, g=g, w=kv_free, ccd=io.get("cc_dep")):
            if ccd is not None and g == 0:
                sy.wait_ge(ccd[0], ccd[1])
            if w is not None:
                sy.wait_ge(spe.h, w)
            c0 = g * 2048
            sldkv.i(sy.dma_start(out=B["ksb"][:, 0:1024], in_=kvp[0:128, c0:c0 + 1024]), 16)
            sldkv.i(sy.dma_start(out=B["ksb"][:, 1024:3072], in_=kvp[128:256, c0:c0 + 2048]), 16)
            sldkv.i(sy.dma_start(out=B["ksb"][:, 3072:4096], in_=kvp[0:128, c0 + 1024:c0 + 2048]), 16)
            v0 = 8192 + g * 2048
            sldkv.i(sy.dma_start(out=B["vsb"][:, 0:1024], in_=kvp[0:128, v0:v0 + 1024]), 16)
            sldkv.i(sy.dma_start(out=B["vsb"][:, 1024:3072], in_=kvp[128:256, v0:v0 + 2048]), 16)
            sldkv.i(sy.dma_start(out=B["vsb"][:, 3072:4096], in_=kvp[0:128, v0 + 1024:v0 + 2048]), 16)
        kv_ready = sldkv.n

        for hp in range(2):
            heads = (g * 4 + 2 * hp, g * 4 + 2 * hp + 1)
            q_ready = [None, None]
            for hi in (0, 1):
                q_free = sm.get(("_att_q_free", hi))

                @block.sync
                def _(sy, h=heads[hi], hi=hi, w=q_free):
                    if w is not None:
                        sy.wait_ge(spe.h, w)
                    sldq[hi].i(sy.dma_start(out=B["qsb2"][hi][:, :], in_=io["q_buf"][h]), 16)
                q_ready[hi] = sldq[hi].n

            for r in range(4):
                L = LHAT[r]
                qsl = slice(r * TN, (r + 1) * TN)
                avrs_free = [sm.get(("_att_avrs", hi)) for hi in (0, 1)]
                pend = None

                for kc in range(L):
                    sslot = kc % 2
                    psc_free = sm.get(("_att_psc", sslot))

                    @block.tensor
                    def _(te, kc=kc, sslot=sslot, qsl=qsl,
                          wk=kv_ready, wq=q_ready, wp=psc_free,
                          wav=(avrs_free if kc == 0 else None)):
                        te.wait_ge(sldq[0].h, wq[0])
                        te.wait_ge(sldq[1].h, wq[1])
                        if kc == 0:
                            te.wait_ge(sldkv.h, wk)
                        if wp is not None:
                            te.wait_ge(sact.h, wp)
                        if wav is not None:
                            for wv_ in wav:
                                if wv_ is not None:
                                    te.wait_ge(sdve.h, wv_)
                        for hi in (0, 1):
                            ins = te.matmul(out=B["psc"][sslot][:, hi * TN:(hi + 1) * TN],
                                            lhsT=B["ksb"][:, kc * 128:(kc + 1) * 128],
                                            rhs=B["qsb2"][hi][:, qsl],
                                            start=True, stop=True)
                        spe.i(ins)
                    sc_done = spe.n

                    pm_free = sm.get(("_att_pm", sslot))

                    @block.scalar
                    def _(sc, sslot=sslot, w=sc_done, wf=pm_free):
                        sc.wait_ge(spe.h, w)
                        if wf is not None:
                            sc.wait_ge(spe.h, wf)
                        sact.i(sc.activation(out=B["p_sb"][sslot][:, :],
                                             in_=B["psc"][sslot][:, :],
                                             func=AF.Exp, scale=SCALE))
                    sm[("_att_psc", sslot)] = sact.n
                    exp_done = sact.n

                    ready = (sact.h, exp_done)
                    if (r, kc) in MASK_IDX:
                        mi = MASK_IDX[(r, kc)]

                        @block.vector
                        def _(v, sslot=sslot, mi=mi, w=exp_done):
                            v.wait_ge(sact.h, w)
                            v.wait_ge(sldm.h, msk_ready)
                            v.wait_ge(sdve.h, ones_dve)
                            for hi in (0, 1):
                                ins = v.tensor_tensor(
                                    out=B["p_sb"][sslot][:, hi * TN:(hi + 1) * TN],
                                    in0=B["p_sb"][sslot][:, hi * TN:(hi + 1) * TN],
                                    in1=B["msk"][:, mi, :], op=OP.mult)
                            sdve.i(ins)
                        ready = (sdve.h, sdve.n)

                    if pend is not None:
                        pkc, pslot, pready = pend
                        for hi in (0, 1):
                            emit_rsav(hi, pkc, pslot, pkc == 0, False,
                                      pready if hi == 0 else None)
                        sm[("_att_pm", pslot)] = spe.n
                    pend = (kc, sslot, ready)

                pkc, pslot, pready = pend
                for hi in (0, 1):
                    emit_rsav(hi, pkc, pslot, pkc == 0, True,
                              pready if hi == 0 else None)
                sm[("_att_pm", pslot)] = spe.n
                av_done = spe.n

                @block.vector
                def _(v, w=av_done):
                    v.wait_ge(spe.h, w)
                    v.reciprocal(out=B["rr"][0][:, :], in_=B["prs"][0][0:1, :])
                    sdve.i(v.reciprocal(out=B["rr"][1][:, :], in_=B["prs"][1][0:1, :]))
                rec_done = sdve.n

                @block.tensor
                def _(te, w=rec_done):
                    te.wait_ge(sdve.h, w)
                    te.matmul(out=B["prs"][0][:, :], lhsT=B["onescf2"][:, :],
                              rhs=B["rr"][0][:, :], start=True, stop=True)
                    spe.i(te.matmul(out=B["prs"][1][:, :], lhsT=B["onescf2"][:, :],
                                    rhs=B["rr"][1][:, :], start=True, stop=True))
                bc_done = spe.n

                for hi in (0, 1):
                    oslot = hi
                    ost_free = sm.get(("_att_ost", oslot))

                    @block.vector
                    def _(v, hi=hi, oslot=oslot, w=bc_done, wo=ost_free):
                        v.wait_ge(spe.h, w)
                        if wo is not None:
                            v.wait_ge(ssto[oslot].h, wo)
                        sdve.i(v.tensor_copy(out=B["rn_sb"][hi][:, :], in_=B["prs"][hi][:, :]))
                        v.wait_ge(sdve.h, sdve.n)
                        sdve.i(v.tensor_tensor(out=B["osb"][oslot][:, :],
                                               in0=B["pav"][hi][:, :],
                                               in1=B["rn_sb"][hi][:, :], op=OP.mult))
                    sm[("_att_avrs", hi)] = sdve.n
                    ev_done = sdve.n

                    @block.gpsimd
                    def _(gp, h=heads[hi], r=r, oslot=oslot, w=ev_done):
                        gp.wait_ge(sdve.h, w)
                        ssto[oslot].i(gp.dma_start(out=io["attn_out"][h][:, r * TN:(r + 1) * TN],
                                                   in_=B["osb"][oslot][:, :]), 16)
                    sm[("_att_ost", oslot)] = ssto[oslot].n
            for hi in (0, 1):
                sm[("_att_q_free", hi)] = spe.n
        sm["_att_kv_free"] = spe.n

def attn_bufs(nc, stack, layer, pfx=""):
    B = {}
    def sb(name, shape, dt):
        B[name] = stack.enter_context(nc.sbuf_tensor(pfx + name, shape, dt))
    sb("ksb", [128, SEQ], BF16)
    sb("vsb", [128, SEQ], BF16)
    B["qsb2"] = [stack.enter_context(nc.sbuf_tensor(pfx + f"qsb2{i}", [128, TLOC], BF16))
                 for i in range(2)]
    sb("msk", [128, N_MSK, TN], BF16)
    B["p_sb"] = [stack.enter_context(nc.sbuf_tensor(pfx + f"p_sb{i}", [128, 2 * TN], BF16))
                 for i in range(2)]
    sb("onesb2", [128, 1], BF16)
    sb("onescf2", [1, 128], F32)
    B["rr"] = [stack.enter_context(nc.sbuf_tensor(pfx + f"rr{i}", [1, TN], F32))
               for i in range(2)]
    B["rn_sb"] = [stack.enter_context(nc.sbuf_tensor(pfx + f"rn_sb{i}", [128, TN], F32))
                  for i in range(2)]
    odt = F32 if layer == 1 else BF16
    B["osb"] = [stack.enter_context(nc.sbuf_tensor(pfx + f"osb{i}", [128, TN], odt))
                for i in range(2)]
    B["psc"] = [stack.enter_context(nc.psum_tensor(pfx + f"psc{i}", [128, 2 * TN], F32))
                for i in range(2)]
    B["pav"] = [stack.enter_context(nc.psum_tensor(pfx + f"pav{i}", [128, TN], F32))
                for i in range(2)]
    B["prs"] = [stack.enter_context(nc.psum_tensor(pfx + f"prs{i}", [128, TN], F32))
                for i in range(2)]
    return B


def build_attn_launch(layer):
    nc = bass.Bass("TRN2", target_bir_lowering=False, debug=False)
    odt = F32 if layer == 1 else BF16
    io = {
        "q_buf": nc.dram_tensor("q_buf", [NH, 128, TLOC], BF16, kind="ExternalInput").ap(),
        "kv_pair": nc.dram_tensor("kv_pair", [256, 16384], BF16, kind="ExternalInput").ap(),
        "amask": nc.dram_tensor("amask", [N_MSK, 128, TN], BF16, kind="ExternalInput").ap(),
        "attn_out": nc.dram_tensor("attn_out", [NH, 128, TLOC], odt, kind="ExternalOutput").ap(),
    }
    with contextlib.ExitStack() as stack:
        B = attn_bufs(nc, stack, layer)
        sm = make_sems(nc, stack)
        blk = stack.enter_context(nc.Block())
        emit_attn(nc, blk, sm, B, io, layer)
        finalize_store_wait(blk, sm)
    return nc




def emit_wo(nc, block, sm, B, io):
    slda = sm.stream("ld_woa")
    sldw = [sm.stream("ld_wow0"), sm.stream("ld_wow1")]
    ssto = [sm.stream("st_wo0"), sm.stream("st_wo1")]
    spe, sdve = sm["pe"], sm["dve"]

    @block.sync
    def _(sy):
        ar = io["attn_out"].rearrange("h p t -> p h t")
        slda.i(sy.dma_start(out=B["asb"][:, :, :], in_=ar), 16)
    a_ready = slda.n

    for oc in range(FC):
        slot = oc % 2
        w_free = sm.get(("_wo_wfree", slot))

        @block.gpsimd
        def _(gp, oc=oc, slot=slot, w=w_free):
            if w is not None:
                gp.wait_ge(spe.h, w)
            ap = io["wo"].rearrange("(ac p) o -> p ac o", p=128)[:, :, oc * 128:(oc + 1) * 128]
            sldw[slot].i(gp.dma_start(out=B["wosb"][slot][:, :, :], in_=ap), 16)
        w_ready = sldw[slot].n

        for t in range(TT):
            ts = slice(t * TN, (t + 1) * TN)
            pslot = t % 2
            p_free = sm.get(("_wo_pfree", pslot))

            @block.tensor
            def _(te, slot=slot, pslot=pslot, ts=ts, w=w_ready, pf=p_free, wa=a_ready):
                te.wait_ge(slda.h, wa)
                te.wait_ge(sldw[slot].h, w)
                if pf is not None:
                    te.wait_ge(sdve.h, pf)
                for ac in range(FC):
                    ins = te.matmul(out=B["pw"][pslot][:, :],
                                    lhsT=B["wosb"][slot][:, ac, :],
                                    rhs=B["asb"][:, ac, ts],
                                    start=(ac == 0), stop=(ac == FC - 1))
                spe.i(ins)
            mm_done = spe.n

            oslot = oc % 2
            st_free = sm.get(("_wo_sfree", oslot)) if t == 0 else None

            @block.vector
            def _(v, pslot=pslot, oslot=oslot, ts=ts, w=mm_done, sf=st_free):
                v.wait_ge(spe.h, w)
                if sf is not None:
                    v.wait_ge(ssto[oslot].h, sf)
                sdve.i(v.tensor_copy(out=B["wout"][oslot][:, ts], in_=B["pw"][pslot][:, :]))
            sm[("_wo_pfree", pslot)] = sdve.n
        ev_done = sdve.n

        @block.gpsimd
        def _(gp, oc=oc, w=ev_done):
            gp.wait_ge(sdve.h, w)
            ssto[oc % 2].i(gp.dma_start(out=io["wo_out"][oc * 128:(oc + 1) * 128, :],
                                        in_=B["wout"][oc % 2][:, :]), 16)
        sm[("_wo_sfree", oc % 2)] = ssto[oc % 2].n
        sm[("_wo_wfree", oc % 2)] = spe.n


def wo_bufs(nc, stack, pfx=""):
    B = {}
    B["asb"] = stack.enter_context(nc.sbuf_tensor(pfx + "asb", [128, FC, TLOC], BF16))
    B["wosb"] = [stack.enter_context(nc.sbuf_tensor(pfx + f"wosb{i}", [128, FC, 128], BF16))
                 for i in range(2)]
    B["wout"] = [stack.enter_context(nc.sbuf_tensor(pfx + f"wout{i}", [128, TLOC], F32))
                 for i in range(2)]
    B["pw"] = [stack.enter_context(nc.psum_tensor(pfx + f"pw{i}", [128, TN], F32))
               for i in range(2)]
    return B


def build_wo_launch():
    nc = bass.Bass("TRN2", target_bir_lowering=False, debug=False)
    io = {
        "attn_out": nc.dram_tensor("attn_out", [NH, 128, TLOC], BF16, kind="ExternalInput").ap(),
        "wo": nc.dram_tensor("wo", [DM, DM], F32, kind="ExternalInput").ap(),
        "wo_out": nc.dram_tensor("wo_out", [DM, TLOC], F32, kind="ExternalOutput").ap(),
    }
    with contextlib.ExitStack() as stack:
        B = wo_bufs(nc, stack)
        sm = make_sems(nc, stack)
        blk = stack.enter_context(nc.Block())
        emit_wo(nc, blk, sm, B, io)
        finalize_store_wait(blk, sm)
    return nc


def emit_ffn0_a(nc, block, sm, B, io):
    sldx = sm.stream("ld_f0x")
    sldw = [sm.stream("ld_f0w0"), sm.stream("ld_f0w1")]
    ssth = [sm.stream("st_f0h0"), sm.stream("st_f0h1")]
    spe, sact, sdve = sm["pe"], sm["act"], sm["dve"]
    xnr = io["xn"].rearrange("(fc p) t -> p fc t", p=128)
    hr = io["h"].rearrange("(ic p) t -> p ic t", p=128)

    @block.sync
    def _(sy):
        sldx.i(sy.dma_start(out=B["xn_sb"][:, :, :], in_=xnr), 16)
        sldx.i(sy.dma_start(out=B["sxn_sb"][:, :], in_=io["sxn"]), 16)
        sldx.i(sy.dma_start(out=B["sg_sb"][:, :], in_=io["sg"]), 16)
        sldx.i(sy.dma_start(out=B["su_sb"][:, :], in_=io["su"]), 16)
    x_ready = sldx.n

    @block.vector
    def _(v):
        v.memset(B["neg128"][:, :], -128.0)
        sdve.i(v.memset(B["onesb3"][:, :], 1.0))
    const_dve = sdve.n

    for ib in range(16):
        slot = ib % 2
        w_free = sm.get(("_f0a_wfree", slot))

        @block.gpsimd
        def _(gp, ib=ib, slot=slot, w=w_free):
            if w is not None:
                gp.wait_ge(spe.h, w)
            isl = slice(ib * 512, (ib + 1) * 512)
            gw = io["gate_wT"].rearrange("(fc p) i -> p fc i", p=128)[:, :, isl]
            uw = io["up_wT"].rearrange("(fc p) i -> p fc i", p=128)[:, :, isl]
            sldw[slot].i(gp.dma_start(out=B["gwsb"][slot][:, :, :], in_=gw), 16)
            sldw[slot].i(gp.dma_start(out=B["uwsb"][slot][:, :, :], in_=uw), 16)
        w_ready = sldw[slot].n

        for ic4 in range(4):
            i = ib * 4 + ic4
            hslot = i % 2
            h_free = sm.get(("_f0a_hfree", hslot))
            for t in range(TT):
                ts = slice(t * TN, (t + 1) * TN)
                pp = t % 2
                pg, pu = B["pg"][pp], B["pu"][pp]
                p_free = sm.get(("_f0a_pfree", pp))

                @block.tensor
                def _(te, ib=ib, ic4=ic4, t=t, ts=ts, slot=slot, pg=pg, pu=pu,
                      w=w_ready, xw=x_ready, pf=p_free, cd=const_dve):
                    te.wait_ge(sldx.h, xw)
                    te.wait_ge(sldw[slot].h, w)
                    te.wait_ge(sdve.h, cd)
                    if pf is not None:
                        te.wait_ge(sdve.h, pf)
                    csl = slice(ic4 * 128, (ic4 + 1) * 128)
                    for fc in range(FC):
                        te.matmul(out=pg[:, :], lhsT=B["gwsb"][slot][:, fc, csl],
                                  rhs=B["xn_sb"][:, fc, ts], start=(fc == 0), stop=False)
                    te.matmul(out=pg[:, :], lhsT=B["neg128"][:, :],
                              rhs=B["sxn_sb"][0:1, ts], start=False, stop=True)
                    for fc in range(FC):
                        te.matmul(out=pu[:, :], lhsT=B["uwsb"][slot][:, fc, csl],
                                  rhs=B["xn_sb"][:, fc, ts], start=(fc == 0), stop=False)
                    ins = te.matmul(out=pu[:, :], lhsT=B["neg128"][:, :],
                                    rhs=B["sxn_sb"][0:1, ts], start=False, stop=True)
                    spe.i(ins)
                mm_done = spe.n

                aslot = t % 2

                @block.scalar
                def _(sc, i=i, pg=pg, aslot=aslot, w=mm_done):
                    sc.wait_ge(spe.h, w)
                    af = sm.get(("_f0a_afree", aslot))
                    if af is not None:
                        sc.wait_ge(sdve.h, af)
                    sact.i(sc.activation(out=B["silu_sb"][aslot][:, :], in_=pg[:, :],
                                         func=AF.Silu, scale=B["sg_sb"][:, i:i + 1]))
                silu_done = sact.n

                @block.vector
                def _(v, i=i, t=t, ts=ts, pu=pu, aslot=aslot, hslot=hslot,
                      w=silu_done, hf=(h_free if t == 0 else None)):
                    v.wait_ge(sact.h, w)
                    if hf is not None:
                        v.wait_ge(ssth[hslot].h, hf[0])
                        v.wait_ge(spe.h, hf[1])
                    sdve.i(v.scalar_tensor_tensor(out=B["hsb"][hslot][:, ts], in0=pu[:, :],
                                                  scalar=B["su_sb"][:, i:i + 1],
                                                  in1=B["silu_sb"][aslot][:, :],
                                                  op0=OP.mult, op1=OP.mult))
                sm[("_f0a_afree", aslot)] = sdve.n
                sm[("_f0a_pfree", pp)] = sdve.n
                h_done = sdve.n

                pend = sm.get("_f0a_hrs_pend")
                if pend is not None:
                    pi, pt, phslot, pw = pend

                    @block.tensor
                    def _(te, pi=pi, pt=pt, phslot=phslot, w=pw):
                        te.wait_ge(sdve.h, w)
                        ins = te.matmul(out=B["phrs"][pt][:, :], lhsT=B["onesb3"][:, :],
                                        rhs=B["hsb"][phslot][:, pt * TN:(pt + 1) * TN],
                                        start=(pi == 0), stop=(pi == 63))
                        spe.i(ins)
                sm["_f0a_hrs_pend"] = (i, t, hslot, h_done)
            pi, pt, phslot, pw = sm.pop("_f0a_hrs_pend")

            @block.tensor
            def _(te, pi=pi, pt=pt, phslot=phslot, w=pw):
                te.wait_ge(sdve.h, w)
                spe.i(te.matmul(out=B["phrs"][pt][:, :], lhsT=B["onesb3"][:, :],
                                rhs=B["hsb"][phslot][:, pt * TN:(pt + 1) * TN],
                                start=(pi == 0), stop=(pi == 63)))
            hrs_done = spe.n

            @block.gpsimd
            def _(gp, i=i, hslot=hslot, w=hrs_done):
                gp.wait_ge(spe.h, w)
                ssth[hslot].i(gp.dma_start(out=hr[:, i, :], in_=B["hsb"][hslot][:, :]), 16)
            sm[("_f0a_hfree", hslot)] = (ssth[hslot].n, spe.n)
        sm[("_f0a_wfree", slot)] = spe.n

    @block.vector
    def _(v):
        v.wait_ge(spe.h, spe.n)
        for t in range(TT):
            ins = v.tensor_copy(out=B["hrs_sb"][0:1, t, :], in_=B["phrs"][t][0:1, :])
        sdve.i(ins)
    sm["_f0_hrs_ready"] = sdve.n


def emit_ffn0_b(nc, block, sm, B, io):
    sldw = [sm.stream("ld_f0dw0"), sm.stream("ld_f0dw1")]
    sldh = [sm.stream("ld_f0h0"), sm.stream("ld_f0h1")]
    sldx = [sm.stream("ld_f0r0"), sm.stream("ld_f0r1")]
    sstx = [sm.stream("st_f0x0"), sm.stream("st_f0x1")]
    sldc = sm.stream("ld_f0c")
    spe, sact, sdve = sm["pe"], sm["act"], sm["dve"]
    hr = io["h"].rearrange("(ic p) t -> p ic t", p=128)
    dwr = io["down_wT"].rearrange("(ic p) o -> p ic o", p=128)
    x0r = io["x0"].rearrange("(oc p) t -> p oc t", p=128)
    wor = io["wo_out"].rearrange("(oc p) t -> p oc t", p=128)
    x1r = io["x1"].rearrange("(oc p) t -> p oc t", p=128)
    hrs_ready = sm["_f0_hrs_ready"]

    @block.sync
    def _(sy):
        sldc.i(sy.dma_start(out=B["sd_sb"][:, :], in_=io["sd"]), 16)
    sd_ready = sldc.n

    for mt in range(4):
        osl = slice(mt * 512, (mt + 1) * 512)
        dw_ready = [None, None]
        for half in range(2):
            dw_free = sm.get(("_f0b_dwfree", half))

            @block.gpsimd
            def _(gp, mt=mt, half=half, osl=osl, w=dw_free):
                if mt == 0:
                    gp.wait_ge(sdve.h, hrs_ready)
                if w is not None:
                    gp.wait_ge(spe.h, w)
                sldw[half].i(gp.dma_start(out=B["dwsb"][half][:, :, :],
                                          in_=dwr[:, half * 32:(half + 1) * 32, osl]), 16)
            dw_ready[half] = sldw[half].n

        for t in range(TT):
            ts = slice(t * TN, (t + 1) * TN)
            ht_ready = [None, None]
            for half in range(2):
                ht_free = sm.get(("_f0b_htfree", half))

                @block.sync
                def _(sy, t=t, ts=ts, half=half, mt=mt, w=ht_free):
                    if mt == 0 and t == 0:
                        sy.wait_ge(sdve.h, hrs_ready)
                    if w is not None:
                        sy.wait_ge(spe.h, w)
                    sldh[half].i(sy.dma_start(out=B["htsb"][half][:, :, :],
                                              in_=hr[:, half * 32:(half + 1) * 32, ts]), 16)
                ht_ready[half] = sldh[half].n

            for oc4 in range(4):
                o = mt * 4 + oc4
                pp = oc4 % 2
                p_free = sm.get(("_f0b_pfree", pp))
                rslot = oc4 % 2
                r_free = sm.get(("_f0b_rfree", rslot))

                @block.sync
                def _(sy, o=o, ts=ts, rslot=rslot, w=r_free,
                      first=(mt == 0 and t == 0 and oc4 < 2)):
                    if first:
                        sy.wait_ge(sdve.h, hrs_ready)
                    if w is not None:
                        sy.wait_ge(sdve.h, w)
                    sldx[rslot].i(sy.dma_start(out=B["x0t"][rslot][:, :], in_=x0r[:, o, ts]), 16)
                    sldx[rslot].i(sy.dma_start(out=B["wot"][rslot][:, :], in_=wor[:, o, ts]), 16)
                r_ready = sldx[rslot].n

                @block.tensor
                def _(te, o=o, t=t, pp=pp, wd=dw_ready, wh=ht_ready, pf=p_free,
                      whr=hrs_ready):
                    te.wait_ge(sdve.h, whr)
                    if pf is not None:
                        te.wait_ge(sdve.h, pf)
                    csl = slice((o % 4) * 128, (o % 4 + 1) * 128)
                    for half in range(2):
                        te.wait_ge(sldw[half].h, wd[half])
                        te.wait_ge(sldh[half].h, wh[half])
                        for ic in range(32):
                            te.matmul(out=B["pd"][pp][:, :],
                                      lhsT=B["dwsb"][half][:, ic, csl],
                                      rhs=B["htsb"][half][:, ic, :],
                                      start=(half == 0 and ic == 0), stop=False)
                    ins = te.matmul(out=B["pd"][pp][:, :], lhsT=B["neg128"][:, :],
                                    rhs=B["hrs_sb"][0:1, t, :], start=False, stop=True)
                    spe.i(ins)
                mm_done = spe.n

                xslot = (mt * 16 + t * 4 + oc4) % 2
                xs_free = sm.get(("_f0b_xsfree", xslot))

                @block.vector
                def _(v, o=o, pp=pp, rslot=rslot, xslot=xslot, w=mm_done,
                      wr=r_ready, xf=xs_free, wsd=sd_ready):
                    v.wait_ge(spe.h, w)
                    v.wait_ge(sldx[rslot].h, wr)
                    v.wait_ge(sldc.h, wsd)
                    if xf is not None:
                        v.wait_ge(sstx[xslot].h, xf)
                    sdve.i(v.scalar_tensor_tensor(out=B["tmpb"][:, :], in0=B["pd"][pp][:, :],
                                                  scalar=B["sd_sb"][:, o:o + 1],
                                                  in1=B["wot"][rslot][:, :],
                                                  op0=OP.mult, op1=OP.add))
                    v.wait_ge(sdve.h, sdve.n)
                    sdve.i(v.tensor_tensor(out=B["x1t"][xslot][:, :], in0=B["tmpb"][:, :],
                                           in1=B["x0t"][rslot][:, :], op=OP.add))
                sm[("_f0b_pfree", pp)] = sdve.n
                sm[("_f0b_rfree", rslot)] = sdve.n
                ev_done = sdve.n

                @block.gpsimd
                def _(gp, o=o, ts=ts, xslot=xslot, w=ev_done):
                    gp.wait_ge(sdve.h, w)
                    sstx[xslot].i(gp.dma_start(out=x1r[:, o, ts], in_=B["x1t"][xslot][:, :]), 16)
                sm[("_f0b_xsfree", xslot)] = sstx[xslot].n
            for half in range(2):
                sm[("_f0b_htfree", half)] = spe.n
        for half in range(2):
            sm[("_f0b_dwfree", half)] = spe.n

def ffn0_small_bufs(nc, stack, pfx=""):
    B = {}
    B["neg128"] = stack.enter_context(nc.sbuf_tensor(pfx + "neg128", [1, 128], BF16))
    B["onesb3"] = stack.enter_context(nc.sbuf_tensor(pfx + "onesb3", [128, 1], BF16))
    B["hrs_sb"] = stack.enter_context(nc.sbuf_tensor(pfx + "hrs_sb", [1, TT, TN], BF16))
    B["sxn_sb"] = stack.enter_context(nc.sbuf_tensor(pfx + "sxn_sb", [1, TLOC], BF16))
    B["sg_sb"] = stack.enter_context(nc.sbuf_tensor(pfx + "sg_sb", [128, 64], F32))
    B["su_sb"] = stack.enter_context(nc.sbuf_tensor(pfx + "su_sb", [128, 64], F32))
    B["sd_sb"] = stack.enter_context(nc.sbuf_tensor(pfx + "sd_sb", [128, 16], F32))
    return B


def ffn0_a_bufs(nc, sa, B, pfx=""):
    B["xn_sb"] = sa.enter_context(nc.sbuf_tensor(pfx + "xn_sb", [128, FC, TLOC], BF16))
    B["gwsb"] = [sa.enter_context(nc.sbuf_tensor(pfx + f"gwsb{i}", [128, FC, 512], BF16))
                 for i in range(2)]
    B["uwsb"] = [sa.enter_context(nc.sbuf_tensor(pfx + f"uwsb{i}", [128, FC, 512], BF16))
                 for i in range(2)]
    B["hsb"] = [sa.enter_context(nc.sbuf_tensor(pfx + f"hsb{i}", [128, TLOC], BF16))
                for i in range(2)]
    B["silu_sb"] = [sa.enter_context(nc.sbuf_tensor(pfx + f"silu{i}", [128, TN], F32))
                    for i in range(2)]
    B["pg"] = [sa.enter_context(nc.psum_tensor(pfx + f"pg{i}", [128, TN], F32))
               for i in range(2)]
    B["pu"] = [sa.enter_context(nc.psum_tensor(pfx + f"pu{i}", [128, TN], F32))
               for i in range(2)]
    B["phrs"] = [sa.enter_context(nc.psum_tensor(pfx + f"phrs{i}", [1, TN], F32))
                 for i in range(TT)]
    return B


def ffn0_b_bufs(nc, sb_, B, pfx=""):
    B["dwsb"] = [sb_.enter_context(nc.sbuf_tensor(pfx + f"dwsb{i}", [128, 32, 512], BF16))
                 for i in range(2)]
    B["htsb"] = [sb_.enter_context(nc.sbuf_tensor(pfx + f"htsb{i}", [128, 32, TN], BF16))
                 for i in range(2)]
    B["x0t"] = [sb_.enter_context(nc.sbuf_tensor(pfx + f"x0t{i}", [128, TN], F32))
                for i in range(2)]
    B["wot"] = [sb_.enter_context(nc.sbuf_tensor(pfx + f"wot{i}", [128, TN], F32))
                for i in range(2)]
    B["x1t"] = [sb_.enter_context(nc.sbuf_tensor(pfx + f"x1t{i}", [128, TN], F32))
                for i in range(2)]
    B["tmpb"] = sb_.enter_context(nc.sbuf_tensor(pfx + "tmpb", [128, TN], F32))
    B["pd"] = [sb_.enter_context(nc.psum_tensor(pfx + f"pd{i}", [128, TN], F32))
               for i in range(2)]
    return B


def moe_router_scoped(nc, rstack, B, pfx=""):
    B["pl_t"] = rstack.enter_context(nc.psum_tensor(pfx + "pl", [128, TN], F32))
    B["pw8_t"] = rstack.enter_context(nc.psum_tensor(pfx + "pw8", [128, TLOC], F32))
    return B


def moe_expert_psums(nc, estack, B, pfx=""):
    B["pwb"] = estack.enter_context(nc.psum_tensor(pfx + "pwb", [128, TN], F32))
    B["pg2"] = [estack.enter_context(nc.psum_tensor(pfx + f"pg2{i}", [128, TN], F32))
                for i in range(2)]
    B["pu2"] = [estack.enter_context(nc.psum_tensor(pfx + f"pu2{i}", [128, TN], F32))
                for i in range(2)]
    B["pd2"] = [estack.enter_context(nc.psum_tensor(pfx + f"pd2{i}", [128, TN], F32))
                for i in range(3)]
    return B


def build_ffn0_launch():
    nc = bass.Bass("TRN2", target_bir_lowering=False, debug=False)
    io = {
        "xn": nc.dram_tensor("xn", [DM, TLOC], BF16, kind="ExternalInput").ap(),
        "sxn": nc.dram_tensor("sxn", [1, TLOC], BF16, kind="ExternalInput").ap(),
        "gate_wT": nc.dram_tensor("gate_wT", [DM, I0], I32, kind="ExternalInput").ap(),
        "up_wT": nc.dram_tensor("up_wT", [DM, I0], I32, kind="ExternalInput").ap(),
        "down_wT": nc.dram_tensor("down_wT", [I0, DM], I32, kind="ExternalInput").ap(),
        "sg": nc.dram_tensor("sg", [128, 64], F32, kind="ExternalInput").ap(),
        "su": nc.dram_tensor("su", [128, 64], F32, kind="ExternalInput").ap(),
        "sd": nc.dram_tensor("sd", [128, 16], F32, kind="ExternalInput").ap(),
        "x0": nc.dram_tensor("x0", [DM, TLOC], F32, kind="ExternalInput").ap(),
        "wo_out": nc.dram_tensor("wo_out", [DM, TLOC], F32, kind="ExternalInput").ap(),
        "x1": nc.dram_tensor("x1", [DM, TLOC], F32, kind="ExternalOutput").ap(),
        "h": nc.dram_tensor("h", [I0, TLOC], BF16, kind="Internal").ap(),
    }
    with contextlib.ExitStack() as stack:
        B = ffn0_small_bufs(nc, stack, "")
        sm = make_sems(nc, stack)
        blk = stack.enter_context(nc.Block())
        with contextlib.ExitStack() as sa:
            ffn0_a_bufs(nc, sa, B, "")
            emit_ffn0_a(nc, blk, sm, B, io)
        with contextlib.ExitStack() as sb_:
            ffn0_b_bufs(nc, sb_, B, "")
            emit_ffn0_b(nc, blk, sm, B, io)
            finalize_store_wait(blk, sm)
    return nc
def emit_router(nc, block, sm, B, io):
    sldx = sm.stream("ld_rtx")
    sldr = sm.stream("ld_rtw")
    spe, sact, sdve = sm["pe"], sm["act"], sm["dve"]
    xnr = io["xn"].rearrange("(fc p) t -> p fc t", p=128)

    @block.gpsimd
    def _(gp):
        rr = io["routerT"].rearrange("(fc p) e -> p fc e", p=128)
        sldr.i(gp.dma_start(out=B["rt_sb"][:, :, :], in_=rr), 16)
        sldr.i(gp.dma_start(out=B["eyef_sb"][:, :], in_=io["eyef"]), 16)
        sldr.i(gp.dma_start(out=B["sel_sb"][:, :], in_=io["sel8"]), 16)
    r_ready = sldr.n

    for tc in range(16):
        xslot = tc % 2
        x_free = sm.get(("_rt_xfree", xslot))

        @block.sync
        def _(sy, tc=tc, xslot=xslot, w=x_free):
            if w is not None:
                sy.wait_ge(spe.h, w)
            sldx.i(sy.dma_start(out=B["gslice"][xslot][:, :, :],
                                in_=xnr[:, :, tc * 128:(tc + 1) * 128]), 16)
        x_ready = sldx.n

        pl_free = sm.get("_rt_plfree")

        @block.tensor
        def _(te, xslot=xslot, w=x_ready, wr=r_ready, pf=pl_free):
            te.wait_ge(sldx.h, w)
            te.wait_ge(sldr.h, wr)
            if pf is not None:
                te.wait_ge(sdve.h, pf)
            for fc in range(FC):
                ins = te.matmul(out=B["pl_t"][:, 0:NE], lhsT=B["gslice"][xslot][:, fc, :],
                                rhs=B["rt_sb"][:, fc, :], start=(fc == 0), stop=(fc == FC - 1))
            spe.i(ins)
        sm[("_rt_xfree", xslot)] = spe.n
        lg_done = spe.n

        @block.vector
        def _(v, w=lg_done):
            v.wait_ge(spe.h, w)
            sdve.i(v.tensor_reduce(out=B["m1"][:, :], in_=B["pl_t"][:, 0:NE], axis=AX.X, op=OP.max))
            v.wait_ge(sdve.h, sdve.n)
            sdve.i(v.tensor_scalar(out=B["eq1"][:, :], in0=B["pl_t"][:, 0:NE],
                                   scalar1=B["m1"][:, 0:1], scalar2=None, op0=OP.is_equal))
            v.wait_ge(sdve.h, sdve.n)
            sdve.i(v.scalar_tensor_tensor(out=B["l2"][:, :], in0=B["eq1"][:, :], scalar=-1e30,
                                          in1=B["pl_t"][:, 0:NE], op0=OP.mult, op1=OP.add))
            v.wait_ge(sdve.h, sdve.n)
            sdve.i(v.tensor_reduce(out=B["m2"][:, :], in_=B["l2"][:, :], axis=AX.X, op=OP.max))
            v.wait_ge(sdve.h, sdve.n)
            sdve.i(v.tensor_tensor(out=B["dd"][:, :], in0=B["m1"][:, :],
                                   in1=B["m2"][:, :], op=OP.subtract))
        sm["_rt_plfree"] = sdve.n
        mx_done = sdve.n

        @block.scalar
        def _(sc, w=mx_done):
            sc.wait_ge(sdve.h, w)
            sact.i(sc.activation(out=B["s1"][:, :], in_=B["dd"][:, :], func=AF.Sigmoid))
        s1_done = sact.n

        @block.vector
        def _(v, tc=tc, w=s1_done):
            v.wait_ge(sact.h, w)
            v.tensor_scalar(out=B["s2"][:, :], in0=B["s1"][:, :], scalar1=-1.0,
                            scalar2=1.0, op0=OP.mult, op1=OP.add)
            v.tensor_scalar(out=B["eq2"][:, :], in0=B["l2"][:, :],
                            scalar1=B["m2"][:, 0:1], scalar2=None, op0=OP.is_equal)
            sdve.i(v.tensor_scalar(out=B["w1"][:, :], in0=B["eq1"][:, :],
                                   scalar1=B["s1"][:, 0:1], scalar2=None, op0=OP.mult))
            v.wait_ge(sdve.h, sdve.n)
            sdve.i(v.scalar_tensor_tensor(out=B["wsb"][:, tc, :], in0=B["eq2"][:, :],
                                          scalar=B["s2"][:, 0:1], in1=B["w1"][:, :],
                                          op0=OP.mult, op1=OP.add))
        w_done = sdve.n

    all_w = sdve.n

    @block.tensor
    def _(te, w=all_w):
        te.wait_ge(sdve.h, w)
        for tc in range(16):
            ins = te.matmul(out=B["pw8_t"][0:NE, tc * 128:(tc + 1) * 128],
                            lhsT=B["wsb"][:, tc, :],
                            rhs=B["eyef_sb"][:, :], start=True, stop=True)
        spe.i(ins)
    tr_done = spe.n

    @block.vector
    def _(v, w=tr_done):
        v.wait_ge(spe.h, w)
        sdve.i(v.tensor_copy(out=B["w8sb"][0:8, :], in_=B["pw8_t"][0:NE, 0:TLOC]))
    sm["_rt_done"] = (sdve.n, spe.n)



def emit_moe_prepass(nc, block, sm, B, io):
    stmo = [sm.stream(f"st_mo{t}") for t in range(TT)]
    sdve = sm["dve"]
    a1r = io["attn1"].rearrange("h p t -> (h p) t")
    rt_done = sm["_rt_done"]

    @block.gpsimd
    def _(gp):
        gp.wait_ge(sdve.h, rt_done[0])
        for t in range(TT):
            ts = slice(t * TN, (t + 1) * TN)
            stmo[t].i(gp.dma_start(out=io["out"][:, ts], in_=io["x1"][:, ts]), 16)
        for t in range(TT):
            ts = slice(t * TN, (t + 1) * TN)
            gp.wait_ge(stmo[t].h, stmo[t].n)
            stmo[t].i(gp.dma_start(out=io["out"][:, ts], in_=a1r[:, ts],
                                   accum_op=OP.add), 16)
            sm[("_me_posb", t)] = (stmo[t].n, t)
    sm["_me_prepass_done"] = sdve.n


def emit_moe(nc, block, sm, B, io):
    sldx = sm.stream("ld_mex")
    sldg = [sm.stream("ld_meg0"), sm.stream("ld_meg1")]
    sldd2 = [sm.stream("ld_med0"), sm.stream("ld_med1")]
    stmo = [sm.stream(f"st_mo{t}") for t in range(TT)]
    spe, sact, sdve = sm["pe"], sm["act"], sm["dve"]
    xnr = io["xn"].rearrange("(fc p) t -> p fc t", p=128)
    outr = io["out"].rearrange("(oc p) t -> p oc t", p=128)
    rt_done = sm["_rt_done"]
    prepass_done = sm["_me_prepass_done"]
    prepass_keys = [(t, sm[("_me_posb", t)][0]) for t in range(TT)
                    if ("_me_posb", t) in sm]

    @block.sync
    def _(sy):
        sldx.i(sy.dma_start(out=B["xn_sb2"][:, :, :], in_=xnr), 16)
    xn_ready = sldx.n

    def emit_gu_load(e, ic):
        wslot = ic % 2
        w_free = sm.get(("_me_wfree", wslot))

        @block.gpsimd
        def _(gp, e=e, ic=ic, wslot=wslot, w=w_free, first=(e == 0 and ic < 2)):
            if first:
                gp.wait_ge(sdve.h, max(rt_done[0], prepass_done))
            if w is not None:
                gp.wait_ge(spe.h, w)
            gu = io["egu"][e].rearrange("(fc p) o -> p fc o", p=128)
            sldg[wslot].i(gp.dma_start(
                out=B["gslice"][wslot][:, :, :],
                in_=gu[:, :, ic * 128:(ic + 1) * 128]), 16)
            sldg[wslot].i(gp.dma_start(
                out=B["uslice"][wslot][:, :, :],
                in_=gu[:, :, 1024 + ic * 128:1024 + (ic + 1) * 128]), 16)
        sm[("_me_wready", wslot)] = sldg[wslot].n

    def emit_gu_group(e, ic, t):
        hslot = e % 2
        wslot = ic % 2
        w_ready = sm[("_me_wready", wslot)]
        ts = slice(t * TN, (t + 1) * TN)
        pp = (ic * TT + t) % 2
        p_free = sm.get(("_me_pgu", pp))
        h_free = sm.get(("_me_hfree", hslot)) if (ic == 0 and t == 0) else None

        @block.tensor
        def _(te, ts=ts, pp=pp, wslot=wslot, w=w_ready, xw=xn_ready,
              pf=p_free, hf=h_free):
            te.wait_ge(sldx.h, xw)
            te.wait_ge(sldg[wslot].h, w)
            if pf is not None:
                te.wait_ge(sdve.h, pf)
            if hf is not None:
                te.wait_ge(spe.h, hf)
            for fc in range(FC):
                te.matmul(out=B["pg2"][pp][:, :], lhsT=B["gslice"][wslot][:, fc, :],
                          rhs=B["xn_sb2"][:, fc, ts], start=(fc == 0), stop=(fc == FC - 1))
            for fc in range(FC):
                ins = te.matmul(out=B["pu2"][pp][:, :], lhsT=B["uslice"][wslot][:, fc, :],
                                rhs=B["xn_sb2"][:, fc, ts], start=(fc == 0), stop=(fc == FC - 1))
            spe.i(ins)
        mm_done = spe.n
        if t == TT - 1:
            sm[("_me_wfree", wslot)] = spe.n

        aslot = (ic * TT + t) % 2

        @block.scalar
        def _(sc, pp=pp, aslot=aslot, w=mm_done):
            sc.wait_ge(spe.h, w)
            af = sm.get(("_me_afree", aslot))
            if af is not None:
                sc.wait_ge(sdve.h, af)
            sact.i(sc.activation(out=B["silu2"][aslot][:, :], in_=B["pg2"][pp][:, :],
                                 func=AF.Silu))
        silu_done = sact.n

        @block.vector
        def _(v, ic=ic, ts=ts, pp=pp, aslot=aslot, hslot=hslot, w=silu_done):
            v.wait_ge(sact.h, w)
            sdve.i(v.scalar_tensor_tensor(out=B["h_et"][hslot][:, ic, ts],
                                          in0=B["pu2"][pp][:, :], scalar=1.0,
                                          in1=B["silu2"][aslot][:, :],
                                          op0=OP.mult, op1=OP.mult))
        sm[("_me_afree", aslot)] = sdve.n
        sm[("_me_pgu", pp)] = sdve.n

    def emit_wbc(e):
        for t in range(TT):
            ts = slice(t * TN, (t + 1) * TN)
            pwb_free = sm.get("_me_pwbfree")

            @block.tensor
            def _(te, e=e, ts=ts, w=rt_done, pf=pwb_free):
                te.wait_ge(sdve.h, w[0])
                if pf is not None:
                    te.wait_ge(sdve.h, pf)
                spe.i(te.matmul(out=B["pwb"][:, :], lhsT=B["sel_sb"][:, e * 128:(e + 1) * 128],
                                rhs=B["w8sb"][0:8, ts], start=True, stop=True))
            wbmm = spe.n

            @block.vector
            def _(v, t=t, w=wbmm, wp=sm.get("_me_wbfree")):
                v.wait_ge(spe.h, w)
                if wp is not None:
                    v.wait_ge(sdve.h, wp)
                sdve.i(v.tensor_copy(out=B["wbcs4"][:, t, :], in_=B["pwb"][:, :]))
                v.wait_ge(sdve.h, sdve.n)
            sm["_me_pwbfree"] = sdve.n
        sm[("_me_wbc_done", e)] = sdve.n

    def emit_down_load(e, oc):
        dslot = oc % 2
        ds_free = sm.get(("_me_dsfree", dslot))

        @block.gpsimd
        def _(gp, e=e, oc=oc, dslot=dslot, w=ds_free,
              first=(e == 0 and oc < 2)):
            if first:
                gp.wait_ge(sdve.h, max(rt_done[0], prepass_done))
            if w is not None:
                gp.wait_ge(spe.h, w)
            dw = io["edown"][e].rearrange("(ic p) o -> p ic o", p=128)
            sldd2[dslot].i(gp.dma_start(
                out=B["dslice"][dslot][:, :, :],
                in_=dw[:, :, oc * 128:(oc + 1) * 128]), 16)
        sm[("_me_dready", dslot)] = sldd2[dslot].n

    def emit_down_group(e, oc, t):
        hslot = e % 2
        dslot = oc % 2
        d_rdy = sm[("_me_dready", dslot)]
        h_done = sm[("_me_hdone", e)]
        wbc_done = sm[("_me_wbc_done", e)]
        ts = slice(t * TN, (t + 1) * TN)
        pp = (oc * TT + t) % 3
        pd_free = sm.get(("_me_pd", pp))
        mslot = oc % 2
        m_free = sm.get(("_me_mo", mslot)) if t == 0 else None

        @block.tensor
        def _(te, ts=ts, pp=pp, dslot=dslot, hslot=hslot,
              w=h_done, wd=d_rdy, pf=pd_free):
            te.wait_ge(sdve.h, w)
            te.wait_ge(sldd2[dslot].h, wd)
            if pf is not None:
                te.wait_ge(sdve.h, pf)
            for ic in range(8):
                ins = te.matmul(out=B["pd2"][pp][:, :],
                                lhsT=B["dslice"][dslot][:, ic, :],
                                rhs=B["h_et"][hslot][:, ic, ts],
                                start=(ic == 0), stop=(ic == 7))
            spe.i(ins)
        dmm_done = spe.n
        if t == TT - 1:
            sm[("_me_dsfree", dslot)] = spe.n

        @block.vector
        def _(v, t=t, ts=ts, pp=pp, mslot=mslot, w=dmm_done,
              wb=wbc_done, mf=m_free):
            v.wait_ge(spe.h, w)
            v.wait_ge(sdve.h, wb)
            if mf is not None:
                v.wait_ge(stmo[mf[1]].h, mf[0])
            sdve.i(v.tensor_tensor(out=B["mo_oc"][mslot][:, ts],
                                   in0=B["pd2"][pp][:, :],
                                   in1=B["wbcs4"][:, t, :], op=OP.mult))
        sm[("_me_pd", pp)] = sdve.n

        if t == TT - 1:
            ev_done = sdve.n

            @block.gpsimd
            def _(gp, e=e, oc=oc, mslot=mslot, w=ev_done, pp_done=prepass_keys):
                gp.wait_ge(sdve.h, w)
                st = stmo[oc % 4]
                if e == 0:
                    for tt_, cnt in pp_done:
                        gp.wait_ge(stmo[tt_].h, cnt)
                gp.wait_ge(st.h, st.n)
                st.i(gp.dma_start(out=outr[:, oc, :], in_=B["mo_oc"][mslot][:, :],
                                  accum_op=OP.add), 16)
            sm[("_me_mo", mslot)] = (stmo[oc % 4].n, oc % 4)

    for e in range(NE + 1):
        gu_tasks = []
        if e < NE:
            for ic in range(8):
                gu_tasks.append(("load", e, ic))
                for t in range(TT):
                    gu_tasks.append(("gu", e, ic, t))
        down_tasks = []
        if e >= 1:
            for oc in range(16):
                down_tasks.append(("dload", e - 1, oc))
                for t in range(TT):
                    down_tasks.append(("down", e - 1, oc, t))
        if e >= 1:
            emit_wbc(e - 1)
        gi, di = 0, 0
        while gi < len(gu_tasks) or di < len(down_tasks):
            if gi < len(gu_tasks):
                tk = gu_tasks[gi]; gi += 1
                if tk[0] == "load":
                    emit_gu_load(tk[1], tk[2])
                else:
                    emit_gu_group(tk[1], tk[2], tk[3])
            for _ in range(2):
                if di < len(down_tasks):
                    tk = down_tasks[di]; di += 1
                    if tk[0] == "dload":
                        emit_down_load(tk[1], tk[2])
                    else:
                        emit_down_group(tk[1], tk[2], tk[3])
        if e < NE:
            sm[("_me_hdone", e)] = sdve.n
        if e >= 1:
            sm[("_me_hfree", (e - 1) % 2)] = spe.n

def build_moe_launch():
    pfx = ""
    nc = bass.Bass("TRN2", target_bir_lowering=False, debug=False)
    io = {
        "xn": nc.dram_tensor("xn", [DM, TLOC], BF16, kind="ExternalInput").ap(),
        "routerT": nc.dram_tensor("routerT", [DM, NE], F32, kind="ExternalInput").ap(),
        "egu": nc.dram_tensor("egu", [NE, DM, DM], F32, kind="ExternalInput").ap(),
        "edown": nc.dram_tensor("edown", [NE, IM, DM], F32, kind="ExternalInput").ap(),
        "sel8": nc.dram_tensor("sel8", [NE, NE * 128], F32, kind="ExternalInput").ap(),
        "eyef": nc.dram_tensor("eyef", [128, 128], F32, kind="ExternalInput").ap(),
        "x1": nc.dram_tensor("x1", [DM, TLOC], F32, kind="ExternalInput").ap(),
        "attn1": nc.dram_tensor("attn1", [NH, 128, TLOC], F32, kind="ExternalInput").ap(),
        "out": nc.dram_tensor("out", [DM, TLOC], F32, kind="ExternalOutput").ap(),
    }
    with contextlib.ExitStack() as stack:
        B = moe_bufs(nc, stack)
        sm = make_sems(nc, stack)
        blk = stack.enter_context(nc.Block())
        with contextlib.ExitStack() as rstack:
            moe_router_scoped(nc, rstack, B, "")
            emit_router(nc, blk, sm, B, io)
            emit_moe_prepass(nc, blk, sm, B, io)
        with contextlib.ExitStack() as estack:
            moe_expert_psums(nc, estack, B, "")
            emit_moe(nc, blk, sm, B, io)
            finalize_store_wait(blk, sm)
    return nc


def moe_bufs(nc, stack, pfx=""):
    B = {}
    def sb(name, shape, dt):
        B[name] = stack.enter_context(nc.sbuf_tensor(pfx + name, shape, dt))
    sb("rt_sb", [128, FC, NE], BF16)
    sb("eyef_sb", [128, 128], F32)
    sb("sel_sb", [NE, NE * 128], F32)
    sb("m1", [128, 1], F32)
    sb("m2", [128, 1], F32)
    sb("dd", [128, 1], F32)
    sb("s1", [128, 1], F32)
    sb("s2", [128, 1], F32)
    sb("eq1", [128, NE], F32)
    sb("eq2", [128, NE], F32)
    sb("l2", [128, NE], F32)
    sb("w1", [128, NE], F32)
    sb("wsb", [128, FC, NE], F32)
    sb("w8sb", [NE, TLOC], F32)
    sb("xn_sb2", [128, FC, TLOC], BF16)
    B["gslice"] = [stack.enter_context(nc.sbuf_tensor(pfx + f"gslice{i}", [128, FC, 128], BF16))
                   for i in range(2)]
    B["uslice"] = [stack.enter_context(nc.sbuf_tensor(pfx + f"uslice{i}", [128, FC, 128], BF16))
                   for i in range(2)]
    B["dslice"] = [stack.enter_context(nc.sbuf_tensor(pfx + f"dslice{i}", [128, 8, 128], BF16))
                   for i in range(2)]
    B["h_et"] = [stack.enter_context(nc.sbuf_tensor(pfx + f"h_et{i}", [128, 8, TLOC], BF16))
                 for i in range(2)]
    B["silu2"] = [stack.enter_context(nc.sbuf_tensor(pfx + f"silu2{i}", [128, TN], F32))
                  for i in range(2)]
    sb("wbcs4", [128, TT, TN], F32)
    B["mo_oc"] = [stack.enter_context(nc.sbuf_tensor(pfx + f"mo_oc{i}", [128, TLOC], F32))
                  for i in range(2)]
    return B




def emit_kv_collective(nc, block, sm, cc_in, cc_out, kv_store_stream):
    scc = sm["cc"]

    @block.gpsimd
    def _(gp):
        gp.wait_ge(kv_store_stream.h, kv_store_stream.n)
        scc.i(gp.collective_compute(
            "AllGather", OP.bypass,
            replica_groups=[[0, 1], [2, 3], [4, 5], [6, 7]],
            ins=[cc_in.opt()], outs=[cc_out.opt()]))
    return scc.n


def build_fused():
    nc = bass.Bass("TRN2", target_bir_lowering=False, debug=False)
    EI, EO, IN = "ExternalInput", "ExternalOutput", "Internal"

    def dt_(name, shape, dt, kind):
        return nc.dram_tensor(name, shape, dt, kind=kind).ap()

    io = {
        "x0": dt_("x0", [DM, TLOC], F32, EI),
        "amask": dt_("amask", [N_MSK, 128, TN], BF16, EI),
        "eye": dt_("eye", [128, 128], BF16, EI),
        "eyef": dt_("eyef", [128, 128], F32, EI),
        "sel8": dt_("sel8", [NE, NE * 128], F32, EI),
        "wq0": dt_("wq0", [DM, DM], F32, EI),
        "wk0": dt_("wk0", [DM, 512], F32, EI),
        "wv0": dt_("wv0", [DM, 512], F32, EI),
        "wo_w": dt_("wo_w", [DM, DM], F32, EI),
        "gate_wT": dt_("gate_wT", [DM, I0], I32, EI),
        "up_wT": dt_("up_wT", [DM, I0], I32, EI),
        "down_wT": dt_("down_wT", [I0, DM], I32, EI),
        "sg": dt_("sg", [128, 64], F32, EI),
        "su": dt_("su", [128, 64], F32, EI),
        "sd": dt_("sd", [128, 16], F32, EI),
        "wq1": dt_("wq1", [NH, DM, HD], F32, EI),
        "wk1": dt_("wk1", [NG, DM, HD], F32, EI),
        "wv1": dt_("wv1", [NG, DM, HD], F32, EI),
        "bqT": dt_("bqT", [128, NH], F32, EI),
        "bkT": dt_("bkT", [128, NG], F32, EI),
        "bvT": dt_("bvT", [128, NG], F32, EI),
        "routerT": dt_("routerT", [DM, NE], F32, EI),
        "egu": dt_("egu", [NE, DM, DM], F32, EI),
        "edown": dt_("edown", [NE, IM, DM], F32, EI),
        "out": dt_("out", [DM, TLOC], F32, EO),
        "xn0": dt_("xn0", [DM, TLOC], BF16, IN),
        "sxn0": dt_("sxn0", [1, TLOC], BF16, IN),
        "xn1": dt_("xn1", [DM, TLOC], BF16, IN),
        "sxn1": dt_("sxn1", [1, TLOC], BF16, IN),
        "q0": dt_("q0", [NH, 128, TLOC], BF16, IN),
        "q1": dt_("q1", [NH, 128, TLOC], BF16, IN),
        "cc_in0": dt_("cc_in0", [128, 16384], BF16, IN),
        "cc_in1": dt_("cc_in1", [128, 16384], BF16, IN),
        "cc_out0": dt_("cc_out0", [256, 16384], BF16, IN),
        "cc_out1": dt_("cc_out1", [256, 16384], BF16, IN),
        "attn0": dt_("attn0", [NH, 128, TLOC], BF16, IN),
        "attn1": dt_("attn1", [NH, 128, TLOC], F32, IN),
        "wo_out": dt_("wo_out", [DM, TLOC], F32, IN),
        "h": dt_("h", [I0, TLOC], BF16, IN),
        "x1": dt_("x1", [DM, TLOC], F32, IN),
    }
    with contextlib.ExitStack() as S0:
        sm = make_sems(nc, S0)
        blk = S0.enter_context(nc.Block())

        def barrier():
            cnts = ([(sm[e].h, sm[e].n) for e in ("pe", "act", "dve", "gp")]
                    + [(s.h, s.n) for s in sm._streams.values()])
            cnts = [(hh, n) for hh, n in cnts if n > 0]
            for ename in ("tensor", "scalar", "vector", "gpsimd", "sync"):
                def f(eng, cnts=cnts):
                    for hh, n in cnts:
                        eng.wait_ge(hh, n)
                getattr(blk, ename)(f)

        for layer in (0, 1):
            P = f"l{layer}"
            xsrc = io["x0"] if layer == 0 else io["x1"]
            xn, sxn = io[f"xn{layer}"], io[f"sxn{layer}"]
            barrier()
            with contextlib.ExitStack() as s:
                B = ln_bufs(nc, s, P + "ln")
                emit_ln(nc, blk, sm, B, {"x": xsrc, "xn": xn, "sxn": sxn})
            barrier()
            with contextlib.ExitStack() as s:
                B = qkv_bufs(nc, s, layer, P + "qk")
                ioq = {"xn": xn, "eye": io["eye"], "q_buf": io[f"q{layer}"],
                       "kv_cc": io[f"cc_in{layer}"]}
                if layer == 0:
                    ioq.update(wq=io["wq0"], wk=io["wk0"], wv=io["wv0"])
                else:
                    ioq.update(wq=io["wq1"], wk=io["wk1"], wv=io["wv1"],
                               bqT=io["bqT"], bkT=io["bkT"], bvT=io["bvT"])
                emit_qkv(nc, blk, sm, B, ioq, layer)
                cc_n = emit_kv_collective(nc, blk, sm, io[f"cc_in{layer}"],
                                          io[f"cc_out{layer}"], sm.stream("st_kv"))
            barrier()
            with contextlib.ExitStack() as s:
                B = attn_bufs(nc, s, layer, P + "at")
                ioa = {"q_buf": io[f"q{layer}"], "kv_pair": io[f"cc_out{layer}"],
                       "amask": io["amask"], "attn_out": io[f"attn{layer}"],
                       "cc_dep": (sm["cc"].h, cc_n)}
                emit_attn(nc, blk, sm, B, ioa, layer)
            if layer == 0:
                barrier()
                with contextlib.ExitStack() as s:
                    B = wo_bufs(nc, s, "wo")
                    emit_wo(nc, blk, sm, B, {"attn_out": io["attn0"],
                                             "wo": io["wo_w"], "wo_out": io["wo_out"]})
                barrier()
                with contextlib.ExitStack() as s:
                    B = ffn0_small_bufs(nc, s, "f0")
                    iof = {"xn": xn, "sxn": sxn, "gate_wT": io["gate_wT"],
                           "up_wT": io["up_wT"], "down_wT": io["down_wT"],
                           "sg": io["sg"], "su": io["su"], "sd": io["sd"],
                           "x0": io["x0"], "wo_out": io["wo_out"], "x1": io["x1"],
                           "h": io["h"]}
                    with contextlib.ExitStack() as sa:
                        ffn0_a_bufs(nc, sa, B, "f0a")
                        emit_ffn0_a(nc, blk, sm, B, iof)
                    with contextlib.ExitStack() as sb_:
                        ffn0_b_bufs(nc, sb_, B, "f0b")
                        emit_ffn0_b(nc, blk, sm, B, iof)
            else:
                barrier()
                with contextlib.ExitStack() as s:
                    B = moe_bufs(nc, s, "me")
                    iom = {"xn": xn, "routerT": io["routerT"], "egu": io["egu"],
                           "edown": io["edown"], "sel8": io["sel8"],
                           "eyef": io["eyef"], "x1": io["x1"],
                           "attn1": io["attn1"], "out": io["out"]}
                    with contextlib.ExitStack() as rs:
                        moe_router_scoped(nc, rs, B, "me")
                        emit_router(nc, blk, sm, B, iom)
                        emit_moe_prepass(nc, blk, sm, B, iom)
                    with contextlib.ExitStack() as es:
                        moe_expert_psums(nc, es, B, "me")
                        emit_moe(nc, blk, sm, B, iom)
        finalize_store_wait(blk, sm)

        @blk.gpsimd
        def _(gp):
            gp.wait_ge(sm["cc"].h, sm["cc"].n)
    return nc


def kernel_fused(**inputs):
    g, per = _prep_inputs(inputs)
    nc = _get_nc("fused", build_fused)
    maps = []
    for c in range(8):
        m = {"x0": per[c]["x0"], "amask": per[c]["amask"],
             "eye": g["eye"], "eyef": g["eyef"], "sel8": g["sel8"],
             "wq0": g["l0_wq"], "wk0": g["l0_wk"], "wv0": g["l0_wv"],
             "wo_w": g["l0_wo"], "gate_wT": g["gate_wT"], "up_wT": g["up_wT"],
             "down_wT": g["down_wT"], "sg": g["sg"], "su": g["su"], "sd": g["sd"],
             "wq1": g["l1_wq"], "wk1": g["l1_wk"], "wv1": g["l1_wv"],
             "bqT": g["l1_bq"], "bkT": g["l1_bk"], "bvT": g["l1_bv"],
             "routerT": g["routerT"], "egu": g["egu"], "edown": g["edown"]}
        maps.append(m)
    res = run_bass_kernel_spmd(nc, maps, core_ids=list(range(8)))
    return _unprep_output([res.results[c]["out"] for c in range(8)])


_NC_CACHE = {}


def _get_nc(name, builder):
    if name not in _NC_CACHE:
        _NC_CACHE[name] = builder()
    return _NC_CACHE[name]


def kernel(**inputs):
    return kernel_fused(**inputs)


def kernel_unfused(**inputs):
    g, per = _prep_inputs(inputs)
    C = list(range(8))

    def run(nc, maps):
        return run_bass_kernel_spmd(nc, maps, core_ids=C).results

    out_final = None
    xcur = [per[c]["x0"] for c in C]
    for layer in (0, 1):
        r = run(_get_nc("ln", build_ln_launch), [{"x": xcur[c]} for c in C])
        xn = [r[c]["xn"] for c in C]
        sxn = [r[c]["sxn"] for c in C]
        if layer == 0:
            maps = [{"xn": xn[c], "eye": g["eye"], "wq": g["l0_wq"],
                     "wk": g["l0_wk"], "wv": g["l0_wv"]} for c in C]
            r = run(_get_nc("qkv0", lambda: build_qkv_launch(0)), maps)
        else:
            maps = [{"xn": xn[c], "eye": g["eye"], "wq": g["l1_wq"],
                     "wk": g["l1_wk"], "wv": g["l1_wv"], "bqT": g["l1_bq"],
                     "bkT": g["l1_bk"], "bvT": g["l1_bv"]} for c in C]
            r = run(_get_nc("qkv1", lambda: build_qkv_launch(1)), maps)
        q = [r[c]["q_buf"] for c in C]
        kv = [r[c]["kv_cc"] for c in C]
        kvp = [np.concatenate([kv[2 * (c // 2)], kv[2 * (c // 2) + 1]], axis=0)
               for c in C]
        maps = [{"q_buf": q[c], "kv_pair": kvp[c], "amask": per[c]["amask"]}
                for c in C]
        if layer == 0:
            r = run(_get_nc("att0", lambda: build_attn_launch(0)), maps)
            attn = [r[c]["attn_out"] for c in C]
            r = run(_get_nc("wo", build_wo_launch),
                    [{"attn_out": attn[c], "wo": g["l0_wo"]} for c in C])
            wo_out = [r[c]["wo_out"] for c in C]
            maps = [{"xn": xn[c], "sxn": sxn[c], "gate_wT": g["gate_wT"],
                     "up_wT": g["up_wT"], "down_wT": g["down_wT"], "sg": g["sg"],
                     "su": g["su"], "sd": g["sd"], "x0": xcur[c],
                     "wo_out": wo_out[c]} for c in C]
            r = run(_get_nc("ffn0", build_ffn0_launch), maps)
            xcur = [r[c]["x1"] for c in C]
        else:
            r = run(_get_nc("att1", lambda: build_attn_launch(1)), maps)
            attn = [r[c]["attn_out"] for c in C]
            maps = [{"xn": xn[c], "routerT": g["routerT"], "egu": g["egu"],
                     "edown": g["edown"], "sel8": g["sel8"], "eyef": g["eyef"],
                     "x1": xcur[c], "attn1": attn[c]} for c in C]
            r = run(_get_nc("moe", build_moe_launch), maps)
            out_final = [r[c]["out"] for c in C]
    return _unprep_output(out_final)
